# version 23
# baseline (speedup 1.0000x reference)
import numpy as np
from contextlib import ExitStack

import concourse.bass as bass
import concourse.mybir as mybir
from concourse.bass_utils import run_bass_kernel_spmd

F32 = mybir.dt.float32
BF16 = mybir.dt.bfloat16
AF = mybir.ActivationFunctionType
ALU = mybir.AluOpType
AX = mybir.AxisListType

H = 1024
P = 256
HF = 4096
T = 512
EPS = 1e-5

QUEUES = ("pe", "dve", "act", "sp", "gp")


class Op:
    __slots__ = ("fn", "reads", "writes", "sem", "need_inc", "inc_by", "count", "extra")

    def __init__(self, fn, reads, writes, sem, inc_by, extra):
        self.fn = fn
        self.reads = reads
        self.writes = writes
        self.sem = sem
        self.need_inc = inc_by > 1
        self.inc_by = inc_by
        self.count = None
        self.extra = extra


class Build:
    def __init__(self):
        self.ops = {q: [] for q in QUEUES}
        self.last_writer = {}
        self.readers = {}
        self.dma_sem_last = {}
        self.sem_names = set(QUEUES)

    def emit(self, q, fn, reads=(), writes=(), sem=None, inc_by=1, extra=()):
        if sem is None:
            sem = q
        self.sem_names.add(sem)
        op = Op(fn, tuple(reads), tuple(writes), sem, inc_by, list(extra))
        deps = []
        for r in op.reads:
            w = self.last_writer.get(r)
            if w is not None:
                deps.append(w)
        for r in op.writes:
            w = self.last_writer.get(r)
            if w is not None:
                deps.append(w)
            for rd in self.readers.get(r, {}).values():
                deps.append(rd)
        for d in deps:
            if d is op:
                continue
            if d.sem == "pe" and q == "pe" and sem == "pe":
                continue
            op.extra.append(d)
        if inc_by > 1:
            prev = self.dma_sem_last.get(sem)
            if prev is not None:
                op.extra.append(prev)
            self.dma_sem_last[sem] = op
        self.ops[q].append(op)
        for r in op.writes:
            self.last_writer[r] = op
            self.readers[r] = {}
        for r in op.reads:
            self.readers.setdefault(r, {})[op.sem] = op
        return op

    def finalize_counts(self):
        for q in QUEUES:
            for op in self.ops[q]:
                for d in op.extra:
                    if isinstance(d, Op):
                        d.need_inc = True
        cnt = {s: 0 for s in self.sem_names}
        for q in QUEUES:
            for op in self.ops[q]:
                if op.need_inc:
                    cnt[op.sem] += op.inc_by
                    op.count = cnt[op.sem]

    def run(self, q, eng, sems):
        waited = {}
        for op in self.ops[q]:
            for d in op.extra:
                if isinstance(d, Op):
                    if d.count is None:
                        continue
                    key, val = d.sem, d.count
                else:
                    key, val = d
                if waited.get(key, -1) >= val:
                    continue
                eng.wait_ge(sems[key], val)
                waited[key] = val
            inst = op.fn(eng)
            if op.need_inc and op.count is not None:
                inst.then_inc(sems[op.sem], op.inc_by)


def build_program(nc: bass.Bass, L=4096, SEQS=2):
    NCH = L // T
    d_x = nc.dram_tensor("x", [SEQS, L, H], F32, kind="ExternalInput").ap()
    d_steps = nc.dram_tensor("steps_raw", [P], F32, kind="ExternalInput").ap()
    d_G = nc.dram_tensor("G_raw", [P], F32, kind="ExternalInput").ap()
    d_A = nc.dram_tensor("A_raw", [P], F32, kind="ExternalInput").ap()
    d_B = nc.dram_tensor("B_mat", [P, H, 2], F32, kind="ExternalInput").ap()
    d_C = nc.dram_tensor("C_mat", [H, P, 2], F32, kind="ExternalInput").ap()
    d_D = nc.dram_tensor("D_vec", [H], F32, kind="ExternalInput").ap()
    d_w1 = nc.dram_tensor("w1", [HF, H], F32, kind="ExternalInput").ap()
    d_b1 = nc.dram_tensor("b1", [HF], F32, kind="ExternalInput").ap()
    d_w2 = nc.dram_tensor("w2", [H, HF], F32, kind="ExternalInput").ap()
    d_b2 = nc.dram_tensor("b2", [H], F32, kind="ExternalInput").ap()
    d_id = nc.dram_tensor("ident", [128, 128], F32, kind="ExternalInput").ap()
    d_out = nc.dram_tensor("out", [SEQS, L, H], F32, kind="ExternalOutput").ap()
    s_w1b = nc.dram_tensor("scr_w1b", [HF, H], BF16, kind="Internal").ap()
    s_w2b = nc.dram_tensor("scr_w2b", [H, HF], BF16, kind="Internal").ap()
    s_w1t = nc.dram_tensor("scr_w1t", [8, 128, HF], BF16, kind="Internal").ap()
    s_w2t = nc.dram_tensor("scr_w2t", [8, 128, 4 * H], BF16, kind="Internal").ap()

    es = ExitStack()
    with es:
        def sb(name, shape, dt):
            return es.enter_context(nc.sbuf_tensor(name, shape, dt))

        cB = sb("cB", [128, 8 * 512], BF16)
        cC = sb("cC", [128, 4 * H], BF16)
        g_a = sb("g_a", [128, 2 * T], F32)
        g_b = sb("g_b", [128, 2 * T], F32)
        g_G1 = sb("g_G1", [128, 2 * T], F32)
        g_G2 = sb("g_G2", [128, 2 * T], F32)
        drep = sb("drep", [128, H], BF16)
        b2rep = sb("b2rep", [128, H], F32)
        b1col = sb("b1col", [128, 32], F32)
        idf = sb("idf", [128, 128], F32)
        id16 = sb("id16", [128, 128], BF16)
        cc = {n: sb("cc_" + n, [128, 4], F32)
              for n in ("q2a", "q2b", "q3a", "q3b", "R1b", "R2a", "R2b",
                        "i11", "i12", "i21", "i22", "g1a", "g1b", "cAT", "cBT",
                        "G1e", "G2e")}
        P1s = [sb(f"P1_{s}", [128, 4], F32) for s in range(SEQS)]
        P2s = [sb(f"P2_{s}", [128, 4], F32) for s in range(SEQS)]
        dlc = sb("dlc", [128, 4], F32)
        dls = sb("dls", [128, 4], F32)
        wq2 = sb("wq2", [128, 4], F32)
        wq3 = sb("wq3", [128, 4], F32)
        wR1 = sb("wR1", [128, 4], F32)
        wR2 = sb("wR2", [128, 4], F32)
        wt1 = sb("wt1", [128, 4], F32)
        wt2 = sb("wt2", [128, 4], F32)
        pce = sb("pce", [128, 4], F32)
        pse = sb("pse", [128, 4], F32)
        pv = sb("pv", [128, 64], F32)
        st1 = sb("st1", [128, 16], F32)
        st2 = sb("st2", [128, 16], F32)
        st3 = sb("st3", [128, 24], F32)
        prow = sb("prow", [8, 128], F32)
        b1row = sb("b1row", [32, 128], F32)
        dr0 = sb("dr0", [1, H], F32)
        dr016 = sb("dr016", [1, H], BF16)
        b2r0 = sb("b2r0", [1, H], F32)
        ones16 = sb("ones16", [1, 128], BF16)
        onesf = sb("onesf", [1, 128], F32)
        sx = [sb(f"sx{r}", [128, 2 * H], F32) for r in range(4)]
        w1s = [sb(f"w1s{r}", [128, 8 * 512], BF16) for r in range(2)]
        w2s = [sb(f"w2s{r}", [128, 4 * H], BF16) for r in range(2)]
        outb = [sb(f"outb{r}", [128, H], F32) for r in range(2)]
        n1tm = sb("n1tm", [128, 4 * H], BF16)
        n1fm = sb("n1fm", [128, 8 * 512], BF16)
        n2fm = sb("n2fm", [128, 8 * 512], BF16)
        n2tm = [sb(f"n2tm{r}", [128, H], BF16) for r in range(4)]
        junk = [sb(f"junk{r}", [128, H], BF16) for r in range(2)]
        u1 = [sb(f"u1_{r}", [128, T], F32) for r in range(4)]
        u2 = [sb(f"u2_{r}", [128, T], F32) for r in range(4)]
        dtm = sb("dtm", [128, 4 * H], BF16)
        a1col = sb("a1col", [128, 32], F32)
        jdve = junk_placeholder = None
        ys = sb("ys", [128, 4 * T], BF16)
        mo = [sb(f"mo{r}", [128, 1056], F32) for r in range(2)]
        tt = [sb(f"tt{r}", [128, H], F32) for r in range(2)]
        gel = [sb(f"gel{r}", [128, 256], BF16) for r in range(4)]
        tabA = mo[0]
        tabB = mo[1]
        scw = u1[0]
        scw2 = u2[0]

        pb = [es.enter_context(nc.psum_tensor("pb0", [128, 512], F32)),
              es.enter_context(nc.psum_tensor("pb1", [128, 512], F32))]
        pcm = [es.enter_context(nc.psum_tensor("pcA", [128, 1024], F32)),
               es.enter_context(nc.psum_tensor("pcB", [128, 1024], F32))]
        pt = [es.enter_context(nc.psum_tensor("pt6", [128, 1024], BF16)),
              es.enter_context(nc.psum_tensor("pt7", [128, 1024], BF16))]

        B = Build()

        def pe(fn, reads=(), writes=()):
            return B.emit("pe", fn, reads, writes)

        def dve(fn, reads=(), writes=()):
            return B.emit("dve", fn, reads, writes)

        def act(fn, reads=(), writes=()):
            return B.emit("act", fn, reads, writes)

        def dma(q, fn, reads=(), writes=(), sem=None):
            return B.emit(q, fn, reads, writes, sem=sem, inc_by=16)

        PB = ["pb0", "pb1"]
        PCM = [["pcA0", "pcA1"], ["pcB0", "pcB1"]]
        PT = ["pt6", "pt7"]

        dma("sp", lambda e: e.dma_start(out=idf[:, :], in_=d_id), writes=["idf"], sem="dp")
        dma("sp", lambda e: e.dma_start(out=prow[0:2, :], in_=d_steps.rearrange("(h p) -> h p", p=128)),
            writes=["prow"], sem="dp")
        dma("sp", lambda e: e.dma_start(out=prow[2:4, :], in_=d_G.rearrange("(h p) -> h p", p=128)),
            writes=["prow"], sem="dp")
        dma("sp", lambda e: e.dma_start(out=prow[4:6, :], in_=d_A.rearrange("(h p) -> h p", p=128)),
            writes=["prow"], sem="dp")
        dma("sp", lambda e: e.dma_start(out=b1row[:, :], in_=d_b1.rearrange("(a p) -> a p", p=128)),
            writes=["b1row"], sem="dp")
        dma("sp", lambda e: e.dma_start(out=dr0[:, :], in_=d_D[None, :]), writes=["dr0"], sem="dp")
        dma("sp", lambda e: e.dma_start(out=b2r0[:, :], in_=d_b2[None, :]), writes=["b2r0"], sem="dp")
        dma("sp", lambda e: e.dma_start(out=sx[0][:, :], in_=d_B[0:128, :, :].rearrange("p h t -> p (h t)")),
            writes=["sx0.0", "sx0.1"], sem="dp")
        dma("sp", lambda e: e.dma_start(out=sx[1][:, :], in_=d_B[128:256, :, :].rearrange("p h t -> p (h t)")),
            writes=["sx1.0", "sx1.1"], sem="dp")

        dve(lambda e: e.tensor_copy(id16[:, :], idf[:, :]), reads=["idf"], writes=["id16"])
        dve(lambda e: e.memset(ones16[:, :], 1.0), writes=["ones16"])
        dve(lambda e: e.memset(onesf[:, :], 1.0), writes=["onesf"])
        for s in range(SEQS):
            dve(lambda e, s=s: e.memset(P1s[s][:, :], 0.0), writes=[f"P1.{s}"])
            dve(lambda e, s=s: e.memset(P2s[s][:, :], 0.0), writes=[f"P2.{s}"])
        dve(lambda e: e.tensor_copy(dr016[:, :], dr0[:, :]), reads=["dr0"], writes=["dr016"])

        pe(lambda e: e.transpose(pcm[0][:, 0:6], prow[0:6, :], idf[0:6, 0:6]),
           reads=["prow", "idf"], writes=[PCM[0][0]])
        dve(lambda e: e.tensor_copy(pv[:, 0:6], pcm[0][:, 0:6]), reads=[PCM[0][0]], writes=["pv"])
        pe(lambda e: e.transpose(pcm[1][:, 0:32], b1row[:, :], idf[0:32, 0:32]),
           reads=["b1row", "idf"], writes=[PCM[1][0]])
        dve(lambda e: e.tensor_copy(b1col[:, 0:32], pcm[1][:, 0:32]), reads=[PCM[1][0]], writes=["b1col"])

        pe(lambda e: e.matmul(pcm[1][:, 0:512], ones16[0:1, :], dr016[0:1, 0:512], start=True, stop=True),
           reads=["ones16", "dr016"], writes=[PCM[1][0]])
        pe(lambda e: e.matmul(pcm[1][:, 512:1024], ones16[0:1, :], dr016[0:1, 512:1024], start=True, stop=True),
           reads=["ones16", "dr016"], writes=[PCM[1][1]])
        dve(lambda e: e.tensor_copy(drep[:, :], pcm[1][:, :]), reads=[PCM[1][0], PCM[1][1]], writes=["drep"])
        pe(lambda e: e.matmul(pcm[0][:, 0:512], onesf[0:1, :], b2r0[0:1, 0:512], start=True, stop=True),
           reads=["onesf", "b2r0"], writes=[PCM[0][0]])
        pe(lambda e: e.matmul(pcm[0][:, 512:1024], onesf[0:1, :], b2r0[0:1, 512:1024], start=True, stop=True),
           reads=["onesf", "b2r0"], writes=[PCM[0][1]])
        dve(lambda e: e.tensor_copy(b2rep[:, :], pcm[0][:, :]), reads=[PCM[0][0], PCM[0][1]], writes=["b2rep"])

        PV = {}
        _n = [6]
        for nme in ("steps", "S", "r2", "r", "rinv", "alpha", "tau", "cth", "s2th",
                    "w2cr", "w1cr", "fscale", "det", "tmpa", "tmpb", "rstep",
                    "Cs", "Bs", "Bss"):
            PV[nme] = slice(_n[0], _n[0] + 2)
            _n[0] += 2
        RAW = {"steps_raw": slice(0, 2), "G_raw": slice(2, 4), "A_raw": slice(4, 6)}

        def pva(nme):
            return pv[:, PV[nme]] if nme in PV else pv[:, RAW[nme]]

        act(lambda e: e.activation(pva("steps"), pva("steps_raw"), AF.Sigmoid), reads=["pv"], writes=["pv"])
        dve(lambda e: e.tensor_relu(pva("tmpa"), pva("G_raw")), reads=["pv"], writes=["pv"])
        dve(lambda e: e.tensor_tensor(pva("S"), pva("tmpa"), pva("steps"), ALU.mult), reads=["pv"], writes=["pv"])
        dve(lambda e: e.tensor_scalar(pva("S"), pva("S"), 1.0, None, ALU.add, ALU.bypass), reads=["pv"], writes=["pv"])
        dve(lambda e: e.reciprocal(pva("r2"), pva("S")), reads=["pv"], writes=["pv"])
        act(lambda e: e.activation(pva("r"), pva("r2"), AF.Sqrt), reads=["pv"], writes=["pv"])
        dve(lambda e: e.reciprocal(pva("rinv"), pva("r")), reads=["pv"], writes=["pv"])
        dve(lambda e: e.tensor_relu(pva("tmpb"), pva("A_raw")), reads=["pv"], writes=["pv"])
        dve(lambda e: e.tensor_tensor(pva("tmpb"), pva("tmpb"), pva("steps"), ALU.mult), reads=["pv"], writes=["pv"])
        dve(lambda e: e.tensor_tensor(pva("tmpb"), pva("tmpb"), pva("steps"), ALU.mult), reads=["pv"], writes=["pv"])
        dve(lambda e: e.tensor_tensor(pva("tmpb"), pva("tmpb"), pva("r2"), ALU.mult), reads=["pv"], writes=["pv"])
        act(lambda e: e.activation(pva("alpha"), pva("tmpb"), AF.Tanh, scale=1.0 / 1.99), reads=["pv"], writes=["pv"])
        dve(lambda e: e.tensor_scalar(pva("alpha"), pva("alpha"), 1.99, None, ALU.mult, ALU.bypass), reads=["pv"], writes=["pv"])
        dve(lambda e: e.tensor_tensor(pva("tau"), pva("r2"), pva("alpha"), ALU.subtract), reads=["pv"], writes=["pv"])
        dve(lambda e: e.tensor_scalar(pva("tau"), pva("tau"), 1.0, None, ALU.add, ALU.bypass), reads=["pv"], writes=["pv"])
        dve(lambda e: e.tensor_tensor(pva("cth"), pva("tau"), pva("rinv"), ALU.mult), reads=["pv"], writes=["pv"])
        dve(lambda e: e.tensor_scalar(pva("cth"), pva("cth"), 0.5, 1.0, ALU.mult, ALU.min), reads=["pv"], writes=["pv"])
        dve(lambda e: e.tensor_scalar(pva("cth"), pva("cth"), -1.0, None, ALU.max, ALU.bypass), reads=["pv"], writes=["pv"])
        dve(lambda e: e.tensor_tensor(pva("s2th"), pva("cth"), pva("cth"), ALU.mult), reads=["pv"], writes=["pv"])
        dve(lambda e: e.tensor_scalar(pva("s2th"), pva("s2th"), -1.0, 1.0, ALU.mult, ALU.add), reads=["pv"], writes=["pv"])
        dve(lambda e: e.tensor_tensor(pva("tmpa"), pva("r"), pva("cth"), ALU.mult), reads=["pv"], writes=["pv"])
        dve(lambda e: e.tensor_tensor(pva("w2cr"), pva("tau"), pva("tmpa"), ALU.subtract), reads=["pv"], writes=["pv"])
        dve(lambda e: e.tensor_tensor(pva("w2cr"), pva("w2cr"), pva("steps"), ALU.mult), reads=["pv"], writes=["pv"])
        dve(lambda e: e.tensor_tensor(pva("w2cr"), pva("w2cr"), pva("rinv"), ALU.mult), reads=["pv"], writes=["pv"])
        dve(lambda e: e.tensor_tensor(pva("w1cr"), pva("r2"), pva("tmpa"), ALU.subtract), reads=["pv"], writes=["pv"])
        dve(lambda e: e.tensor_tensor(pva("w1cr"), pva("w1cr"), pva("alpha"), ALU.subtract), reads=["pv"], writes=["pv"])
        dve(lambda e: e.tensor_tensor(pva("w1cr"), pva("w1cr"), pva("rinv"), ALU.mult), reads=["pv"], writes=["pv"])
        dve(lambda e: e.tensor_tensor(pva("fscale"), pva("steps"), pva("r2"), ALU.mult), reads=["pv"], writes=["pv"])
        dve(lambda e: e.tensor_tensor(pva("tmpb"), pva("steps"), pva("steps"), ALU.mult), reads=["pv"], writes=["pv"])
        dve(lambda e: e.tensor_tensor(pva("tmpb"), pva("tmpb"), pva("s2th"), ALU.mult), reads=["pv"], writes=["pv"])
        dve(lambda e: e.tensor_tensor(pva("det"), pva("w2cr"), pva("w2cr"), ALU.mult), reads=["pv"], writes=["pv"])
        dve(lambda e: e.tensor_tensor(pva("det"), pva("det"), pva("tmpb"), ALU.add), reads=["pv"], writes=["pv"])
        dve(lambda e: e.reciprocal(pva("det"), pva("det")), reads=["pv"], writes=["pv"])

        def dup4(dstname, dst, src):
            dve(lambda e: e.tensor_copy(dst[:, 0:2], src), reads=["pv"], writes=[dstname])
            dve(lambda e: e.tensor_copy(dst[:, 2:4], src), reads=["pv"], writes=[dstname])

        dup4("cc.q2a", cc["q2a"], pva("fscale"))
        dve(lambda e: e.tensor_tensor(pva("tmpb"), pva("alpha"), pva("tmpa"), ALU.add), reads=["pv"], writes=["pv"])
        dve(lambda e: e.tensor_scalar(pva("tmpb"), pva("tmpb"), -1.0, 1.0, ALU.mult, ALU.add), reads=["pv"], writes=["pv"])
        dup4("cc.q2b", cc["q2b"], pva("tmpb"))
        dve(lambda e: e.tensor_tensor(pva("tmpb"), pva("r2"), pva("tmpa"), ALU.subtract), reads=["pv"], writes=["pv"])
        dup4("cc.q3a", cc["q3a"], pva("tmpb"))
        dve(lambda e: e.reciprocal(pva("rstep"), pva("steps")), reads=["pv"], writes=["pv"])
        dve(lambda e: e.tensor_tensor(pva("tmpb"), pva("alpha"), pva("rstep"), ALU.mult), reads=["pv"], writes=["pv"])
        dve(lambda e: e.tensor_scalar(pva("tmpb"), pva("tmpb"), -1.0, None, ALU.mult, ALU.bypass), reads=["pv"], writes=["pv"])
        dup4("cc.q3b", cc["q3b"], pva("tmpb"))
        dup4("cc.R1b", cc["R1b"], pva("tmpa"))
        dve(lambda e: e.tensor_tensor(pva("tmpb"), pva("r"), pva("s2th"), ALU.mult), reads=["pv"], writes=["pv"])
        dve(lambda e: e.tensor_scalar(pva("tmpb"), pva("tmpb"), -1.0, None, ALU.mult, ALU.bypass), reads=["pv"], writes=["pv"])
        dup4("cc.R2a", cc["R2a"], pva("tmpb"))
        dup4("cc.R2b", cc["R2b"], pva("cth"))
        dve(lambda e: e.tensor_tensor(pva("tmpb"), pva("steps"), pva("s2th"), ALU.mult), reads=["pv"], writes=["pv"])
        dve(lambda e: e.tensor_tensor(pva("tmpb"), pva("tmpb"), pva("det"), ALU.mult), reads=["pv"], writes=["pv"])
        dup4("cc.i11", cc["i11"], pva("tmpb"))
        dve(lambda e: e.tensor_tensor(pva("tmpb"), pva("w2cr"), pva("det"), ALU.mult), reads=["pv"], writes=["pv"])
        dup4("cc.i12", cc["i12"], pva("tmpb"))
        dve(lambda e: e.tensor_scalar(pva("tmpb"), pva("tmpb"), -1.0, None, ALU.mult, ALU.bypass), reads=["pv"], writes=["pv"])
        dup4("cc.i21", cc["i21"], pva("tmpb"))
        dve(lambda e: e.tensor_tensor(pva("tmpb"), pva("steps"), pva("det"), ALU.mult), reads=["pv"], writes=["pv"])
        dup4("cc.i22", cc["i22"], pva("tmpb"))

        def emit_table(Cd, Bd, cresC, cresB, sres, sres2, colw, basen, ncols):
            for h in range(2):
                co = h * colw
                base = pva(basen)[:, h:h + 1]
                cthh = pv[:, PV["cth"].start + h: PV["cth"].start + h + 1]
                s2h = pv[:, PV["s2th"].start + h: PV["s2th"].start + h + 1]
                Bss = pv[:, PV["Bss"].start + h: PV["Bss"].start + h + 1]
                dve(lambda e, a=Cd[:, co:co + 1]: e.memset(a, 1.0), writes=[cresC])
                dve(lambda e, a=Bd[:, co:co + 1]: e.memset(a, 0.0), writes=[cresB])
                dve(lambda e, a=Cd[:, co + 1:co + 2], b=base, c=cthh: e.tensor_tensor(a, b, c, ALU.mult),
                    reads=["pv"], writes=[cresC])
                dve(lambda e, a=Bd[:, co + 1:co + 2], b=base: e.tensor_copy(a, b), reads=["pv"], writes=[cresB])
                w_ = 2
                while w_ < ncols:
                    n = min(w_ - 1, ncols - w_)
                    Csc = Cd[:, co + w_ - 1:co + w_]
                    Bsc = Bd[:, co + w_ - 1:co + w_]
                    dve(lambda e, Bss=Bss, Bsc=Bsc, s2h=s2h: e.tensor_tensor(Bss, Bsc, s2h, ALU.mult),
                        reads=[cresB, "pv"], writes=["pv"])
                    dve(lambda e, n=n, c0=co, Csc=Csc: e.tensor_scalar(scw[:, 0:n], Cd[:, c0 + 1:c0 + 1 + n], Csc, None, ALU.mult, ALU.bypass),
                        reads=[cresC], writes=[sres])
                    dve(lambda e, n=n, c0=co, Bsc=Bsc: e.tensor_scalar(scw2[:, 0:n], Cd[:, c0 + 1:c0 + 1 + n], Bsc, None, ALU.mult, ALU.bypass),
                        reads=[cresC, cresB], writes=[sres2])
                    dve(lambda e, n=n, c0=co, w_=w_, Bss=Bss: e.tensor_scalar(Cd[:, c0 + w_:c0 + w_ + n], Bd[:, c0 + 1:c0 + 1 + n], Bss, None, ALU.mult, ALU.bypass),
                        reads=[cresB, "pv"], writes=[cresC])
                    dve(lambda e, n=n, c0=co, w_=w_: e.tensor_tensor(Cd[:, c0 + w_:c0 + w_ + n], scw[:, 0:n], Cd[:, c0 + w_:c0 + w_ + n], ALU.subtract),
                        reads=[cresC, sres], writes=[cresC])
                    dve(lambda e, n=n, c0=co, w_=w_, Csc=Csc: e.tensor_scalar(Bd[:, c0 + w_:c0 + w_ + n], Bd[:, c0 + 1:c0 + 1 + n], Csc, None, ALU.mult, ALU.bypass),
                        reads=[cresB, cresC], writes=[cresB])
                    dve(lambda e, n=n, c0=co, w_=w_: e.tensor_tensor(Bd[:, c0 + w_:c0 + w_ + n], Bd[:, c0 + w_:c0 + w_ + n], scw2[:, 0:n], ALU.add),
                        reads=[cresB, sres2], writes=[cresB])
                    w_ = w_ + n

        emit_table(g_a, g_b, "ga", "gb", "u1.0", "u2.0", T, "rinv", T)
        for h in range(2):
            rc = pv[:, PV["r"].start + h: PV["r"].start + h + 1]
            ric = pv[:, PV["rinv"].start + h: PV["rinv"].start + h + 1]
            co = h * 513
            go = h * T
            dve(lambda e: e.memset(scw2[:, 0:512], 0.0), writes=["u2.0"])
            dve(lambda e, rc=rc: e.tensor_scalar(scw[:, 0:512], scw2[:, 0:512], rc, None, ALU.add, ALU.bypass),
                reads=["pv", "u2.0"], writes=["u1.0"])
            dve(lambda e, ric=ric: e.tensor_tensor_scan(scw[:, 0:512], scw[:, 0:512], scw2[:, 0:512], ric, ALU.mult, ALU.add),
                reads=["u1.0", "u2.0", "pv"], writes=["u1.0"])
            dve(lambda e, co=co, go=go: e.tensor_tensor(tabA[:, co:co + 512], g_a[:, go:go + 512], scw[:, 0:512], ALU.mult),
                reads=["ga", "u1.0"], writes=["mo0"])
            dve(lambda e, co=co: e.tensor_tensor(tabA[:, co:co + 512], tabA[:, co:co + 512], scw[:, 0:512], ALU.mult),
                reads=["mo0", "u1.0"], writes=["mo0"])
            dve(lambda e, co=co, go=go: e.tensor_tensor(tabB[:, co:co + 512], g_b[:, go:go + 512], scw[:, 0:512], ALU.mult),
                reads=["gb", "u1.0"], writes=["mo1"])
            dve(lambda e, co=co: e.tensor_tensor(tabB[:, co:co + 512], tabB[:, co:co + 512], scw[:, 0:512], ALU.mult),
                reads=["mo1", "u1.0"], writes=["mo1"])
            s2h = pv[:, PV["s2th"].start + h: PV["s2th"].start + h + 1]
            dve(lambda e, co=co: e.tensor_tensor(wt1[:, 0:1], tabB[:, co + 511:co + 512], tabB[:, co + 1:co + 2], ALU.mult),
                reads=["mo1"], writes=["wt1"])
            dve(lambda e, s2h=s2h: e.tensor_tensor(wt1[:, 0:1], wt1[:, 0:1], s2h, ALU.mult),
                reads=["wt1", "pv"], writes=["wt1"])
            dve(lambda e, co=co: e.tensor_tensor(tabA[:, co + 512:co + 513], tabA[:, co + 511:co + 512], tabA[:, co + 1:co + 2], ALU.mult),
                reads=["mo0"], writes=["mo0"])
            dve(lambda e, co=co: e.tensor_tensor(tabA[:, co + 512:co + 513], tabA[:, co + 512:co + 513], wt1[:, 0:1], ALU.subtract),
                reads=["mo0", "wt1"], writes=["mo0"])
            dve(lambda e, co=co: e.tensor_tensor(wt1[:, 0:1], tabB[:, co + 511:co + 512], tabA[:, co + 1:co + 2], ALU.mult),
                reads=["mo1", "mo0"], writes=["wt1"])
            dve(lambda e, co=co: e.tensor_tensor(wt2[:, 0:1], tabA[:, co + 511:co + 512], tabB[:, co + 1:co + 2], ALU.mult),
                reads=["mo0", "mo1"], writes=["wt2"])
            dve(lambda e, co=co: e.tensor_tensor(tabB[:, co + 512:co + 513], wt1[:, 0:1], wt2[:, 0:1], ALU.add),
                reads=["wt1", "wt2"], writes=["mo1"])

        for h in range(2):
            co = h * 513
            go = h * T
            stc = pv[:, PV["steps"].start + h: PV["steps"].start + h + 1]
            s2c = pv[:, PV["s2th"].start + h: PV["s2th"].start + h + 1]
            w2c_ = pv[:, PV["w2cr"].start + h: PV["w2cr"].start + h + 1]
            w1c_ = pv[:, PV["w1cr"].start + h: PV["w1cr"].start + h + 1]
            rin = pv[:, PV["rinv"].start + h: PV["rinv"].start + h + 1]
            dve(lambda e, co=co, go=go, stc=stc: e.tensor_scalar(g_G1[:, go:go + T], tabA[:, co:co + T], stc, None, ALU.mult, ALU.bypass),
                reads=["mo0", "pv"], writes=["gG1"])
            dve(lambda e, co=co, w2c_=w2c_: e.tensor_scalar(scw[:, 0:T], tabB[:, co:co + T], w2c_, None, ALU.mult, ALU.bypass),
                reads=["mo1", "pv"], writes=["u1.0"])
            dve(lambda e, go=go: e.tensor_tensor(g_G1[:, go:go + T], g_G1[:, go:go + T], scw[:, 0:T], ALU.add),
                reads=["gG1", "u1.0"], writes=["gG1"])
            dve(lambda e, co=co, go=go, stc=stc: e.tensor_scalar(g_G2[:, go:go + T], tabB[:, co:co + T], stc, None, ALU.mult, ALU.bypass),
                reads=["mo1", "pv"], writes=["gG2"])
            dve(lambda e, go=go, s2c=s2c: e.tensor_scalar(g_G2[:, go:go + T], g_G2[:, go:go + T], s2c, None, ALU.mult, ALU.bypass),
                reads=["gG2", "pv"], writes=["gG2"])
            dve(lambda e, co=co, w2c_=w2c_: e.tensor_scalar(scw[:, 0:T], tabA[:, co:co + T], w2c_, None, ALU.mult, ALU.bypass),
                reads=["mo0", "pv"], writes=["u1.0"])
            dve(lambda e, go=go: e.tensor_tensor(g_G2[:, go:go + T], g_G2[:, go:go + T], scw[:, 0:T], ALU.subtract),
                reads=["gG2", "u1.0"], writes=["gG2"])
            for off in (0, 2):
                col = h + off
                atm1 = tabA[:, co + T - 1:co + T]
                btm1 = tabB[:, co + T - 1:co + T]
                atT = tabA[:, co + T:co + T + 1]
                btT = tabB[:, co + T:co + T + 1]
                dve(lambda e, col=col, w1c_=w1c_, btm1=btm1: e.tensor_tensor(cc["g1a"][:, col:col + 1], w1c_, btm1, ALU.mult),
                    reads=["mo1", "pv"], writes=["cc.g1a"])
                dve(lambda e, col=col, atm1=atm1: e.tensor_tensor(cc["g1a"][:, col:col + 1], cc["g1a"][:, col:col + 1], atm1, ALU.add),
                    reads=["cc.g1a", "mo0"], writes=["cc.g1a"])
                dve(lambda e, col=col, s2c=s2c, btm1=btm1: e.tensor_tensor(cc["g1b"][:, col:col + 1], s2c, btm1, ALU.mult),
                    reads=["mo1", "pv"], writes=["cc.g1b"])
                dve(lambda e, w1c_=w1c_, atm1=atm1: e.tensor_tensor(wt1[:, 0:1], w1c_, atm1, ALU.mult),
                    reads=["mo0", "pv"], writes=["wt1"])
                dve(lambda e, col=col: e.tensor_tensor(cc["g1b"][:, col:col + 1], cc["g1b"][:, col:col + 1], wt1[:, 0:1], ALU.subtract),
                    reads=["cc.g1b", "wt1"], writes=["cc.g1b"])
                dve(lambda e, col=col, atT=atT: e.tensor_copy(cc["cAT"][:, col:col + 1], atT),
                    reads=["mo0"], writes=["cc.cAT"])
                dve(lambda e, col=col, btT=btT, rin=rin: e.tensor_tensor(cc["cBT"][:, col:col + 1], btT, rin, ALU.mult),
                    reads=["mo1", "pv"], writes=["cc.cBT"])

        for b4 in range(4):
            hp = b4 % 2
            dve(lambda e, b4=b4, hp=hp: e.tensor_copy(cc["G1e"][:, b4:b4 + 1], g_G1[:, hp * T + T - 1: hp * T + T]),
                reads=["gG1"], writes=["cc.G1e"])
            dve(lambda e, b4=b4, hp=hp: e.tensor_copy(cc["G2e"][:, b4:b4 + 1], g_G2[:, hp * T + T - 1: hp * T + T]),
                reads=["gG2"], writes=["cc.G2e"])

        for pbk in range(2):
            fsc = pv[:, PV["fscale"].start + pbk: PV["fscale"].start + pbk + 1]
            dve(lambda e, pbk=pbk, fsc=fsc: e.tensor_scalar(dtm[:, pbk * 2048:(pbk + 1) * 2048], sx[pbk][:, :], fsc, None, ALU.mult, ALU.bypass),
                reads=[f"sx{pbk}.0", f"sx{pbk}.1", "pv"], writes=["dtm"])
        bview = dtm.ap().rearrange("q (pb h t) -> q pb h t", pb=2, t=2)
        for t_ in range(2):
            for pbk in range(2):
                bk = (t_ * 2 + pbk) % 2
                for hb in range(8):
                    inap = bview[:, pbk, hb * 128:(hb + 1) * 128, t_]
                    pe(lambda e, bk=bk, hb=hb, inap=inap:
                       e.transpose(pt[bk][:, hb * 128:(hb + 1) * 128], inap, id16[:, :]),
                       reads=["dtm", "id16"], writes=[PT[bk]])
                outap = cB.ap().rearrange("q (k phi) -> q k phi", k=8)[:, :, t_ * 256 + pbk * 128: t_ * 256 + pbk * 128 + 128]
                dve(lambda e, bk=bk, outap=outap: e.tensor_copy(outap, pt[bk][:, :]),
                    reads=[PT[bk]], writes=["cB"])

        for rnd in range(2):
            dma("sp", lambda e, rnd=rnd: e.dma_start(
                out=sx[2][:, :].rearrange("p (db qt) -> p db qt", db=4),
                in_=d_C[rnd * 512:(rnd + 1) * 512, :, :].rearrange("(db p) q t -> p db (q t)", p=128)),
                writes=["sx2.0", "sx2.1"], sem="dp")
            dve(lambda e: e.tensor_copy(ys[:, 0:2048], sx[2][:, :]),
                reads=["sx2.0", "sx2.1"], writes=["ys.0", "ys.1", "ys.2", "ys.3"])
            cview = ys[:, 0:2048].rearrange("q (db pq t) -> q db pq t", db=4, t=2)
            for db4 in range(4):
                db = rnd * 4 + db4
                for kphi in range(4):
                    t_ = kphi // 2
                    pbk = kphi % 2
                    bk = (db4 * 4 + kphi) % 2
                    inap = cview[:, db4, pbk * 128:(pbk + 1) * 128, t_]
                    pe(lambda e, bk=bk, inap=inap: e.transpose(pt[bk][:, 0:128], inap, id16[:, :]),
                       reads=["ys.0", "ys.1", "ys.2", "ys.3", "id16"], writes=[PT[bk]])
                    dst = cC[:, kphi * 1024 + db * 128: kphi * 1024 + db * 128 + 128]
                    if t_ == 0:
                        dve(lambda e, bk=bk, dst=dst: e.tensor_copy(dst, pt[bk][:, 0:128]),
                            reads=[PT[bk]], writes=["cC"])
                    else:
                        dve(lambda e, bk=bk, dst=dst: e.tensor_scalar(dst, pt[bk][:, 0:128], -1.0, None, ALU.mult, ALU.bypass),
                            reads=[PT[bk]], writes=["cC"])

        halfseq = [(c, s, hf) for c in range(NCH) for s in range(SEQS) for hf in range(2)]
        slot_of = {key: i % 4 for i, key in enumerate(halfseq)}
        next_load = [0]

        def emit_xloads(upto):
            while next_load[0] < min(upto, len(halfseq)):
                ckey = halfseq[next_load[0]]
                c, s, hf = ckey
                r = slot_of[ckey]
                dma("act", lambda e, c=c, s=s, hf=hf, r=r: e.dma_start(
                    out=sx[r][:, :].rearrange("p (tl d) -> p tl d", tl=2),
                    in_=d_x[s].rearrange("(a p) d -> p a d", p=128)[:, (c * 2 + hf) * 2:(c * 2 + hf) * 2 + 2, :]),
                    writes=[f"sx{r}.0", f"sx{r}.1"], sem=f"dx{r}")
                next_load[0] += 1

        emit_xloads(4)

        def ln_batch_stats(stn, stt_, xtiles, xres):
            for i in range(4):
                act(lambda e, i=i, xt=xtiles[i]: e.activation(junk[0][:, :], xt, AF.Square, accum_out=stt_[:, 4 + i:5 + i]),
                    reads=[xres[i]], writes=["junk0", stn + ".q"])
                dve(lambda e, i=i, xt=xtiles[i]: e.tensor_scalar(junk[1][:, :], xt, 1.0, None, ALU.mult, ALU.add, accum_out=stt_[:, i:i + 1]),
                    reads=[xres[i]], writes=["junk1", stn + ".s"])
            dve(lambda e: e.tensor_scalar(stt_[:, 8:12], stt_[:, 0:4], -1.0 / H, None, ALU.mult, ALU.bypass),
                reads=[stn + ".s"], writes=[stn + ".nm"])
            dve(lambda e: e.tensor_tensor(wt1[:, 0:4], stt_[:, 8:12], stt_[:, 8:12], ALU.mult),
                reads=[stn + ".nm"], writes=["wt1"])
            dve(lambda e: e.tensor_scalar(wt2[:, 0:4], stt_[:, 4:8], 1.0 / H, None, ALU.mult, ALU.bypass),
                reads=[stn + ".q"], writes=["wt2"])
            dve(lambda e: e.tensor_tensor(wt2[:, 0:4], wt2[:, 0:4], wt1[:, 0:4], ALU.subtract),
                reads=["wt2", "wt1"], writes=["wt2"])
            dve(lambda e: e.tensor_scalar(wt2[:, 0:4], wt2[:, 0:4], EPS, None, ALU.add, ALU.bypass),
                reads=["wt2"], writes=["wt2"])
            act(lambda e: e.activation(wt2[:, 0:4], wt2[:, 0:4], AF.Sqrt),
                reads=["wt2"], writes=["wt2"])
            dve(lambda e: e.reciprocal(stt_[:, 12:16], wt2[:, 0:4]), reads=["wt2"], writes=[stn + ".rs"])
            dve(lambda e: e.tensor_tensor(stt_[:, 8:12], stt_[:, 8:12], stt_[:, 12:16], ALU.mult),
                reads=[stn + ".nm", stn + ".rs"], writes=[stn + ".nm"])

        def ln_tile_stats(srcap, srcres, jslot):
            act(lambda e: e.activation(junk[0][:, :], srcap, AF.Square, accum_out=st3[:, 17:18]),
                reads=[srcres], writes=["junk0", "st3.q2"])
            dve(lambda e: e.tensor_scalar(junk[1][:, :], srcap, 1.0, None, ALU.mult, ALU.add, accum_out=st3[:, 16:17]),
                reads=[srcres], writes=["junk1", "st3.s"])
            dve(lambda e: e.tensor_scalar(st3[:, 18:19], st3[:, 16:17], -1.0 / H, None, ALU.mult, ALU.bypass),
                reads=["st3.s"], writes=["st3.nm"])
            dve(lambda e: e.tensor_tensor(wt1[:, 0:1], st3[:, 18:19], st3[:, 18:19], ALU.mult),
                reads=["st3.nm"], writes=["wt1"])
            dve(lambda e: e.tensor_scalar(wt2[:, 0:1], st3[:, 17:18], 1.0 / H, None, ALU.mult, ALU.bypass),
                reads=["st3.q2"], writes=["wt2"])
            dve(lambda e: e.tensor_tensor(wt2[:, 0:1], wt2[:, 0:1], wt1[:, 0:1], ALU.subtract),
                reads=["wt2", "wt1"], writes=["wt2"])
            dve(lambda e: e.tensor_scalar(wt2[:, 0:1], wt2[:, 0:1], EPS, None, ALU.add, ALU.bypass),
                reads=["wt2"], writes=["wt2"])
            act(lambda e: e.activation(wt2[:, 0:1], wt2[:, 0:1], AF.Sqrt), reads=["wt2"], writes=["wt2"])
            dve(lambda e: e.reciprocal(st3[:, 19:20], wt2[:, 0:1]), reads=["wt2"], writes=["st3.rs"])
            dve(lambda e: e.tensor_tensor(st3[:, 18:19], st3[:, 18:19], st3[:, 19:20], ALU.mult),
                reads=["st3.nm", "st3.rs"], writes=["st3.nm"])

        NU = NCH * SEQS
        units = [(c, s) for c in range(NCH) for s in range(SEQS)]

        def unit_tiles(u):
            c, s = units[u]
            r0 = slot_of[(c, s, 0)]
            r1 = slot_of[(c, s, 1)]
            xres = [f"sx{r0}.0", f"sx{r0}.1", f"sx{r1}.0", f"sx{r1}.1"]
            xtiles = [sx[r0][:, 0:H], sx[r0][:, H:2 * H], sx[r1][:, 0:H], sx[r1][:, H:2 * H]]
            return c, s, xres, xtiles

        def fe1a(u):
            c, s, xres, xtiles = unit_tiles(u)
            for i in range(4):
                act(lambda e, i=i, xt=xtiles[i]: e.activation(junk[0][:, :], xt, AF.Square, accum_out=st1[:, 4 + i:5 + i]),
                    reads=[xres[i]], writes=["junk0", "st1.q"])
                dve(lambda e, i=i, xt=xtiles[i]: e.tensor_scalar(junk[1][:, :], xt, 1.0, None, ALU.mult, ALU.add, accum_out=st1[:, i:i + 1]),
                    reads=[xres[i]], writes=["junk1", "st1.s"])

        def fe1b(u):
            c, s, xres, xtiles = unit_tiles(u)
            dve(lambda e: e.tensor_scalar(st1[:, 8:12], st1[:, 0:4], -1.0 / H, None, ALU.mult, ALU.bypass),
                reads=["st1.s"], writes=["st1.nm"])
            dve(lambda e: e.tensor_tensor(wt1[:, 0:4], st1[:, 8:12], st1[:, 8:12], ALU.mult),
                reads=["st1.nm"], writes=["wt1"])
            dve(lambda e: e.tensor_scalar(wt2[:, 0:4], st1[:, 4:8], 1.0 / H, None, ALU.mult, ALU.bypass),
                reads=["st1.q"], writes=["wt2"])
            dve(lambda e: e.tensor_tensor(wt2[:, 0:4], wt2[:, 0:4], wt1[:, 0:4], ALU.subtract),
                reads=["wt2", "wt1"], writes=["wt2"])
            dve(lambda e: e.tensor_scalar(wt2[:, 0:4], wt2[:, 0:4], EPS, None, ALU.add, ALU.bypass),
                reads=["wt2"], writes=["wt2"])
            act(lambda e: e.activation(wt2[:, 0:4], wt2[:, 0:4], AF.Sqrt), reads=["wt2"], writes=["wt2"])
            dve(lambda e: e.reciprocal(st1[:, 12:16], wt2[:, 0:4]), reads=["wt2"], writes=["st1.rs"])
            dve(lambda e: e.tensor_tensor(st1[:, 8:12], st1[:, 8:12], st1[:, 12:16], ALU.mult),
                reads=["st1.nm", "st1.rs"], writes=["st1.nm"])
            for i in range(4):
                act(lambda e, i=i, xt=xtiles[i]: e.activation(n1tm[:, i * H:(i + 1) * H], xt, AF.Identity,
                                                scale=st1[:, 12 + i:13 + i], bias=st1[:, 8 + i:9 + i]),
                    reads=[xres[i], "st1.rs", "st1.nm"], writes=["n1tm"])

        def t1(u):
            for i in range(4):
                bk = i % 2
                for k in range(8):
                    pe(lambda e, i=i, k=k, bk=bk: e.transpose(
                        pt[bk][:, k * 128:(k + 1) * 128], n1tm[:, i * H + k * 128: i * H + (k + 1) * 128], id16[:, :]),
                        reads=["n1tm", "id16"], writes=[PT[bk]])
                outap = n1fm.ap().rearrange("q (k j) -> q k j", k=8)[:, :, i * 128:(i + 1) * 128]
                dve(lambda e, bk=bk, outap=outap: e.tensor_copy(outap, pt[bk][:, :]),
                    reads=[PT[bk]], writes=["n1fm"])

        def dterms(u):
            for m in range(4):
                dve(lambda e, m=m: e.tensor_tensor(dtm[:, m * H:(m + 1) * H], drep[:, :], n1tm[:, m * H:(m + 1) * H], ALU.mult),
                    reads=["drep", "n1tm"], writes=["dtm"])

        def scan_pre(u):
            c, s, xres, xtiles = unit_tiles(u)
            P1c, P2c = P1s[s], P2s[s]
            rP1, rP2 = f"P1.{s}", f"P2.{s}"
            dve(lambda e, P1c=P1c: e.tensor_tensor(wq2[:, :], cc["q2a"][:, :], P1c[:, :], ALU.mult), reads=["cc.q2a", rP1], writes=["wq2"])
            dve(lambda e, P2c=P2c: e.tensor_tensor(wt1[:, 0:4], cc["q2b"][:, :], P2c[:, :], ALU.mult), reads=["cc.q2b", rP2], writes=["wt1"])
            dve(lambda e: e.tensor_tensor(wq2[:, :], wq2[:, :], wt1[:, 0:4], ALU.add), reads=["wq2", "wt1"], writes=["wq2"])
            dve(lambda e, P1c=P1c: e.tensor_tensor(wq3[:, :], cc["q3a"][:, :], P1c[:, :], ALU.mult), reads=["cc.q3a", rP1], writes=["wq3"])
            dve(lambda e, P2c=P2c: e.tensor_tensor(wt1[:, 0:4], cc["q3b"][:, :], P2c[:, :], ALU.mult), reads=["cc.q3b", rP2], writes=["wt1"])
            dve(lambda e: e.tensor_tensor(wq3[:, :], wq3[:, :], wt1[:, 0:4], ALU.add), reads=["wq3", "wt1"], writes=["wq3"])
            dve(lambda e, P2c=P2c: e.tensor_tensor(wR1[:, :], cc["R1b"][:, :], P2c[:, :], ALU.mult), reads=["cc.R1b", rP2], writes=["wR1"])
            dve(lambda e: e.tensor_tensor(wR1[:, :], wR1[:, :], wq2[:, :], ALU.add), reads=["wR1", "wq2"], writes=["wR1"])
            dve(lambda e, P2c=P2c: e.tensor_tensor(wR2[:, :], cc["R2a"][:, :], P2c[:, :], ALU.mult), reads=["cc.R2a", rP2], writes=["wR2"])
            dve(lambda e: e.tensor_tensor(wt1[:, 0:4], cc["R2b"][:, :], wq2[:, :], ALU.mult), reads=["cc.R2b", "wq2"], writes=["wt1"])
            dve(lambda e: e.tensor_tensor(wR2[:, :], wR2[:, :], wt1[:, 0:4], ALU.add), reads=["wR2", "wt1"], writes=["wR2"])
            dve(lambda e: e.tensor_tensor(dlc[:, :], cc["i11"][:, :], wR1[:, :], ALU.mult), reads=["cc.i11", "wR1"], writes=["dlc"])
            dve(lambda e: e.tensor_tensor(wt1[:, 0:4], cc["i12"][:, :], wR2[:, :], ALU.mult), reads=["cc.i12", "wR2"], writes=["wt1"])
            dve(lambda e: e.tensor_tensor(dlc[:, :], dlc[:, :], wt1[:, 0:4], ALU.add), reads=["dlc", "wt1"], writes=["dlc"])
            dve(lambda e: e.tensor_tensor(dls[:, :], cc["i21"][:, :], wR1[:, :], ALU.mult), reads=["cc.i21", "wR1"], writes=["dls"])
            dve(lambda e: e.tensor_tensor(wt1[:, 0:4], cc["i22"][:, :], wR2[:, :], ALU.mult), reads=["cc.i22", "wR2"], writes=["wt1"])
            dve(lambda e: e.tensor_tensor(dls[:, :], dls[:, :], wt1[:, 0:4], ALU.add), reads=["dls", "wt1"], writes=["dls"])

        def bu_scan(u):
            for b in range(4):
                bk = b % 2
                hp = b % 2
                for k in range(8):
                    pe(lambda e, b=b, k=k, bk=bk: e.matmul(
                        pb[bk][:, :], cB[:, k * 512 + b * 128: k * 512 + (b + 1) * 128],
                        n1fm[:, k * 512:(k + 1) * 512], start=(k == 0), stop=(k == 7)),
                        reads=["cB", "n1fm"], writes=[PB[bk]])
                dve(lambda e, b=b, bk=bk, hp=hp: e.tensor_tensor(u1[b][:, :], pb[bk][:, :], g_a[:, hp * T:(hp + 1) * T], ALU.mult),
                    reads=[PB[bk], "ga"], writes=[f"u1.{b}"])
                dve(lambda e, b=b, bk=bk, hp=hp: e.tensor_tensor(u2[b][:, :], pb[bk][:, :], g_b[:, hp * T:(hp + 1) * T], ALU.mult),
                    reads=[PB[bk], "gb"], writes=[f"u2.{b}"])
            for b in range(4):
                hp = b % 2
                dve(lambda e, b=b: e.tensor_tensor_scan(u1[b][:, :], u1[b][:, :], u1[b][:, :], dlc[:, b:b + 1], ALU.add, ALU.bypass),
                    reads=[f"u1.{b}", "dlc"], writes=[f"u1.{b}"])
                dve(lambda e, b=b: e.tensor_tensor_scan(u2[b][:, :], u2[b][:, :], u2[b][:, :], dls[:, b:b + 1], ALU.add, ALU.bypass),
                    reads=[f"u2.{b}", "dls"], writes=[f"u2.{b}"])
                dve(lambda e, b=b: e.tensor_copy(pce[:, b:b + 1], u1[b][:, T - 1:T]),
                    reads=[f"u1.{b}"], writes=["pce"])
                dve(lambda e, b=b: e.tensor_copy(pse[:, b:b + 1], u2[b][:, T - 1:T]),
                    reads=[f"u2.{b}"], writes=["pse"])
                dve(lambda e, b=b, hp=hp: e.tensor_tensor(u1[b][:, :], u1[b][:, :], g_G1[:, hp * T:(hp + 1) * T], ALU.mult),
                    reads=[f"u1.{b}", "gG1"], writes=[f"u1.{b}"])
                dve(lambda e, b=b, hp=hp: e.tensor_tensor(u2[b][:, :], u2[b][:, :], g_G2[:, hp * T:(hp + 1) * T], ALU.mult),
                    reads=[f"u2.{b}", "gG2"], writes=[f"u2.{b}"])
                dve(lambda e, b=b: e.tensor_tensor(ys[:, b * T:(b + 1) * T], u1[b][:, :], u2[b][:, :], ALU.add),
                    reads=[f"u1.{b}", f"u2.{b}"], writes=[f"ys.{b}"])

        def carry_update(u):
            c, s, xres, xtiles = unit_tiles(u)
            P1c, P2c = P1s[s], P2s[s]
            rP1, rP2 = f"P1.{s}", f"P2.{s}"
            dve(lambda e, P2c=P2c: e.tensor_tensor(P2c[:, :], cc["G1e"][:, :], pce[:, :], ALU.mult), reads=["cc.G1e", "pce"], writes=[rP2])
            dve(lambda e: e.tensor_tensor(wt2[:, 0:4], cc["G2e"][:, :], pse[:, :], ALU.mult), reads=["cc.G2e", "pse"], writes=["wt2"])
            dve(lambda e, P2c=P2c: e.tensor_tensor(P2c[:, :], P2c[:, :], wt2[:, 0:4], ALU.add), reads=[rP2, "wt2"], writes=[rP2])
            dve(lambda e: e.tensor_tensor(pce[:, :], pce[:, :], dlc[:, :], ALU.subtract), reads=["pce", "dlc"], writes=["pce"])
            dve(lambda e: e.tensor_tensor(pse[:, :], pse[:, :], dls[:, :], ALU.subtract), reads=["pse", "dls"], writes=["pse"])
            dve(lambda e, P1c=P1c: e.tensor_tensor(wt1[:, 0:4], cc["cAT"][:, :], P1c[:, :], ALU.mult), reads=["cc.cAT", rP1], writes=["wt1"])
            dve(lambda e: e.tensor_tensor(wt2[:, 0:4], cc["cBT"][:, :], wq3[:, :], ALU.mult), reads=["cc.cBT", "wq3"], writes=["wt2"])
            dve(lambda e: e.tensor_tensor(wt1[:, 0:4], wt1[:, 0:4], wt2[:, 0:4], ALU.add), reads=["wt1", "wt2"], writes=["wt1"])
            dve(lambda e, P1c=P1c: e.tensor_tensor(P1c[:, :], cc["g1a"][:, :], pce[:, :], ALU.mult), reads=["cc.g1a", "pce"], writes=[rP1])
            dve(lambda e: e.tensor_tensor(wt2[:, 0:4], cc["g1b"][:, :], pse[:, :], ALU.mult), reads=["cc.g1b", "pse"], writes=["wt2"])
            dve(lambda e, P1c=P1c: e.tensor_tensor(P1c[:, :], P1c[:, :], wt2[:, 0:4], ALU.add), reads=[rP1, "wt2"], writes=[rP1])
            dve(lambda e, P1c=P1c: e.tensor_tensor(P1c[:, :], P1c[:, :], wt1[:, 0:4], ALU.add), reads=[rP1, "wt1"], writes=[rP1])

        def cmm_one(u, m):
            mb = m % 2
            for kb in range(4):
                for ng in range(2):
                    pe(lambda e, m=m, kb=kb, ng=ng, mb=mb: e.matmul(
                        pcm[mb][:, ng * 512:(ng + 1) * 512],
                        ys[:, kb * T + m * 128: kb * T + (m + 1) * 128],
                        cC[:, kb * 1024 + ng * 512: kb * 1024 + (ng + 1) * 512],
                        start=(kb == 0), stop=(kb == 3)),
                        reads=[f"ys.{kb}", "cC"], writes=[PCM[mb][ng]])

        def mixes(u):
            c, s, xres, xtiles = unit_tiles(u)
            mixt = [(mo[0][:, 0:H], "mo0"), (mo[1][:, 0:H], "mo1"),
                    (outb[0][:, :], "outb0"), (outb[1][:, :], "outb1")]
            return mixt

        def mix_one(u, m, mixt):
            mb = m % 2
            mt, mres = mixt[m]
            dve(lambda e, m=m, mb=mb, mt=mt: e.tensor_tensor(mt, pcm[mb][:, :], dtm[:, m * H:(m + 1) * H], ALU.add),
                reads=[PCM[mb][0], PCM[mb][1], "dtm"], writes=[mres])

        def ln3_batch(u, mixt):
            c, s, xres, xtiles = unit_tiles(u)
            for m in range(4):
                mt, mres = mixt[m]
                act(lambda e, m=m, mt=mt: e.activation(junk[0][:, :], mt, AF.Square, accum_out=st3[:, 4 + m:5 + m]),
                    reads=[mres], writes=["junk0", "st3b.q"])
                dve(lambda e, m=m, mt=mt: e.tensor_scalar(junk[1][:, :], mt, 1.0, None, ALU.mult, ALU.add, accum_out=st3[:, m:m + 1]),
                    reads=[mres], writes=["junk1", "st3b.s"])
            dve(lambda e: e.tensor_scalar(st3[:, 8:12], st3[:, 0:4], -1.0 / H, None, ALU.mult, ALU.bypass),
                reads=["st3b.s"], writes=["st3b.nm"])
            dve(lambda e: e.tensor_tensor(wt1[:, 0:4], st3[:, 8:12], st3[:, 8:12], ALU.mult),
                reads=["st3b.nm"], writes=["wt1"])
            dve(lambda e: e.tensor_scalar(wt2[:, 0:4], st3[:, 4:8], 1.0 / H, None, ALU.mult, ALU.bypass),
                reads=["st3b.q"], writes=["wt2"])
            dve(lambda e: e.tensor_tensor(wt2[:, 0:4], wt2[:, 0:4], wt1[:, 0:4], ALU.subtract),
                reads=["wt2", "wt1"], writes=["wt2"])
            dve(lambda e: e.tensor_scalar(wt2[:, 0:4], wt2[:, 0:4], EPS, None, ALU.add, ALU.bypass),
                reads=["wt2"], writes=["wt2"])
            act(lambda e: e.activation(wt2[:, 0:4], wt2[:, 0:4], AF.Sqrt), reads=["wt2"], writes=["wt2"])
            dve(lambda e: e.reciprocal(st3[:, 12:16], wt2[:, 0:4]), reads=["wt2"], writes=["st3b.rs"])
            dve(lambda e: e.tensor_tensor(st3[:, 8:12], st3[:, 8:12], st3[:, 12:16], ALU.mult),
                reads=["st3b.nm", "st3b.rs"], writes=["st3b.nm"])
            for m in range(4):
                mt, mres = mixt[m]
                tb = m % 2
                act(lambda e, m=m, mt=mt, tb=tb: e.activation(tt[tb][:, :], mt, AF.Identity,
                                                              scale=st3[:, 12 + m:13 + m], bias=st3[:, 8 + m:9 + m]),
                    reads=[mres, "st3b.rs", "st3b.nm"], writes=[f"tt{tb}"])
                dve(lambda e, m=m, tb=tb, xt=xtiles[m]: e.tensor_tensor(xt, xt, tt[tb][:, :], ALU.add),
                    reads=[xres[m], f"tt{tb}"], writes=[xres[m]])

        def ln2_batch(u):
            c, s, xres, xtiles = unit_tiles(u)
            for m in range(4):
                act(lambda e, m=m, xt=xtiles[m]: e.activation(junk[0][:, :], xt, AF.Square, accum_out=st2[:, 4 + m:5 + m]),
                    reads=[xres[m]], writes=["junk0", "st2.q"])
                dve(lambda e, m=m, xt=xtiles[m]: e.tensor_scalar(junk[1][:, :], xt, 1.0, None, ALU.mult, ALU.add, accum_out=st2[:, m:m + 1]),
                    reads=[xres[m]], writes=["junk1", "st2.s"])
            dve(lambda e: e.tensor_scalar(st2[:, 8:12], st2[:, 0:4], -1.0 / H, None, ALU.mult, ALU.bypass),
                reads=["st2.s"], writes=["st2.nm"])
            dve(lambda e: e.tensor_tensor(wt1[:, 0:4], st2[:, 8:12], st2[:, 8:12], ALU.mult),
                reads=["st2.nm"], writes=["wt1"])
            dve(lambda e: e.tensor_scalar(wt2[:, 0:4], st2[:, 4:8], 1.0 / H, None, ALU.mult, ALU.bypass),
                reads=["st2.q"], writes=["wt2"])
            dve(lambda e: e.tensor_tensor(wt2[:, 0:4], wt2[:, 0:4], wt1[:, 0:4], ALU.subtract),
                reads=["wt2", "wt1"], writes=["wt2"])
            dve(lambda e: e.tensor_scalar(wt2[:, 0:4], wt2[:, 0:4], EPS, None, ALU.add, ALU.bypass),
                reads=["wt2"], writes=["wt2"])
            act(lambda e: e.activation(wt2[:, 0:4], wt2[:, 0:4], AF.Sqrt), reads=["wt2"], writes=["wt2"])
            dve(lambda e: e.reciprocal(st2[:, 12:16], wt2[:, 0:4]), reads=["wt2"], writes=["st2.rs"])
            dve(lambda e: e.tensor_tensor(st2[:, 8:12], st2[:, 8:12], st2[:, 12:16], ALU.mult),
                reads=["st2.nm", "st2.rs"], writes=["st2.nm"])
            for m in range(4):
                act(lambda e, m=m, xt=xtiles[m]: e.activation(n2tm[m][:, :], xt, AF.Identity,
                                                              scale=st2[:, 12 + m:13 + m], bias=st2[:, 8 + m:9 + m]),
                    reads=[xres[m], "st2.rs", "st2.nm"], writes=[f"n2tm{m}"])

        def t3_one(u, m):
            nb = m
            bk = m % 2
            for k in range(8):
                pe(lambda e, k=k, nb=nb, bk=bk: e.transpose(pt[bk][:, k * 128:(k + 1) * 128],
                                                     n2tm[nb][:, k * 128:(k + 1) * 128], id16[:, :]),
                    reads=[f"n2tm{nb}", "id16"], writes=[PT[bk]])
            outap = n2fm.ap().rearrange("q (k j) -> q k j", k=8)[:, :, m * 128:(m + 1) * 128]
            dve(lambda e, bk=bk, outap=outap: e.tensor_copy(outap, pt[bk][:, :]),
                reads=[PT[bk]], writes=["n2fm"])

        def mixer_back(u, next_u):
            mixt = mixes(u)
            cmm_one(u, 0)
            cmm_one(u, 1)
            mix_one(u, 0, mixt)
            cmm_one(u, 2)
            mix_one(u, 1, mixt)
            cmm_one(u, 3)
            mix_one(u, 2, mixt)
            mix_one(u, 3, mixt)
            ln3_batch(u, mixt)
            ln2_batch(u)
            if next_u is not None:
                t1(next_u)
            t3_one(u, 0)
            t3_one(u, 1)
            t3_one(u, 2)
            t3_one(u, 3)

        wload_state = {"next": 0}
        NPIECE_TOTAL = None

        def emit_piece_load(g):
            if NPIECE_TOTAL is not None and g >= NPIECE_TOTAL[0]:
                return
            piece = g % 8
            wr = g % 2
            dma("sp", lambda e, piece=piece, wr=wr: e.dma_start(
                out=w1s[wr][:, :], in_=s_w1t[piece, :, :]),
                reads=["s_w1t"], writes=[f"w1s{wr}"], sem=f"dw1{wr}")
            dma("sp", lambda e, piece=piece, wr=wr: e.dma_start(
                out=w2s[wr][:, :], in_=s_w2t[piece, :, :]),
                reads=["s_w2t"], writes=[f"w2s{wr}"], sem=f"dw2{wr}")

        def ffn(u):
            c, s, xres, xtiles = unit_tiles(u)
            for hf in range(2):
                gbase = (u * 2 + hf) * 8

                def ffn1(hb):
                    piece, hbq, bk = hb // 4, hb % 4, hb % 2
                    wr = (gbase + piece) % 2
                    for k in range(8):
                        pe(lambda e, k=k, bk=bk, wr=wr, hbq=hbq, hf=hf: e.matmul(
                            pb[bk][:, 0:256],
                            w1s[wr][:, k * 512 + hbq * 128: k * 512 + hbq * 128 + 128],
                            n2fm[:, k * 512 + hf * 256: k * 512 + hf * 256 + 256],
                            start=(k == 0), stop=(k == 7)),
                            reads=[f"w1s{wr}", "n2fm"], writes=[PB[bk]])
                    gs = hb % 4
                    act(lambda e, hb=hb, bk=bk, gs=gs: e.activation(
                        gel[gs][:, :], pb[bk][:, 0:256], AF.Gelu, bias=b1col[:, hb:hb + 1],
                        scale=a1col[:, hb:hb + 1]),
                        reads=[PB[bk], "b1col", "a1col"], writes=[f"gel{gs}"])

                def ffn2(hb):
                    piece, hbq, gs = hb // 4, hb % 4, hb % 4
                    wr = (gbase + piece) % 2
                    for mg in range(2):
                        for ng in range(2):
                            pe(lambda e, hb=hb, mg=mg, ng=ng, wr=wr, gs=gs, hbq=hbq: e.matmul(
                                pcm[mg][:, ng * 512:(ng + 1) * 512],
                                gel[gs][:, mg * 128:(mg + 1) * 128],
                                w2s[wr][:, hbq * 1024 + ng * 512: hbq * 1024 + (ng + 1) * 512],
                                start=(hb == 0), stop=(hb == 31), skip_group_check=True),
                                reads=[f"gel{gs}", f"w2s{wr}"], writes=[PCM[mg][ng]])

                ffn1(0)
                for hb in range(1, 32):
                    ffn1(hb)
                    ffn2(hb - 1)
                    if (hb - 1) % 4 == 3:
                        emit_piece_load(gbase + (hb - 1) // 4 + 2)
                ffn2(31)
                emit_piece_load(gbase + 9)
                for mg in range(2):
                    m = hf * 2 + mg
                    dve(lambda e, mg=mg: e.tensor_tensor(mo[mg][:, 0:H], pcm[mg][:, :], b2rep[:, :], ALU.add),
                        reads=[PCM[mg][0], PCM[mg][1], "b2rep"], writes=[f"mo{mg}"])
                    ln_tile_stats(mo[mg][:, 0:H], f"mo{mg}", mg)
                    act(lambda e, mg=mg: e.activation(tt[mg][:, :], mo[mg][:, 0:H], AF.Identity, scale=st3[:, 19:20], bias=st3[:, 18:19]),
                        reads=[f"mo{mg}", "st3.rs", "st3.nm"], writes=[f"tt{mg}"])
                    dve(lambda e, m=m, mg=mg, xt=xtiles[m]: e.tensor_tensor(outb[mg][:, :], xt, tt[mg][:, :], ALU.add),
                        reads=[xres[m], f"tt{mg}"], writes=[f"outb{mg}"])
                    tok_tile = c * 4 + hf * 2 + mg
                    dma("gp", lambda e, s=s, tok_tile=tok_tile, mg=mg: e.dma_start(
                        out=d_out[s].rearrange("(a p) d -> p a d", p=128)[:, tok_tile, :],
                        in_=outb[mg][:, :]),
                        reads=[f"outb{mg}"], writes=[], sem=f"do{mg}")
                emit_xloads(2 * u + 5 + hf)
                if hf == 1 and u + 2 < NU:
                    fe1a(u + 2)

        NPIECE_TOTAL = [NU * 16]
        fe1a(0)
        fe1b(0)
        t1(0)
        dterms(0)
        scan_pre(0)
        bu_scan(0)
        carry_update(0)
        emit_xloads(4)
        wbuf = [(outb[0][:, :], "outb0"), (outb[1][:, :], "outb1"), (tt[0][:, :], "tt0"), (tt[1][:, :], "tt1")]
        for chunk in range(8):
            for blk in range(4):
                buf, bres = wbuf[blk]
                dma("sp", lambda e, chunk=chunk, blk=blk, buf=buf: e.dma_start(
                    out=buf, in_=d_w1[chunk * 512 + blk * 128: chunk * 512 + (blk + 1) * 128, :]),
                    writes=[bres], sem=f"dwl{blk}")
            for blk in range(4):
                buf, bres = wbuf[blk]
                hb = chunk * 4 + blk
                act(lambda e, buf=buf, hb=hb, blk=blk:
                    e.activation(junk[blk % 2][:, :], buf, AF.Abs, accum_out=a1col[:, hb:hb + 1]),
                    reads=[bres], writes=[f"junk{blk % 2}", "a1col"])
                act(lambda e, buf=buf, blk=blk:
                    e.activation(n1tm[:, blk * 1024:(blk + 1) * 1024], buf, AF.Sign),
                    reads=[bres], writes=["n1tm"])
            dma("gp", lambda e, chunk=chunk: e.dma_start(
                out=s_w1b[chunk * 512:(chunk + 1) * 512, :].rearrange("(a p) d -> p a d", p=128),
                in_=n1tm[:, :].rearrange("p (a d) -> p a d", a=4)),
                reads=["n1tm"], writes=["s_w1b"], sem="dws")
        dve(lambda e: e.tensor_scalar(a1col[:, :], a1col[:, :], 1.0 / H, None, ALU.mult, ALU.bypass),
            reads=["a1col"], writes=["a1col"])
        for chunk in range(8):
            for q4 in range(4):
                buf, bres = wbuf[q4]
                dma("sp", lambda e, chunk=chunk, q4=q4, buf=buf: e.dma_start(
                    out=buf, in_=d_w2[chunk * 128:(chunk + 1) * 128, q4 * 1024:(q4 + 1) * 1024]),
                    writes=[bres], sem=f"dwl{q4}")
            for q4 in range(4):
                buf, bres = wbuf[q4]
                acc = st3[:, 4 + q4:5 + q4]
                act(lambda e, q4=q4, buf=buf, acc=acc:
                    e.activation(junk[q4 % 2][:, :], buf, AF.Abs, accum_out=acc),
                    reads=[bres], writes=[f"junk{q4 % 2}", "st3.q"])
                act(lambda e, q4=q4, buf=buf:
                    e.activation(n1tm[:, q4 * 1024:(q4 + 1) * 1024], buf, AF.Sign),
                    reads=[bres], writes=["n1tm"])
            dve(lambda e: e.tensor_tensor(st3[:, 4:5], st3[:, 4:5], st3[:, 5:6], ALU.add), reads=["st3.q"], writes=["st3.q"])
            dve(lambda e: e.tensor_tensor(st3[:, 6:7], st3[:, 6:7], st3[:, 7:8], ALU.add), reads=["st3.q"], writes=["st3.q"])
            dve(lambda e: e.tensor_tensor(st3[:, 4:5], st3[:, 4:5], st3[:, 6:7], ALU.add), reads=["st3.q"], writes=["st3.q"])
            dve(lambda e: e.tensor_scalar(st3[:, 4:5], st3[:, 4:5], 1.0 / HF, None, ALU.mult, ALU.bypass), reads=["st3.q"], writes=["st3.q"])
            dve(lambda e: e.tensor_scalar(n1tm[:, :], n1tm[:, :], st3[:, 4:5], None, ALU.mult, ALU.bypass),
                reads=["n1tm", "st3.q"], writes=["n1tm"])
            dma("gp", lambda e, chunk=chunk: e.dma_start(
                out=s_w2b[chunk * 128:(chunk + 1) * 128, :], in_=n1tm[:, :]),
                reads=["n1tm"], writes=["s_w2b"], sem="dws")
        trb1 = [("n1fm", n1fm), ("n2fm", n2fm)]
        for kb in range(8):
            nm, buf = trb1[kb % 2]
            dma("sp", lambda e, kb=kb, buf=buf: e.dma_start(
                out=buf[:, :], in_=s_w1b[:, kb * 128:(kb + 1) * 128], transpose=True),
                reads=["s_w1b"], writes=[nm], sem=f"dtr{kb % 2}")
            dma("gp", lambda e, kb=kb, buf=buf: e.dma_start(
                out=s_w1t[:, :, kb * 512:(kb + 1) * 512].rearrange("j p h -> p j h"),
                in_=buf[:, :].rearrange("p (j h) -> p j h", j=8)),
                reads=[nm], writes=["s_w1t"], sem=f"dwb{kb % 2}")
        trb2 = [("n1fm", n1fm[:, 0:1024]), ("n2fm", n2fm[:, 0:1024]), ("n1tm", n1tm[:, 0:1024])]
        for hb in range(32):
            nm, bap = trb2[hb % 3]
            dma("sp", lambda e, hb=hb, bap=bap: e.dma_start(
                out=bap, in_=s_w2b[:, hb * 128:(hb + 1) * 128], transpose=True),
                reads=["s_w2b"], writes=[nm], sem=f"dtr{hb % 3}")
            dma("gp", lambda e, hb=hb, bap=bap: e.dma_start(
                out=s_w2t[hb // 4, :, (hb % 4) * 1024:(hb % 4 + 1) * 1024], in_=bap),
                reads=[nm], writes=["s_w2t"], sem=f"dwb{hb % 3}")

        emit_piece_load(0)
        emit_piece_load(1)
        if NU > 1:
            fe1a(1)
        for u in range(NU):
            if u > 0:
                dterms(u)
            if u + 1 < NU:
                fe1b(u + 1)
            mixer_back(u, u + 1 if u + 1 < NU else None)
            if u + 1 < NU:
                scan_pre(u + 1)
                bu_scan(u + 1)
                carry_update(u + 1)
            ffn(u)

        last_do = [B.dma_sem_last.get("do0"), B.dma_sem_last.get("do1")]
        fin = B.emit("sp", lambda e: e.nop(), reads=(), writes=())
        for d in last_do:
            if d is not None:
                fin.extra.append(d)

        B.finalize_counts()
        sems = {}
        for s_ in sorted(B.sem_names):
            sems[s_] = es.enter_context(nc.semaphore(s_))
        block = es.enter_context(nc.Block())
        block.tensor(lambda e: B.run("pe", e, sems))
        block.vector(lambda e: B.run("dve", e, sems))
        block.scalar(lambda e: B.run("act", e, sems))
        block.sync(lambda e: B.run("sp", e, sems))
        block.gpsimd(lambda e: B.run("gp", e, sems))
        nc._B = B
    return nc


def kernel(L0_attn_input_sequence, L0_attn_steps_raw, L0_attn_G_raw, L0_attn_A_raw,
           L0_attn_B_mat, L0_attn_C_mat, L0_attn_D_vec,
           L0_ffn_w1, L0_ffn_b1, L0_ffn_w2, L0_ffn_b2):
    x = np.ascontiguousarray(np.asarray(L0_attn_input_sequence, dtype=np.float32))
    n_cores = 8
    params = {
        "steps_raw": np.ascontiguousarray(np.asarray(L0_attn_steps_raw, dtype=np.float32)),
        "G_raw": np.ascontiguousarray(np.asarray(L0_attn_G_raw, dtype=np.float32)),
        "A_raw": np.ascontiguousarray(np.asarray(L0_attn_A_raw, dtype=np.float32)),
        "B_mat": np.ascontiguousarray(np.asarray(L0_attn_B_mat, dtype=np.float32)),
        "C_mat": np.ascontiguousarray(np.asarray(L0_attn_C_mat, dtype=np.float32)),
        "D_vec": np.ascontiguousarray(np.asarray(L0_attn_D_vec, dtype=np.float32)),
        "w1": np.ascontiguousarray(np.asarray(L0_ffn_w1, dtype=np.float32)),
        "b1": np.ascontiguousarray(np.asarray(L0_ffn_b1, dtype=np.float32)),
        "w2": np.ascontiguousarray(np.asarray(L0_ffn_w2, dtype=np.float32)),
        "b2": np.ascontiguousarray(np.asarray(L0_ffn_b2, dtype=np.float32)),
        "ident": np.eye(128, dtype=np.float32),
    }
    nc = bass.Bass("TRN2", target_bir_lowering=False, debug=False)
    build_program(nc)
    in_maps = [dict(x=x[2 * i:2 * i + 2], **params) for i in range(n_cores)]
    res = run_bass_kernel_spmd(nc, in_maps, core_ids=list(range(n_cores)))
    return np.concatenate([np.asarray(r["out"], dtype=np.float32) for r in res.results], axis=0)


# version 24
# speedup vs baseline: 1.0599x; 1.0599x over previous
import numpy as np
from contextlib import ExitStack

import concourse.bass as bass
import concourse.mybir as mybir
from concourse.bass_utils import run_bass_kernel_spmd

F32 = mybir.dt.float32
BF16 = mybir.dt.bfloat16
AF = mybir.ActivationFunctionType
ALU = mybir.AluOpType
AX = mybir.AxisListType

H = 1024
P = 256
HF = 4096
T = 512
EPS = 1e-5

QUEUES = ("pe", "dve", "act", "sp", "gp")


class Op:
    __slots__ = ("fn", "reads", "writes", "sem", "need_inc", "inc_by", "count", "extra")

    def __init__(self, fn, reads, writes, sem, inc_by, extra):
        self.fn = fn
        self.reads = reads
        self.writes = writes
        self.sem = sem
        self.need_inc = inc_by > 1
        self.inc_by = inc_by
        self.count = None
        self.extra = extra


class Build:
    def __init__(self):
        self.ops = {q: [] for q in QUEUES}
        self.last_writer = {}
        self.readers = {}
        self.dma_sem_last = {}
        self.sem_names = set(QUEUES)

    def emit(self, q, fn, reads=(), writes=(), sem=None, inc_by=1, extra=()):
        if sem is None:
            sem = q
        self.sem_names.add(sem)
        op = Op(fn, tuple(reads), tuple(writes), sem, inc_by, list(extra))
        deps = []
        for r in op.reads:
            w = self.last_writer.get(r)
            if w is not None:
                deps.append(w)
        for r in op.writes:
            w = self.last_writer.get(r)
            if w is not None:
                deps.append(w)
            for rd in self.readers.get(r, {}).values():
                deps.append(rd)
        for d in deps:
            if d is op:
                continue
            if d.sem == "pe" and q == "pe" and sem == "pe":
                continue
            op.extra.append(d)
        if inc_by > 1:
            prev = self.dma_sem_last.get(sem)
            if prev is not None:
                op.extra.append(prev)
            self.dma_sem_last[sem] = op
        self.ops[q].append(op)
        for r in op.writes:
            self.last_writer[r] = op
            self.readers[r] = {}
        for r in op.reads:
            self.readers.setdefault(r, {})[op.sem] = op
        return op

    def finalize_counts(self):
        for q in QUEUES:
            for op in self.ops[q]:
                for d in op.extra:
                    if isinstance(d, Op):
                        d.need_inc = True
        cnt = {s: 0 for s in self.sem_names}
        for q in QUEUES:
            for op in self.ops[q]:
                if op.need_inc:
                    cnt[op.sem] += op.inc_by
                    op.count = cnt[op.sem]

    def run(self, q, eng, sems):
        waited = {}
        for op in self.ops[q]:
            for d in op.extra:
                if isinstance(d, Op):
                    if d.count is None:
                        continue
                    key, val = d.sem, d.count
                else:
                    key, val = d
                if waited.get(key, -1) >= val:
                    continue
                eng.wait_ge(sems[key], val)
                waited[key] = val
            inst = op.fn(eng)
            if op.need_inc and op.count is not None:
                inst.then_inc(sems[op.sem], op.inc_by)


def build_program(nc: bass.Bass, L=4096, SEQS=2):
    NCH = L // T
    d_x = nc.dram_tensor("x", [SEQS, L, H], F32, kind="ExternalInput").ap()
    d_steps = nc.dram_tensor("steps_raw", [P], F32, kind="ExternalInput").ap()
    d_G = nc.dram_tensor("G_raw", [P], F32, kind="ExternalInput").ap()
    d_A = nc.dram_tensor("A_raw", [P], F32, kind="ExternalInput").ap()
    d_B = nc.dram_tensor("B_mat", [P, H, 2], F32, kind="ExternalInput").ap()
    d_C = nc.dram_tensor("C_mat", [H, P, 2], F32, kind="ExternalInput").ap()
    d_D = nc.dram_tensor("D_vec", [H], F32, kind="ExternalInput").ap()
    d_w1 = nc.dram_tensor("w1", [HF, H], F32, kind="ExternalInput").ap()
    d_b1 = nc.dram_tensor("b1", [HF], F32, kind="ExternalInput").ap()
    d_w2 = nc.dram_tensor("w2", [H, HF], F32, kind="ExternalInput").ap()
    d_b2 = nc.dram_tensor("b2", [H], F32, kind="ExternalInput").ap()
    d_id = nc.dram_tensor("ident", [128, 128], F32, kind="ExternalInput").ap()
    d_out = nc.dram_tensor("out", [SEQS, L, H], F32, kind="ExternalOutput").ap()
    s_w1b = nc.dram_tensor("scr_w1b", [HF, H], BF16, kind="Internal").ap()
    s_w2b = nc.dram_tensor("scr_w2b", [H, HF], BF16, kind="Internal").ap()
    s_w1t = nc.dram_tensor("scr_w1t", [8, 128, HF], BF16, kind="Internal").ap()
    s_w2t = nc.dram_tensor("scr_w2t", [8, 128, 4 * H], BF16, kind="Internal").ap()

    es = ExitStack()
    with es:
        def sb(name, shape, dt):
            return es.enter_context(nc.sbuf_tensor(name, shape, dt))

        cB = sb("cB", [128, 8 * 512], BF16)
        cC = sb("cC", [128, 4 * H], BF16)
        g_a = sb("g_a", [128, 2 * T], F32)
        g_b = sb("g_b", [128, 2 * T], F32)
        g_G1 = sb("g_G1", [128, 2 * T], F32)
        g_G2 = sb("g_G2", [128, 2 * T], F32)
        drep = sb("drep", [128, H], BF16)
        b2rep = sb("b2rep", [128, H], F32)
        b1col = sb("b1col", [128, 32], F32)
        idf = sb("idf", [128, 128], F32)
        id16 = sb("id16", [128, 128], BF16)
        cc = {n: sb("cc_" + n, [128, 4], F32)
              for n in ("q2a", "q2b", "q3a", "q3b", "R1b", "R2a", "R2b",
                        "i11", "i12", "i21", "i22", "g1a", "g1b", "cAT", "cBT",
                        "G1e", "G2e")}
        P1s = [sb(f"P1_{s}", [128, 4], F32) for s in range(SEQS)]
        P2s = [sb(f"P2_{s}", [128, 4], F32) for s in range(SEQS)]
        dlc = sb("dlc", [128, 4], F32)
        dls = sb("dls", [128, 4], F32)
        wq2 = sb("wq2", [128, 4], F32)
        wq3 = sb("wq3", [128, 4], F32)
        wR1 = sb("wR1", [128, 4], F32)
        wR2 = sb("wR2", [128, 4], F32)
        wt1 = sb("wt1", [128, 4], F32)
        wt2 = sb("wt2", [128, 4], F32)
        pce = sb("pce", [128, 4], F32)
        pse = sb("pse", [128, 4], F32)
        pv = sb("pv", [128, 64], F32)
        st1 = sb("st1", [128, 16], F32)
        st2 = sb("st2", [128, 16], F32)
        st3 = sb("st3", [128, 24], F32)
        prow = sb("prow", [8, 128], F32)
        b1row = sb("b1row", [32, 128], F32)
        dr0 = sb("dr0", [1, H], F32)
        dr016 = sb("dr016", [1, H], BF16)
        b2r0 = sb("b2r0", [1, H], F32)
        ones16 = sb("ones16", [1, 128], BF16)
        onesf = sb("onesf", [1, 128], F32)
        sx = [sb(f"sx{r}", [128, 2 * H], F32) for r in range(4)]
        w1s = [sb(f"w1s{r}", [128, 8 * 512], BF16) for r in range(2)]
        w2s = [sb(f"w2s{r}", [128, 4 * H], BF16) for r in range(2)]
        outb = [sb(f"outb{r}", [128, H], F32) for r in range(2)]
        n1tm = sb("n1tm", [128, 4 * H], BF16)
        n1fm = sb("n1fm", [128, 8 * 512], BF16)
        n2fm = sb("n2fm", [128, 8 * 512], BF16)
        n2tm = [sb(f"n2tm{r}", [128, H], BF16) for r in range(4)]
        junk = [sb(f"junk{r}", [128, H], BF16) for r in range(2)]
        u1 = [sb(f"u1_{r}", [128, T], F32) for r in range(4)]
        u2 = [sb(f"u2_{r}", [128, T], F32) for r in range(4)]
        dtm = sb("dtm", [128, 4 * H], BF16)
        a1col = sb("a1col", [128, 32], F32)
        jdve = junk_placeholder = None
        ys = sb("ys", [128, 4 * T], BF16)
        mo = [sb(f"mo{r}", [128, 1056], F32) for r in range(2)]
        tt = [sb(f"tt{r}", [128, H], F32) for r in range(2)]
        gel = [sb(f"gel{r}", [128, 256], BF16) for r in range(4)]
        tabA = mo[0]
        tabB = mo[1]
        scw = u1[0]
        scw2 = u2[0]

        pb = [es.enter_context(nc.psum_tensor("pb0", [128, 512], F32)),
              es.enter_context(nc.psum_tensor("pb1", [128, 512], F32))]
        pcm = [es.enter_context(nc.psum_tensor("pcA", [128, 1024], F32)),
               es.enter_context(nc.psum_tensor("pcB", [128, 1024], F32))]
        pt = [es.enter_context(nc.psum_tensor("pt6", [128, 1024], BF16)),
              es.enter_context(nc.psum_tensor("pt7", [128, 1024], BF16))]

        B = Build()

        def pe(fn, reads=(), writes=()):
            return B.emit("pe", fn, reads, writes)

        def dve(fn, reads=(), writes=()):
            return B.emit("dve", fn, reads, writes)

        def act(fn, reads=(), writes=()):
            return B.emit("act", fn, reads, writes)

        def dma(q, fn, reads=(), writes=(), sem=None):
            return B.emit(q, fn, reads, writes, sem=sem, inc_by=16)

        PB = ["pb0", "pb1"]
        PCM = [["pcA0", "pcA1"], ["pcB0", "pcB1"]]
        PT = ["pt6", "pt7"]

        dma("sp", lambda e: e.dma_start(out=idf[:, :], in_=d_id), writes=["idf"], sem="dp")
        dma("sp", lambda e: e.dma_start(out=prow[0:2, :], in_=d_steps.rearrange("(h p) -> h p", p=128)),
            writes=["prow"], sem="dp")
        dma("sp", lambda e: e.dma_start(out=prow[2:4, :], in_=d_G.rearrange("(h p) -> h p", p=128)),
            writes=["prow"], sem="dp")
        dma("sp", lambda e: e.dma_start(out=prow[4:6, :], in_=d_A.rearrange("(h p) -> h p", p=128)),
            writes=["prow"], sem="dp")
        dma("sp", lambda e: e.dma_start(out=b1row[:, :], in_=d_b1.rearrange("(a p) -> a p", p=128)),
            writes=["b1row"], sem="dp")
        dma("sp", lambda e: e.dma_start(out=dr0[:, :], in_=d_D[None, :]), writes=["dr0"], sem="dp")
        dma("sp", lambda e: e.dma_start(out=b2r0[:, :], in_=d_b2[None, :]), writes=["b2r0"], sem="dp")
        dma("sp", lambda e: e.dma_start(out=sx[0][:, :], in_=d_B[0:128, :, :].rearrange("p h t -> p (h t)")),
            writes=["sx0.0", "sx0.1"], sem="dp")
        dma("sp", lambda e: e.dma_start(out=sx[1][:, :], in_=d_B[128:256, :, :].rearrange("p h t -> p (h t)")),
            writes=["sx1.0", "sx1.1"], sem="dp")

        dve(lambda e: e.tensor_copy(id16[:, :], idf[:, :]), reads=["idf"], writes=["id16"])
        dve(lambda e: e.memset(ones16[:, :], 1.0), writes=["ones16"])
        dve(lambda e: e.memset(onesf[:, :], 1.0), writes=["onesf"])
        for s in range(SEQS):
            dve(lambda e, s=s: e.memset(P1s[s][:, :], 0.0), writes=[f"P1.{s}"])
            dve(lambda e, s=s: e.memset(P2s[s][:, :], 0.0), writes=[f"P2.{s}"])
        dve(lambda e: e.tensor_copy(dr016[:, :], dr0[:, :]), reads=["dr0"], writes=["dr016"])

        pe(lambda e: e.transpose(pcm[0][:, 0:6], prow[0:6, :], idf[0:6, 0:6]),
           reads=["prow", "idf"], writes=[PCM[0][0]])
        dve(lambda e: e.tensor_copy(pv[:, 0:6], pcm[0][:, 0:6]), reads=[PCM[0][0]], writes=["pv"])
        pe(lambda e: e.transpose(pcm[1][:, 0:32], b1row[:, :], idf[0:32, 0:32]),
           reads=["b1row", "idf"], writes=[PCM[1][0]])
        dve(lambda e: e.tensor_copy(b1col[:, 0:32], pcm[1][:, 0:32]), reads=[PCM[1][0]], writes=["b1col"])

        pe(lambda e: e.matmul(pcm[1][:, 0:512], ones16[0:1, :], dr016[0:1, 0:512], start=True, stop=True),
           reads=["ones16", "dr016"], writes=[PCM[1][0]])
        pe(lambda e: e.matmul(pcm[1][:, 512:1024], ones16[0:1, :], dr016[0:1, 512:1024], start=True, stop=True),
           reads=["ones16", "dr016"], writes=[PCM[1][1]])
        dve(lambda e: e.tensor_copy(drep[:, :], pcm[1][:, :]), reads=[PCM[1][0], PCM[1][1]], writes=["drep"])
        pe(lambda e: e.matmul(pcm[0][:, 0:512], onesf[0:1, :], b2r0[0:1, 0:512], start=True, stop=True),
           reads=["onesf", "b2r0"], writes=[PCM[0][0]])
        pe(lambda e: e.matmul(pcm[0][:, 512:1024], onesf[0:1, :], b2r0[0:1, 512:1024], start=True, stop=True),
           reads=["onesf", "b2r0"], writes=[PCM[0][1]])
        dve(lambda e: e.tensor_copy(b2rep[:, :], pcm[0][:, :]), reads=[PCM[0][0], PCM[0][1]], writes=["b2rep"])

        PV = {}
        _n = [6]
        for nme in ("steps", "S", "r2", "r", "rinv", "alpha", "tau", "cth", "s2th",
                    "w2cr", "w1cr", "fscale", "det", "tmpa", "tmpb", "rstep",
                    "Cs", "Bs", "Bss"):
            PV[nme] = slice(_n[0], _n[0] + 2)
            _n[0] += 2
        RAW = {"steps_raw": slice(0, 2), "G_raw": slice(2, 4), "A_raw": slice(4, 6)}

        def pva(nme):
            return pv[:, PV[nme]] if nme in PV else pv[:, RAW[nme]]

        act(lambda e: e.activation(pva("steps"), pva("steps_raw"), AF.Sigmoid), reads=["pv"], writes=["pv"])
        dve(lambda e: e.tensor_relu(pva("tmpa"), pva("G_raw")), reads=["pv"], writes=["pv"])
        dve(lambda e: e.tensor_tensor(pva("S"), pva("tmpa"), pva("steps"), ALU.mult), reads=["pv"], writes=["pv"])
        dve(lambda e: e.tensor_scalar(pva("S"), pva("S"), 1.0, None, ALU.add, ALU.bypass), reads=["pv"], writes=["pv"])
        dve(lambda e: e.reciprocal(pva("r2"), pva("S")), reads=["pv"], writes=["pv"])
        act(lambda e: e.activation(pva("r"), pva("r2"), AF.Sqrt), reads=["pv"], writes=["pv"])
        dve(lambda e: e.reciprocal(pva("rinv"), pva("r")), reads=["pv"], writes=["pv"])
        dve(lambda e: e.tensor_relu(pva("tmpb"), pva("A_raw")), reads=["pv"], writes=["pv"])
        dve(lambda e: e.tensor_tensor(pva("tmpb"), pva("tmpb"), pva("steps"), ALU.mult), reads=["pv"], writes=["pv"])
        dve(lambda e: e.tensor_tensor(pva("tmpb"), pva("tmpb"), pva("steps"), ALU.mult), reads=["pv"], writes=["pv"])
        dve(lambda e: e.tensor_tensor(pva("tmpb"), pva("tmpb"), pva("r2"), ALU.mult), reads=["pv"], writes=["pv"])
        act(lambda e: e.activation(pva("alpha"), pva("tmpb"), AF.Tanh, scale=1.0 / 1.99), reads=["pv"], writes=["pv"])
        dve(lambda e: e.tensor_scalar(pva("alpha"), pva("alpha"), 1.99, None, ALU.mult, ALU.bypass), reads=["pv"], writes=["pv"])
        dve(lambda e: e.tensor_tensor(pva("tau"), pva("r2"), pva("alpha"), ALU.subtract), reads=["pv"], writes=["pv"])
        dve(lambda e: e.tensor_scalar(pva("tau"), pva("tau"), 1.0, None, ALU.add, ALU.bypass), reads=["pv"], writes=["pv"])
        dve(lambda e: e.tensor_tensor(pva("cth"), pva("tau"), pva("rinv"), ALU.mult), reads=["pv"], writes=["pv"])
        dve(lambda e: e.tensor_scalar(pva("cth"), pva("cth"), 0.5, 1.0, ALU.mult, ALU.min), reads=["pv"], writes=["pv"])
        dve(lambda e: e.tensor_scalar(pva("cth"), pva("cth"), -1.0, None, ALU.max, ALU.bypass), reads=["pv"], writes=["pv"])
        dve(lambda e: e.tensor_tensor(pva("s2th"), pva("cth"), pva("cth"), ALU.mult), reads=["pv"], writes=["pv"])
        dve(lambda e: e.tensor_scalar(pva("s2th"), pva("s2th"), -1.0, 1.0, ALU.mult, ALU.add), reads=["pv"], writes=["pv"])
        dve(lambda e: e.tensor_tensor(pva("tmpa"), pva("r"), pva("cth"), ALU.mult), reads=["pv"], writes=["pv"])
        dve(lambda e: e.tensor_tensor(pva("w2cr"), pva("tau"), pva("tmpa"), ALU.subtract), reads=["pv"], writes=["pv"])
        dve(lambda e: e.tensor_tensor(pva("w2cr"), pva("w2cr"), pva("steps"), ALU.mult), reads=["pv"], writes=["pv"])
        dve(lambda e: e.tensor_tensor(pva("w2cr"), pva("w2cr"), pva("rinv"), ALU.mult), reads=["pv"], writes=["pv"])
        dve(lambda e: e.tensor_tensor(pva("w1cr"), pva("r2"), pva("tmpa"), ALU.subtract), reads=["pv"], writes=["pv"])
        dve(lambda e: e.tensor_tensor(pva("w1cr"), pva("w1cr"), pva("alpha"), ALU.subtract), reads=["pv"], writes=["pv"])
        dve(lambda e: e.tensor_tensor(pva("w1cr"), pva("w1cr"), pva("rinv"), ALU.mult), reads=["pv"], writes=["pv"])
        dve(lambda e: e.tensor_tensor(pva("fscale"), pva("steps"), pva("r2"), ALU.mult), reads=["pv"], writes=["pv"])
        dve(lambda e: e.tensor_tensor(pva("tmpb"), pva("steps"), pva("steps"), ALU.mult), reads=["pv"], writes=["pv"])
        dve(lambda e: e.tensor_tensor(pva("tmpb"), pva("tmpb"), pva("s2th"), ALU.mult), reads=["pv"], writes=["pv"])
        dve(lambda e: e.tensor_tensor(pva("det"), pva("w2cr"), pva("w2cr"), ALU.mult), reads=["pv"], writes=["pv"])
        dve(lambda e: e.tensor_tensor(pva("det"), pva("det"), pva("tmpb"), ALU.add), reads=["pv"], writes=["pv"])
        dve(lambda e: e.reciprocal(pva("det"), pva("det")), reads=["pv"], writes=["pv"])

        def dup4(dstname, dst, src):
            dve(lambda e: e.tensor_copy(dst[:, 0:2], src), reads=["pv"], writes=[dstname])
            dve(lambda e: e.tensor_copy(dst[:, 2:4], src), reads=["pv"], writes=[dstname])

        dup4("cc.q2a", cc["q2a"], pva("fscale"))
        dve(lambda e: e.tensor_tensor(pva("tmpb"), pva("alpha"), pva("tmpa"), ALU.add), reads=["pv"], writes=["pv"])
        dve(lambda e: e.tensor_scalar(pva("tmpb"), pva("tmpb"), -1.0, 1.0, ALU.mult, ALU.add), reads=["pv"], writes=["pv"])
        dup4("cc.q2b", cc["q2b"], pva("tmpb"))
        dve(lambda e: e.tensor_tensor(pva("tmpb"), pva("r2"), pva("tmpa"), ALU.subtract), reads=["pv"], writes=["pv"])
        dup4("cc.q3a", cc["q3a"], pva("tmpb"))
        dve(lambda e: e.reciprocal(pva("rstep"), pva("steps")), reads=["pv"], writes=["pv"])
        dve(lambda e: e.tensor_tensor(pva("tmpb"), pva("alpha"), pva("rstep"), ALU.mult), reads=["pv"], writes=["pv"])
        dve(lambda e: e.tensor_scalar(pva("tmpb"), pva("tmpb"), -1.0, None, ALU.mult, ALU.bypass), reads=["pv"], writes=["pv"])
        dup4("cc.q3b", cc["q3b"], pva("tmpb"))
        dup4("cc.R1b", cc["R1b"], pva("tmpa"))
        dve(lambda e: e.tensor_tensor(pva("tmpb"), pva("r"), pva("s2th"), ALU.mult), reads=["pv"], writes=["pv"])
        dve(lambda e: e.tensor_scalar(pva("tmpb"), pva("tmpb"), -1.0, None, ALU.mult, ALU.bypass), reads=["pv"], writes=["pv"])
        dup4("cc.R2a", cc["R2a"], pva("tmpb"))
        dup4("cc.R2b", cc["R2b"], pva("cth"))
        dve(lambda e: e.tensor_tensor(pva("tmpb"), pva("steps"), pva("s2th"), ALU.mult), reads=["pv"], writes=["pv"])
        dve(lambda e: e.tensor_tensor(pva("tmpb"), pva("tmpb"), pva("det"), ALU.mult), reads=["pv"], writes=["pv"])
        dup4("cc.i11", cc["i11"], pva("tmpb"))
        dve(lambda e: e.tensor_tensor(pva("tmpb"), pva("w2cr"), pva("det"), ALU.mult), reads=["pv"], writes=["pv"])
        dup4("cc.i12", cc["i12"], pva("tmpb"))
        dve(lambda e: e.tensor_scalar(pva("tmpb"), pva("tmpb"), -1.0, None, ALU.mult, ALU.bypass), reads=["pv"], writes=["pv"])
        dup4("cc.i21", cc["i21"], pva("tmpb"))
        dve(lambda e: e.tensor_tensor(pva("tmpb"), pva("steps"), pva("det"), ALU.mult), reads=["pv"], writes=["pv"])
        dup4("cc.i22", cc["i22"], pva("tmpb"))

        wbuf = [(outb[0][:, :], "outb0"), (outb[1][:, :], "outb1"), (tt[0][:, :], "tt0"), (tt[1][:, :], "tt1")]
        for chunk in range(8):
            for blk in range(4):
                buf, bres = wbuf[blk]
                dma("sp", lambda e, chunk=chunk, blk=blk, buf=buf: e.dma_start(
                    out=buf, in_=d_w1[chunk * 512 + blk * 128: chunk * 512 + (blk + 1) * 128, :]),
                    writes=[bres], sem=f"dwl{blk}")
            for blk in range(4):
                buf, bres = wbuf[blk]
                hb = chunk * 4 + blk
                act(lambda e, buf=buf, hb=hb, blk=blk:
                    e.activation(junk[blk % 2][:, :], buf, AF.Abs, accum_out=a1col[:, hb:hb + 1]),
                    reads=[bres], writes=[f"junk{blk % 2}", "a1col"])
                act(lambda e, buf=buf, blk=blk:
                    e.activation(n1tm[:, blk * 1024:(blk + 1) * 1024], buf, AF.Sign),
                    reads=[bres], writes=["n1tm"])
            dma("gp", lambda e, chunk=chunk: e.dma_start(
                out=s_w1b[chunk * 512:(chunk + 1) * 512, :].rearrange("(a p) d -> p a d", p=128),
                in_=n1tm[:, :].rearrange("p (a d) -> p a d", a=4)),
                reads=["n1tm"], writes=["s_w1b"], sem="dws")
        dve(lambda e: e.tensor_scalar(a1col[:, :], a1col[:, :], 1.0 / H, None, ALU.mult, ALU.bypass),
            reads=["a1col"], writes=["a1col"])
        for chunk in range(8):
            for q4 in range(4):
                buf, bres = wbuf[q4]
                dma("sp", lambda e, chunk=chunk, q4=q4, buf=buf: e.dma_start(
                    out=buf, in_=d_w2[chunk * 128:(chunk + 1) * 128, q4 * 1024:(q4 + 1) * 1024]),
                    writes=[bres], sem=f"dwl{q4}")
            for q4 in range(4):
                buf, bres = wbuf[q4]
                acc = st3[:, 4 + q4:5 + q4]
                act(lambda e, q4=q4, buf=buf, acc=acc:
                    e.activation(junk[q4 % 2][:, :], buf, AF.Abs, accum_out=acc),
                    reads=[bres], writes=[f"junk{q4 % 2}", "st3.q"])
                act(lambda e, q4=q4, buf=buf:
                    e.activation(n1tm[:, q4 * 1024:(q4 + 1) * 1024], buf, AF.Sign),
                    reads=[bres], writes=["n1tm"])
            dve(lambda e: e.tensor_tensor(st3[:, 4:5], st3[:, 4:5], st3[:, 5:6], ALU.add), reads=["st3.q"], writes=["st3.q"])
            dve(lambda e: e.tensor_tensor(st3[:, 6:7], st3[:, 6:7], st3[:, 7:8], ALU.add), reads=["st3.q"], writes=["st3.q"])
            dve(lambda e: e.tensor_tensor(st3[:, 4:5], st3[:, 4:5], st3[:, 6:7], ALU.add), reads=["st3.q"], writes=["st3.q"])
            dve(lambda e: e.tensor_scalar(st3[:, 4:5], st3[:, 4:5], 1.0 / HF, None, ALU.mult, ALU.bypass), reads=["st3.q"], writes=["st3.q"])
            dve(lambda e: e.tensor_scalar(n1tm[:, :], n1tm[:, :], st3[:, 4:5], None, ALU.mult, ALU.bypass),
                reads=["n1tm", "st3.q"], writes=["n1tm"])
            dma("gp", lambda e, chunk=chunk: e.dma_start(
                out=s_w2b[chunk * 128:(chunk + 1) * 128, :], in_=n1tm[:, :]),
                reads=["n1tm"], writes=["s_w2b"], sem="dws")
        trb1 = [("n1fm", n1fm), ("n2fm", n2fm)]
        for kb in range(8):
            nm, buf = trb1[kb % 2]
            dma("sp", lambda e, kb=kb, buf=buf: e.dma_start(
                out=buf[:, :], in_=s_w1b[:, kb * 128:(kb + 1) * 128], transpose=True),
                reads=["s_w1b"], writes=[nm], sem=f"dtr{kb % 2}")
            dma("gp", lambda e, kb=kb, buf=buf: e.dma_start(
                out=s_w1t[:, :, kb * 512:(kb + 1) * 512].rearrange("j p h -> p j h"),
                in_=buf[:, :].rearrange("p (j h) -> p j h", j=8)),
                reads=[nm], writes=["s_w1t"], sem=f"dwb{kb % 2}")
        trb2 = [("n1fm", n1fm[:, 0:1024]), ("n2fm", n2fm[:, 0:1024]), ("n1tm", n1tm[:, 0:1024])]
        for hb in range(32):
            nm, bap = trb2[hb % 3]
            dma("sp", lambda e, hb=hb, bap=bap: e.dma_start(
                out=bap, in_=s_w2b[:, hb * 128:(hb + 1) * 128], transpose=True),
                reads=["s_w2b"], writes=[nm], sem=f"dtr{hb % 3}")
            dma("gp", lambda e, hb=hb, bap=bap: e.dma_start(
                out=s_w2t[hb // 4, :, (hb % 4) * 1024:(hb % 4 + 1) * 1024], in_=bap),
                reads=[nm], writes=["s_w2t"], sem=f"dwb{hb % 3}")

        def emit_table(Cd, Bd, cresC, cresB, sres, sres2, colw, basen, ncols):
            for h in range(2):
                co = h * colw
                base = pva(basen)[:, h:h + 1]
                cthh = pv[:, PV["cth"].start + h: PV["cth"].start + h + 1]
                s2h = pv[:, PV["s2th"].start + h: PV["s2th"].start + h + 1]
                Bss = pv[:, PV["Bss"].start + h: PV["Bss"].start + h + 1]
                dve(lambda e, a=Cd[:, co:co + 1]: e.memset(a, 1.0), writes=[cresC])
                dve(lambda e, a=Bd[:, co:co + 1]: e.memset(a, 0.0), writes=[cresB])
                dve(lambda e, a=Cd[:, co + 1:co + 2], b=base, c=cthh: e.tensor_tensor(a, b, c, ALU.mult),
                    reads=["pv"], writes=[cresC])
                dve(lambda e, a=Bd[:, co + 1:co + 2], b=base: e.tensor_copy(a, b), reads=["pv"], writes=[cresB])
                w_ = 2
                while w_ < ncols:
                    n = min(w_ - 1, ncols - w_)
                    Csc = Cd[:, co + w_ - 1:co + w_]
                    Bsc = Bd[:, co + w_ - 1:co + w_]
                    dve(lambda e, Bss=Bss, Bsc=Bsc, s2h=s2h: e.tensor_tensor(Bss, Bsc, s2h, ALU.mult),
                        reads=[cresB, "pv"], writes=["pv"])
                    dve(lambda e, n=n, c0=co, Csc=Csc: e.tensor_scalar(scw[:, 0:n], Cd[:, c0 + 1:c0 + 1 + n], Csc, None, ALU.mult, ALU.bypass),
                        reads=[cresC], writes=[sres])
                    dve(lambda e, n=n, c0=co, Bsc=Bsc: e.tensor_scalar(scw2[:, 0:n], Cd[:, c0 + 1:c0 + 1 + n], Bsc, None, ALU.mult, ALU.bypass),
                        reads=[cresC, cresB], writes=[sres2])
                    dve(lambda e, n=n, c0=co, w_=w_, Bss=Bss: e.tensor_scalar(Cd[:, c0 + w_:c0 + w_ + n], Bd[:, c0 + 1:c0 + 1 + n], Bss, None, ALU.mult, ALU.bypass),
                        reads=[cresB, "pv"], writes=[cresC])
                    dve(lambda e, n=n, c0=co, w_=w_: e.tensor_tensor(Cd[:, c0 + w_:c0 + w_ + n], scw[:, 0:n], Cd[:, c0 + w_:c0 + w_ + n], ALU.subtract),
                        reads=[cresC, sres], writes=[cresC])
                    dve(lambda e, n=n, c0=co, w_=w_, Csc=Csc: e.tensor_scalar(Bd[:, c0 + w_:c0 + w_ + n], Bd[:, c0 + 1:c0 + 1 + n], Csc, None, ALU.mult, ALU.bypass),
                        reads=[cresB, cresC], writes=[cresB])
                    dve(lambda e, n=n, c0=co, w_=w_: e.tensor_tensor(Bd[:, c0 + w_:c0 + w_ + n], Bd[:, c0 + w_:c0 + w_ + n], scw2[:, 0:n], ALU.add),
                        reads=[cresB, sres2], writes=[cresB])
                    w_ = w_ + n

        emit_table(g_a, g_b, "ga", "gb", "u1.0", "u2.0", T, "rinv", T)
        for h in range(2):
            rc = pv[:, PV["r"].start + h: PV["r"].start + h + 1]
            ric = pv[:, PV["rinv"].start + h: PV["rinv"].start + h + 1]
            co = h * 513
            go = h * T
            dve(lambda e: e.memset(scw2[:, 0:512], 0.0), writes=["u2.0"])
            dve(lambda e, rc=rc: e.tensor_scalar(scw[:, 0:512], scw2[:, 0:512], rc, None, ALU.add, ALU.bypass),
                reads=["pv", "u2.0"], writes=["u1.0"])
            dve(lambda e, ric=ric: e.tensor_tensor_scan(scw[:, 0:512], scw[:, 0:512], scw2[:, 0:512], ric, ALU.mult, ALU.add),
                reads=["u1.0", "u2.0", "pv"], writes=["u1.0"])
            dve(lambda e, co=co, go=go: e.tensor_tensor(tabA[:, co:co + 512], g_a[:, go:go + 512], scw[:, 0:512], ALU.mult),
                reads=["ga", "u1.0"], writes=["mo0"])
            dve(lambda e, co=co: e.tensor_tensor(tabA[:, co:co + 512], tabA[:, co:co + 512], scw[:, 0:512], ALU.mult),
                reads=["mo0", "u1.0"], writes=["mo0"])
            dve(lambda e, co=co, go=go: e.tensor_tensor(tabB[:, co:co + 512], g_b[:, go:go + 512], scw[:, 0:512], ALU.mult),
                reads=["gb", "u1.0"], writes=["mo1"])
            dve(lambda e, co=co: e.tensor_tensor(tabB[:, co:co + 512], tabB[:, co:co + 512], scw[:, 0:512], ALU.mult),
                reads=["mo1", "u1.0"], writes=["mo1"])
            s2h = pv[:, PV["s2th"].start + h: PV["s2th"].start + h + 1]
            dve(lambda e, co=co: e.tensor_tensor(wt1[:, 0:1], tabB[:, co + 511:co + 512], tabB[:, co + 1:co + 2], ALU.mult),
                reads=["mo1"], writes=["wt1"])
            dve(lambda e, s2h=s2h: e.tensor_tensor(wt1[:, 0:1], wt1[:, 0:1], s2h, ALU.mult),
                reads=["wt1", "pv"], writes=["wt1"])
            dve(lambda e, co=co: e.tensor_tensor(tabA[:, co + 512:co + 513], tabA[:, co + 511:co + 512], tabA[:, co + 1:co + 2], ALU.mult),
                reads=["mo0"], writes=["mo0"])
            dve(lambda e, co=co: e.tensor_tensor(tabA[:, co + 512:co + 513], tabA[:, co + 512:co + 513], wt1[:, 0:1], ALU.subtract),
                reads=["mo0", "wt1"], writes=["mo0"])
            dve(lambda e, co=co: e.tensor_tensor(wt1[:, 0:1], tabB[:, co + 511:co + 512], tabA[:, co + 1:co + 2], ALU.mult),
                reads=["mo1", "mo0"], writes=["wt1"])
            dve(lambda e, co=co: e.tensor_tensor(wt2[:, 0:1], tabA[:, co + 511:co + 512], tabB[:, co + 1:co + 2], ALU.mult),
                reads=["mo0", "mo1"], writes=["wt2"])
            dve(lambda e, co=co: e.tensor_tensor(tabB[:, co + 512:co + 513], wt1[:, 0:1], wt2[:, 0:1], ALU.add),
                reads=["wt1", "wt2"], writes=["mo1"])

        for h in range(2):
            co = h * 513
            go = h * T
            stc = pv[:, PV["steps"].start + h: PV["steps"].start + h + 1]
            s2c = pv[:, PV["s2th"].start + h: PV["s2th"].start + h + 1]
            w2c_ = pv[:, PV["w2cr"].start + h: PV["w2cr"].start + h + 1]
            w1c_ = pv[:, PV["w1cr"].start + h: PV["w1cr"].start + h + 1]
            rin = pv[:, PV["rinv"].start + h: PV["rinv"].start + h + 1]
            dve(lambda e, co=co, go=go, stc=stc: e.tensor_scalar(g_G1[:, go:go + T], tabA[:, co:co + T], stc, None, ALU.mult, ALU.bypass),
                reads=["mo0", "pv"], writes=["gG1"])
            dve(lambda e, co=co, w2c_=w2c_: e.tensor_scalar(scw[:, 0:T], tabB[:, co:co + T], w2c_, None, ALU.mult, ALU.bypass),
                reads=["mo1", "pv"], writes=["u1.0"])
            dve(lambda e, go=go: e.tensor_tensor(g_G1[:, go:go + T], g_G1[:, go:go + T], scw[:, 0:T], ALU.add),
                reads=["gG1", "u1.0"], writes=["gG1"])
            dve(lambda e, co=co, go=go, stc=stc: e.tensor_scalar(g_G2[:, go:go + T], tabB[:, co:co + T], stc, None, ALU.mult, ALU.bypass),
                reads=["mo1", "pv"], writes=["gG2"])
            dve(lambda e, go=go, s2c=s2c: e.tensor_scalar(g_G2[:, go:go + T], g_G2[:, go:go + T], s2c, None, ALU.mult, ALU.bypass),
                reads=["gG2", "pv"], writes=["gG2"])
            dve(lambda e, co=co, w2c_=w2c_: e.tensor_scalar(scw[:, 0:T], tabA[:, co:co + T], w2c_, None, ALU.mult, ALU.bypass),
                reads=["mo0", "pv"], writes=["u1.0"])
            dve(lambda e, go=go: e.tensor_tensor(g_G2[:, go:go + T], g_G2[:, go:go + T], scw[:, 0:T], ALU.subtract),
                reads=["gG2", "u1.0"], writes=["gG2"])
            for off in (0, 2):
                col = h + off
                atm1 = tabA[:, co + T - 1:co + T]
                btm1 = tabB[:, co + T - 1:co + T]
                atT = tabA[:, co + T:co + T + 1]
                btT = tabB[:, co + T:co + T + 1]
                dve(lambda e, col=col, w1c_=w1c_, btm1=btm1: e.tensor_tensor(cc["g1a"][:, col:col + 1], w1c_, btm1, ALU.mult),
                    reads=["mo1", "pv"], writes=["cc.g1a"])
                dve(lambda e, col=col, atm1=atm1: e.tensor_tensor(cc["g1a"][:, col:col + 1], cc["g1a"][:, col:col + 1], atm1, ALU.add),
                    reads=["cc.g1a", "mo0"], writes=["cc.g1a"])
                dve(lambda e, col=col, s2c=s2c, btm1=btm1: e.tensor_tensor(cc["g1b"][:, col:col + 1], s2c, btm1, ALU.mult),
                    reads=["mo1", "pv"], writes=["cc.g1b"])
                dve(lambda e, w1c_=w1c_, atm1=atm1: e.tensor_tensor(wt1[:, 0:1], w1c_, atm1, ALU.mult),
                    reads=["mo0", "pv"], writes=["wt1"])
                dve(lambda e, col=col: e.tensor_tensor(cc["g1b"][:, col:col + 1], cc["g1b"][:, col:col + 1], wt1[:, 0:1], ALU.subtract),
                    reads=["cc.g1b", "wt1"], writes=["cc.g1b"])
                dve(lambda e, col=col, atT=atT: e.tensor_copy(cc["cAT"][:, col:col + 1], atT),
                    reads=["mo0"], writes=["cc.cAT"])
                dve(lambda e, col=col, btT=btT, rin=rin: e.tensor_tensor(cc["cBT"][:, col:col + 1], btT, rin, ALU.mult),
                    reads=["mo1", "pv"], writes=["cc.cBT"])

        for b4 in range(4):
            hp = b4 % 2
            dve(lambda e, b4=b4, hp=hp: e.tensor_copy(cc["G1e"][:, b4:b4 + 1], g_G1[:, hp * T + T - 1: hp * T + T]),
                reads=["gG1"], writes=["cc.G1e"])
            dve(lambda e, b4=b4, hp=hp: e.tensor_copy(cc["G2e"][:, b4:b4 + 1], g_G2[:, hp * T + T - 1: hp * T + T]),
                reads=["gG2"], writes=["cc.G2e"])

        for pbk in range(2):
            fsc = pv[:, PV["fscale"].start + pbk: PV["fscale"].start + pbk + 1]
            dve(lambda e, pbk=pbk, fsc=fsc: e.tensor_scalar(dtm[:, pbk * 2048:(pbk + 1) * 2048], sx[pbk][:, :], fsc, None, ALU.mult, ALU.bypass),
                reads=[f"sx{pbk}.0", f"sx{pbk}.1", "pv"], writes=["dtm"])
        bview = dtm.ap().rearrange("q (pb h t) -> q pb h t", pb=2, t=2)
        for t_ in range(2):
            for pbk in range(2):
                bk = (t_ * 2 + pbk) % 2
                for hb in range(8):
                    inap = bview[:, pbk, hb * 128:(hb + 1) * 128, t_]
                    pe(lambda e, bk=bk, hb=hb, inap=inap:
                       e.transpose(pt[bk][:, hb * 128:(hb + 1) * 128], inap, id16[:, :]),
                       reads=["dtm", "id16"], writes=[PT[bk]])
                outap = cB.ap().rearrange("q (k phi) -> q k phi", k=8)[:, :, t_ * 256 + pbk * 128: t_ * 256 + pbk * 128 + 128]
                dve(lambda e, bk=bk, outap=outap: e.tensor_copy(outap, pt[bk][:, :]),
                    reads=[PT[bk]], writes=["cB"])

        for rnd in range(2):
            dma("sp", lambda e, rnd=rnd: e.dma_start(
                out=sx[2][:, :].rearrange("p (db qt) -> p db qt", db=4),
                in_=d_C[rnd * 512:(rnd + 1) * 512, :, :].rearrange("(db p) q t -> p db (q t)", p=128)),
                writes=["sx2.0", "sx2.1"], sem="dp")
            dve(lambda e: e.tensor_copy(ys[:, 0:2048], sx[2][:, :]),
                reads=["sx2.0", "sx2.1"], writes=["ys.0", "ys.1", "ys.2", "ys.3"])
            cview = ys[:, 0:2048].rearrange("q (db pq t) -> q db pq t", db=4, t=2)
            for db4 in range(4):
                db = rnd * 4 + db4
                for kphi in range(4):
                    t_ = kphi // 2
                    pbk = kphi % 2
                    bk = (db4 * 4 + kphi) % 2
                    inap = cview[:, db4, pbk * 128:(pbk + 1) * 128, t_]
                    pe(lambda e, bk=bk, inap=inap: e.transpose(pt[bk][:, 0:128], inap, id16[:, :]),
                       reads=["ys.0", "ys.1", "ys.2", "ys.3", "id16"], writes=[PT[bk]])
                    dst = cC[:, kphi * 1024 + db * 128: kphi * 1024 + db * 128 + 128]
                    if t_ == 0:
                        dve(lambda e, bk=bk, dst=dst: e.tensor_copy(dst, pt[bk][:, 0:128]),
                            reads=[PT[bk]], writes=["cC"])
                    else:
                        dve(lambda e, bk=bk, dst=dst: e.tensor_scalar(dst, pt[bk][:, 0:128], -1.0, None, ALU.mult, ALU.bypass),
                            reads=[PT[bk]], writes=["cC"])

        halfseq = [(c, s, hf) for c in range(NCH) for s in range(SEQS) for hf in range(2)]
        slot_of = {key: i % 4 for i, key in enumerate(halfseq)}
        next_load = [0]

        def emit_xloads(upto):
            while next_load[0] < min(upto, len(halfseq)):
                ckey = halfseq[next_load[0]]
                c, s, hf = ckey
                r = slot_of[ckey]
                dma("gp", lambda e, c=c, s=s, hf=hf, r=r: e.dma_start(
                    out=sx[r][:, :].rearrange("p (tl d) -> p tl d", tl=2),
                    in_=d_x[s].rearrange("(a p) d -> p a d", p=128)[:, (c * 2 + hf) * 2:(c * 2 + hf) * 2 + 2, :]),
                    writes=[f"sx{r}.0", f"sx{r}.1"], sem=f"dx{r}")
                next_load[0] += 1

        emit_xloads(4)

        def ln_batch_stats(stn, stt_, xtiles, xres):
            for i in range(4):
                act(lambda e, i=i, xt=xtiles[i]: e.activation(junk[0][:, :], xt, AF.Square, accum_out=stt_[:, 4 + i:5 + i]),
                    reads=[xres[i]], writes=["junk0", stn + ".q"])
                dve(lambda e, i=i, xt=xtiles[i]: e.tensor_scalar(junk[1][:, :], xt, 1.0, None, ALU.mult, ALU.add, accum_out=stt_[:, i:i + 1]),
                    reads=[xres[i]], writes=["junk1", stn + ".s"])
            dve(lambda e: e.tensor_scalar(stt_[:, 8:12], stt_[:, 0:4], -1.0 / H, None, ALU.mult, ALU.bypass),
                reads=[stn + ".s"], writes=[stn + ".nm"])
            dve(lambda e: e.tensor_tensor(wt1[:, 0:4], stt_[:, 8:12], stt_[:, 8:12], ALU.mult),
                reads=[stn + ".nm"], writes=["wt1"])
            dve(lambda e: e.tensor_scalar(wt2[:, 0:4], stt_[:, 4:8], 1.0 / H, None, ALU.mult, ALU.bypass),
                reads=[stn + ".q"], writes=["wt2"])
            dve(lambda e: e.tensor_tensor(wt2[:, 0:4], wt2[:, 0:4], wt1[:, 0:4], ALU.subtract),
                reads=["wt2", "wt1"], writes=["wt2"])
            dve(lambda e: e.tensor_scalar(wt2[:, 0:4], wt2[:, 0:4], EPS, None, ALU.add, ALU.bypass),
                reads=["wt2"], writes=["wt2"])
            act(lambda e: e.activation(wt2[:, 0:4], wt2[:, 0:4], AF.Sqrt),
                reads=["wt2"], writes=["wt2"])
            dve(lambda e: e.reciprocal(stt_[:, 12:16], wt2[:, 0:4]), reads=["wt2"], writes=[stn + ".rs"])
            dve(lambda e: e.tensor_tensor(stt_[:, 8:12], stt_[:, 8:12], stt_[:, 12:16], ALU.mult),
                reads=[stn + ".nm", stn + ".rs"], writes=[stn + ".nm"])

        def ln_tile_stats(srcap, srcres, jslot):
            act(lambda e: e.activation(junk[0][:, :], srcap, AF.Square, accum_out=st3[:, 17:18]),
                reads=[srcres], writes=["junk0", "st3.q2"])
            dve(lambda e: e.tensor_scalar(junk[1][:, :], srcap, 1.0, None, ALU.mult, ALU.add, accum_out=st3[:, 16:17]),
                reads=[srcres], writes=["junk1", "st3.s"])
            dve(lambda e: e.tensor_scalar(st3[:, 18:19], st3[:, 16:17], -1.0 / H, None, ALU.mult, ALU.bypass),
                reads=["st3.s"], writes=["st3.nm"])
            dve(lambda e: e.tensor_tensor(wt1[:, 0:1], st3[:, 18:19], st3[:, 18:19], ALU.mult),
                reads=["st3.nm"], writes=["wt1"])
            dve(lambda e: e.tensor_scalar(wt2[:, 0:1], st3[:, 17:18], 1.0 / H, None, ALU.mult, ALU.bypass),
                reads=["st3.q2"], writes=["wt2"])
            dve(lambda e: e.tensor_tensor(wt2[:, 0:1], wt2[:, 0:1], wt1[:, 0:1], ALU.subtract),
                reads=["wt2", "wt1"], writes=["wt2"])
            dve(lambda e: e.tensor_scalar(wt2[:, 0:1], wt2[:, 0:1], EPS, None, ALU.add, ALU.bypass),
                reads=["wt2"], writes=["wt2"])
            act(lambda e: e.activation(wt2[:, 0:1], wt2[:, 0:1], AF.Sqrt), reads=["wt2"], writes=["wt2"])
            dve(lambda e: e.reciprocal(st3[:, 19:20], wt2[:, 0:1]), reads=["wt2"], writes=["st3.rs"])
            dve(lambda e: e.tensor_tensor(st3[:, 18:19], st3[:, 18:19], st3[:, 19:20], ALU.mult),
                reads=["st3.nm", "st3.rs"], writes=["st3.nm"])

        NU = NCH * SEQS
        units = [(c, s) for c in range(NCH) for s in range(SEQS)]

        def unit_tiles(u):
            c, s = units[u]
            r0 = slot_of[(c, s, 0)]
            r1 = slot_of[(c, s, 1)]
            xres = [f"sx{r0}.0", f"sx{r0}.1", f"sx{r1}.0", f"sx{r1}.1"]
            xtiles = [sx[r0][:, 0:H], sx[r0][:, H:2 * H], sx[r1][:, 0:H], sx[r1][:, H:2 * H]]
            return c, s, xres, xtiles

        def fe1a(u):
            c, s, xres, xtiles = unit_tiles(u)
            for i in range(4):
                act(lambda e, i=i, xt=xtiles[i]: e.activation(junk[0][:, :], xt, AF.Square, accum_out=st1[:, 4 + i:5 + i]),
                    reads=[xres[i]], writes=["junk0", "st1.q"])
                dve(lambda e, i=i, xt=xtiles[i]: e.tensor_scalar(junk[1][:, :], xt, 1.0, None, ALU.mult, ALU.add, accum_out=st1[:, i:i + 1]),
                    reads=[xres[i]], writes=["junk1", "st1.s"])

        def fe1b(u):
            c, s, xres, xtiles = unit_tiles(u)
            dve(lambda e: e.tensor_scalar(st1[:, 8:12], st1[:, 0:4], -1.0 / H, None, ALU.mult, ALU.bypass),
                reads=["st1.s"], writes=["st1.nm"])
            dve(lambda e: e.tensor_tensor(wt1[:, 0:4], st1[:, 8:12], st1[:, 8:12], ALU.mult),
                reads=["st1.nm"], writes=["wt1"])
            dve(lambda e: e.tensor_scalar(wt2[:, 0:4], st1[:, 4:8], 1.0 / H, None, ALU.mult, ALU.bypass),
                reads=["st1.q"], writes=["wt2"])
            dve(lambda e: e.tensor_tensor(wt2[:, 0:4], wt2[:, 0:4], wt1[:, 0:4], ALU.subtract),
                reads=["wt2", "wt1"], writes=["wt2"])
            dve(lambda e: e.tensor_scalar(wt2[:, 0:4], wt2[:, 0:4], EPS, None, ALU.add, ALU.bypass),
                reads=["wt2"], writes=["wt2"])
            act(lambda e: e.activation(wt2[:, 0:4], wt2[:, 0:4], AF.Sqrt), reads=["wt2"], writes=["wt2"])
            dve(lambda e: e.reciprocal(st1[:, 12:16], wt2[:, 0:4]), reads=["wt2"], writes=["st1.rs"])
            dve(lambda e: e.tensor_tensor(st1[:, 8:12], st1[:, 8:12], st1[:, 12:16], ALU.mult),
                reads=["st1.nm", "st1.rs"], writes=["st1.nm"])
            for i in range(4):
                act(lambda e, i=i, xt=xtiles[i]: e.activation(n1tm[:, i * H:(i + 1) * H], xt, AF.Identity,
                                                scale=st1[:, 12 + i:13 + i], bias=st1[:, 8 + i:9 + i]),
                    reads=[xres[i], "st1.rs", "st1.nm"], writes=["n1tm"])

        def t1(u):
            for i in range(4):
                bk = i % 2
                for k in range(8):
                    pe(lambda e, i=i, k=k, bk=bk: e.transpose(
                        pt[bk][:, k * 128:(k + 1) * 128], n1tm[:, i * H + k * 128: i * H + (k + 1) * 128], id16[:, :]),
                        reads=["n1tm", "id16"], writes=[PT[bk]])
                outap = n1fm.ap().rearrange("q (k j) -> q k j", k=8)[:, :, i * 128:(i + 1) * 128]
                dve(lambda e, bk=bk, outap=outap: e.tensor_copy(outap, pt[bk][:, :]),
                    reads=[PT[bk]], writes=["n1fm"])

        def dterms(u):
            for m in range(4):
                dve(lambda e, m=m: e.tensor_tensor(dtm[:, m * H:(m + 1) * H], drep[:, :], n1tm[:, m * H:(m + 1) * H], ALU.mult),
                    reads=["drep", "n1tm"], writes=["dtm"])

        def scan_pre(u):
            c, s, xres, xtiles = unit_tiles(u)
            P1c, P2c = P1s[s], P2s[s]
            rP1, rP2 = f"P1.{s}", f"P2.{s}"
            dve(lambda e, P1c=P1c: e.tensor_tensor(wq2[:, :], cc["q2a"][:, :], P1c[:, :], ALU.mult), reads=["cc.q2a", rP1], writes=["wq2"])
            dve(lambda e, P2c=P2c: e.tensor_tensor(wt1[:, 0:4], cc["q2b"][:, :], P2c[:, :], ALU.mult), reads=["cc.q2b", rP2], writes=["wt1"])
            dve(lambda e: e.tensor_tensor(wq2[:, :], wq2[:, :], wt1[:, 0:4], ALU.add), reads=["wq2", "wt1"], writes=["wq2"])
            dve(lambda e, P1c=P1c: e.tensor_tensor(wq3[:, :], cc["q3a"][:, :], P1c[:, :], ALU.mult), reads=["cc.q3a", rP1], writes=["wq3"])
            dve(lambda e, P2c=P2c: e.tensor_tensor(wt1[:, 0:4], cc["q3b"][:, :], P2c[:, :], ALU.mult), reads=["cc.q3b", rP2], writes=["wt1"])
            dve(lambda e: e.tensor_tensor(wq3[:, :], wq3[:, :], wt1[:, 0:4], ALU.add), reads=["wq3", "wt1"], writes=["wq3"])
            dve(lambda e, P2c=P2c: e.tensor_tensor(wR1[:, :], cc["R1b"][:, :], P2c[:, :], ALU.mult), reads=["cc.R1b", rP2], writes=["wR1"])
            dve(lambda e: e.tensor_tensor(wR1[:, :], wR1[:, :], wq2[:, :], ALU.add), reads=["wR1", "wq2"], writes=["wR1"])
            dve(lambda e, P2c=P2c: e.tensor_tensor(wR2[:, :], cc["R2a"][:, :], P2c[:, :], ALU.mult), reads=["cc.R2a", rP2], writes=["wR2"])
            dve(lambda e: e.tensor_tensor(wt1[:, 0:4], cc["R2b"][:, :], wq2[:, :], ALU.mult), reads=["cc.R2b", "wq2"], writes=["wt1"])
            dve(lambda e: e.tensor_tensor(wR2[:, :], wR2[:, :], wt1[:, 0:4], ALU.add), reads=["wR2", "wt1"], writes=["wR2"])
            dve(lambda e: e.tensor_tensor(dlc[:, :], cc["i11"][:, :], wR1[:, :], ALU.mult), reads=["cc.i11", "wR1"], writes=["dlc"])
            dve(lambda e: e.tensor_tensor(wt1[:, 0:4], cc["i12"][:, :], wR2[:, :], ALU.mult), reads=["cc.i12", "wR2"], writes=["wt1"])
            dve(lambda e: e.tensor_tensor(dlc[:, :], dlc[:, :], wt1[:, 0:4], ALU.add), reads=["dlc", "wt1"], writes=["dlc"])
            dve(lambda e: e.tensor_tensor(dls[:, :], cc["i21"][:, :], wR1[:, :], ALU.mult), reads=["cc.i21", "wR1"], writes=["dls"])
            dve(lambda e: e.tensor_tensor(wt1[:, 0:4], cc["i22"][:, :], wR2[:, :], ALU.mult), reads=["cc.i22", "wR2"], writes=["wt1"])
            dve(lambda e: e.tensor_tensor(dls[:, :], dls[:, :], wt1[:, 0:4], ALU.add), reads=["dls", "wt1"], writes=["dls"])

        def bu_scan(u):
            for b in range(4):
                bk = b % 2
                hp = b % 2
                for k in range(8):
                    pe(lambda e, b=b, k=k, bk=bk: e.matmul(
                        pb[bk][:, :], cB[:, k * 512 + b * 128: k * 512 + (b + 1) * 128],
                        n1fm[:, k * 512:(k + 1) * 512], start=(k == 0), stop=(k == 7)),
                        reads=["cB", "n1fm"], writes=[PB[bk]])
                dve(lambda e, b=b, bk=bk, hp=hp: e.tensor_tensor(u1[b][:, :], pb[bk][:, :], g_a[:, hp * T:(hp + 1) * T], ALU.mult),
                    reads=[PB[bk], "ga"], writes=[f"u1.{b}"])
                dve(lambda e, b=b, bk=bk, hp=hp: e.tensor_tensor(u2[b][:, :], pb[bk][:, :], g_b[:, hp * T:(hp + 1) * T], ALU.mult),
                    reads=[PB[bk], "gb"], writes=[f"u2.{b}"])
            for b in range(4):
                hp = b % 2
                dve(lambda e, b=b: e.tensor_tensor_scan(u1[b][:, :], u1[b][:, :], u1[b][:, :], dlc[:, b:b + 1], ALU.add, ALU.bypass),
                    reads=[f"u1.{b}", "dlc"], writes=[f"u1.{b}"])
                dve(lambda e, b=b: e.tensor_tensor_scan(u2[b][:, :], u2[b][:, :], u2[b][:, :], dls[:, b:b + 1], ALU.add, ALU.bypass),
                    reads=[f"u2.{b}", "dls"], writes=[f"u2.{b}"])
                dve(lambda e, b=b: e.tensor_copy(pce[:, b:b + 1], u1[b][:, T - 1:T]),
                    reads=[f"u1.{b}"], writes=["pce"])
                dve(lambda e, b=b: e.tensor_copy(pse[:, b:b + 1], u2[b][:, T - 1:T]),
                    reads=[f"u2.{b}"], writes=["pse"])
                dve(lambda e, b=b, hp=hp: e.tensor_tensor(u1[b][:, :], u1[b][:, :], g_G1[:, hp * T:(hp + 1) * T], ALU.mult),
                    reads=[f"u1.{b}", "gG1"], writes=[f"u1.{b}"])
                dve(lambda e, b=b, hp=hp: e.tensor_tensor(u2[b][:, :], u2[b][:, :], g_G2[:, hp * T:(hp + 1) * T], ALU.mult),
                    reads=[f"u2.{b}", "gG2"], writes=[f"u2.{b}"])
                dve(lambda e, b=b: e.tensor_tensor(ys[:, b * T:(b + 1) * T], u1[b][:, :], u2[b][:, :], ALU.add),
                    reads=[f"u1.{b}", f"u2.{b}"], writes=[f"ys.{b}"])

        def carry_update(u):
            c, s, xres, xtiles = unit_tiles(u)
            P1c, P2c = P1s[s], P2s[s]
            rP1, rP2 = f"P1.{s}", f"P2.{s}"
            dve(lambda e, P2c=P2c: e.tensor_tensor(P2c[:, :], cc["G1e"][:, :], pce[:, :], ALU.mult), reads=["cc.G1e", "pce"], writes=[rP2])
            dve(lambda e: e.tensor_tensor(wt2[:, 0:4], cc["G2e"][:, :], pse[:, :], ALU.mult), reads=["cc.G2e", "pse"], writes=["wt2"])
            dve(lambda e, P2c=P2c: e.tensor_tensor(P2c[:, :], P2c[:, :], wt2[:, 0:4], ALU.add), reads=[rP2, "wt2"], writes=[rP2])
            dve(lambda e: e.tensor_tensor(pce[:, :], pce[:, :], dlc[:, :], ALU.subtract), reads=["pce", "dlc"], writes=["pce"])
            dve(lambda e: e.tensor_tensor(pse[:, :], pse[:, :], dls[:, :], ALU.subtract), reads=["pse", "dls"], writes=["pse"])
            dve(lambda e, P1c=P1c: e.tensor_tensor(wt1[:, 0:4], cc["cAT"][:, :], P1c[:, :], ALU.mult), reads=["cc.cAT", rP1], writes=["wt1"])
            dve(lambda e: e.tensor_tensor(wt2[:, 0:4], cc["cBT"][:, :], wq3[:, :], ALU.mult), reads=["cc.cBT", "wq3"], writes=["wt2"])
            dve(lambda e: e.tensor_tensor(wt1[:, 0:4], wt1[:, 0:4], wt2[:, 0:4], ALU.add), reads=["wt1", "wt2"], writes=["wt1"])
            dve(lambda e, P1c=P1c: e.tensor_tensor(P1c[:, :], cc["g1a"][:, :], pce[:, :], ALU.mult), reads=["cc.g1a", "pce"], writes=[rP1])
            dve(lambda e: e.tensor_tensor(wt2[:, 0:4], cc["g1b"][:, :], pse[:, :], ALU.mult), reads=["cc.g1b", "pse"], writes=["wt2"])
            dve(lambda e, P1c=P1c: e.tensor_tensor(P1c[:, :], P1c[:, :], wt2[:, 0:4], ALU.add), reads=[rP1, "wt2"], writes=[rP1])
            dve(lambda e, P1c=P1c: e.tensor_tensor(P1c[:, :], P1c[:, :], wt1[:, 0:4], ALU.add), reads=[rP1, "wt1"], writes=[rP1])

        def cmm_one(u, m):
            mb = m % 2
            for kb in range(4):
                for ng in range(2):
                    pe(lambda e, m=m, kb=kb, ng=ng, mb=mb: e.matmul(
                        pcm[mb][:, ng * 512:(ng + 1) * 512],
                        ys[:, kb * T + m * 128: kb * T + (m + 1) * 128],
                        cC[:, kb * 1024 + ng * 512: kb * 1024 + (ng + 1) * 512],
                        start=(kb == 0), stop=(kb == 3)),
                        reads=[f"ys.{kb}", "cC"], writes=[PCM[mb][ng]])

        def mixes(u):
            c, s, xres, xtiles = unit_tiles(u)
            mixt = [(mo[0][:, 0:H], "mo0"), (mo[1][:, 0:H], "mo1"),
                    (outb[0][:, :], "outb0"), (outb[1][:, :], "outb1")]
            return mixt

        def mix_one(u, m, mixt):
            mb = m % 2
            mt, mres = mixt[m]
            dve(lambda e, m=m, mb=mb, mt=mt: e.tensor_tensor(mt, pcm[mb][:, :], dtm[:, m * H:(m + 1) * H], ALU.add),
                reads=[PCM[mb][0], PCM[mb][1], "dtm"], writes=[mres])

        def ln3_batch(u, mixt):
            c, s, xres, xtiles = unit_tiles(u)
            for m in range(4):
                mt, mres = mixt[m]
                act(lambda e, m=m, mt=mt: e.activation(junk[0][:, :], mt, AF.Square, accum_out=st3[:, 4 + m:5 + m]),
                    reads=[mres], writes=["junk0", "st3b.q"])
                dve(lambda e, m=m, mt=mt: e.tensor_scalar(junk[1][:, :], mt, 1.0, None, ALU.mult, ALU.add, accum_out=st3[:, m:m + 1]),
                    reads=[mres], writes=["junk1", "st3b.s"])
            dve(lambda e: e.tensor_scalar(st3[:, 8:12], st3[:, 0:4], -1.0 / H, None, ALU.mult, ALU.bypass),
                reads=["st3b.s"], writes=["st3b.nm"])
            dve(lambda e: e.tensor_tensor(wt1[:, 0:4], st3[:, 8:12], st3[:, 8:12], ALU.mult),
                reads=["st3b.nm"], writes=["wt1"])
            dve(lambda e: e.tensor_scalar(wt2[:, 0:4], st3[:, 4:8], 1.0 / H, None, ALU.mult, ALU.bypass),
                reads=["st3b.q"], writes=["wt2"])
            dve(lambda e: e.tensor_tensor(wt2[:, 0:4], wt2[:, 0:4], wt1[:, 0:4], ALU.subtract),
                reads=["wt2", "wt1"], writes=["wt2"])
            dve(lambda e: e.tensor_scalar(wt2[:, 0:4], wt2[:, 0:4], EPS, None, ALU.add, ALU.bypass),
                reads=["wt2"], writes=["wt2"])
            act(lambda e: e.activation(wt2[:, 0:4], wt2[:, 0:4], AF.Sqrt), reads=["wt2"], writes=["wt2"])
            dve(lambda e: e.reciprocal(st3[:, 12:16], wt2[:, 0:4]), reads=["wt2"], writes=["st3b.rs"])
            dve(lambda e: e.tensor_tensor(st3[:, 8:12], st3[:, 8:12], st3[:, 12:16], ALU.mult),
                reads=["st3b.nm", "st3b.rs"], writes=["st3b.nm"])
            for m in range(4):
                mt, mres = mixt[m]
                tb = m % 2
                act(lambda e, m=m, mt=mt, tb=tb: e.activation(tt[tb][:, :], mt, AF.Identity,
                                                              scale=st3[:, 12 + m:13 + m], bias=st3[:, 8 + m:9 + m]),
                    reads=[mres, "st3b.rs", "st3b.nm"], writes=[f"tt{tb}"])
                dve(lambda e, m=m, tb=tb, xt=xtiles[m]: e.tensor_tensor(xt, xt, tt[tb][:, :], ALU.add),
                    reads=[xres[m], f"tt{tb}"], writes=[xres[m]])

        def ln2_batch(u):
            c, s, xres, xtiles = unit_tiles(u)
            for m in range(4):
                act(lambda e, m=m, xt=xtiles[m]: e.activation(junk[0][:, :], xt, AF.Square, accum_out=st2[:, 4 + m:5 + m]),
                    reads=[xres[m]], writes=["junk0", "st2.q"])
                dve(lambda e, m=m, xt=xtiles[m]: e.tensor_scalar(junk[1][:, :], xt, 1.0, None, ALU.mult, ALU.add, accum_out=st2[:, m:m + 1]),
                    reads=[xres[m]], writes=["junk1", "st2.s"])
            dve(lambda e: e.tensor_scalar(st2[:, 8:12], st2[:, 0:4], -1.0 / H, None, ALU.mult, ALU.bypass),
                reads=["st2.s"], writes=["st2.nm"])
            dve(lambda e: e.tensor_tensor(wt1[:, 0:4], st2[:, 8:12], st2[:, 8:12], ALU.mult),
                reads=["st2.nm"], writes=["wt1"])
            dve(lambda e: e.tensor_scalar(wt2[:, 0:4], st2[:, 4:8], 1.0 / H, None, ALU.mult, ALU.bypass),
                reads=["st2.q"], writes=["wt2"])
            dve(lambda e: e.tensor_tensor(wt2[:, 0:4], wt2[:, 0:4], wt1[:, 0:4], ALU.subtract),
                reads=["wt2", "wt1"], writes=["wt2"])
            dve(lambda e: e.tensor_scalar(wt2[:, 0:4], wt2[:, 0:4], EPS, None, ALU.add, ALU.bypass),
                reads=["wt2"], writes=["wt2"])
            act(lambda e: e.activation(wt2[:, 0:4], wt2[:, 0:4], AF.Sqrt), reads=["wt2"], writes=["wt2"])
            dve(lambda e: e.reciprocal(st2[:, 12:16], wt2[:, 0:4]), reads=["wt2"], writes=["st2.rs"])
            dve(lambda e: e.tensor_tensor(st2[:, 8:12], st2[:, 8:12], st2[:, 12:16], ALU.mult),
                reads=["st2.nm", "st2.rs"], writes=["st2.nm"])
            for m in range(4):
                act(lambda e, m=m, xt=xtiles[m]: e.activation(n2tm[m][:, :], xt, AF.Identity,
                                                              scale=st2[:, 12 + m:13 + m], bias=st2[:, 8 + m:9 + m]),
                    reads=[xres[m], "st2.rs", "st2.nm"], writes=[f"n2tm{m}"])

        def t3_one(u, m):
            nb = m
            bk = m % 2
            for k in range(8):
                pe(lambda e, k=k, nb=nb, bk=bk: e.transpose(pt[bk][:, k * 128:(k + 1) * 128],
                                                     n2tm[nb][:, k * 128:(k + 1) * 128], id16[:, :]),
                    reads=[f"n2tm{nb}", "id16"], writes=[PT[bk]])
            outap = n2fm.ap().rearrange("q (k j) -> q k j", k=8)[:, :, m * 128:(m + 1) * 128]
            dve(lambda e, bk=bk, outap=outap: e.tensor_copy(outap, pt[bk][:, :]),
                reads=[PT[bk]], writes=["n2fm"])

        def mixer_back(u, next_u):
            mixt = mixes(u)
            cmm_one(u, 0)
            cmm_one(u, 1)
            mix_one(u, 0, mixt)
            cmm_one(u, 2)
            mix_one(u, 1, mixt)
            cmm_one(u, 3)
            mix_one(u, 2, mixt)
            mix_one(u, 3, mixt)
            ln3_batch(u, mixt)
            ln2_batch(u)
            if next_u is not None:
                t1(next_u)
            t3_one(u, 0)
            t3_one(u, 1)
            t3_one(u, 2)
            t3_one(u, 3)

        wload_state = {"next": 0}
        NPIECE_TOTAL = None

        def emit_w1_load(g):
            if NPIECE_TOTAL is not None and g >= NPIECE_TOTAL[0]:
                return
            piece, wr = g % 8, g % 2
            dma("sp", lambda e, piece=piece, wr=wr: e.dma_start(
                out=w1s[wr][:, :], in_=s_w1t[piece, :, :]),
                reads=["s_w1t"], writes=[f"w1s{wr}"], sem=f"dw1{wr}")

        def emit_w2_load(g):
            if NPIECE_TOTAL is not None and g >= NPIECE_TOTAL[0]:
                return
            piece, wr = g % 8, g % 2
            dma("sp", lambda e, piece=piece, wr=wr: e.dma_start(
                out=w2s[wr][:, :], in_=s_w2t[piece, :, :]),
                reads=["s_w2t"], writes=[f"w2s{wr}"], sem=f"dw2{wr}")

        def ffn(u):
            c, s, xres, xtiles = unit_tiles(u)
            for hf in range(2):
                gbase = (u * 2 + hf) * 8

                def ffn1(hb):
                    piece, hbq, bk = hb // 4, hb % 4, hb % 2
                    wr = (gbase + piece) % 2
                    for k in range(8):
                        pe(lambda e, k=k, bk=bk, wr=wr, hbq=hbq, hf=hf: e.matmul(
                            pb[bk][:, 0:256],
                            w1s[wr][:, k * 512 + hbq * 128: k * 512 + hbq * 128 + 128],
                            n2fm[:, k * 512 + hf * 256: k * 512 + hf * 256 + 256],
                            start=(k == 0), stop=(k == 7)),
                            reads=[f"w1s{wr}", "n2fm"], writes=[PB[bk]])
                    gs = hb % 4
                    act(lambda e, hb=hb, bk=bk, gs=gs: e.activation(
                        gel[gs][:, :], pb[bk][:, 0:256], AF.Gelu, bias=b1col[:, hb:hb + 1],
                        scale=a1col[:, hb:hb + 1]),
                        reads=[PB[bk], "b1col", "a1col"], writes=[f"gel{gs}"])

                def ffn2(hb):
                    piece, hbq, gs = hb // 4, hb % 4, hb % 4
                    wr = (gbase + piece) % 2
                    for mg in range(2):
                        for ng in range(2):
                            pe(lambda e, hb=hb, mg=mg, ng=ng, wr=wr, gs=gs, hbq=hbq: e.matmul(
                                pcm[mg][:, ng * 512:(ng + 1) * 512],
                                gel[gs][:, mg * 128:(mg + 1) * 128],
                                w2s[wr][:, hbq * 1024 + ng * 512: hbq * 1024 + (ng + 1) * 512],
                                start=(hb == 0), stop=(hb == 31), skip_group_check=True),
                                reads=[f"gel{gs}", f"w2s{wr}"], writes=[PCM[mg][ng]])

                ffn1(0)
                for hb in range(1, 32):
                    ffn1(hb)
                    ffn2(hb - 1)
                    if hb % 4 == 3:
                        emit_w1_load(gbase + hb // 4 + 2)
                    if (hb - 1) % 4 == 3:
                        emit_w2_load(gbase + (hb - 1) // 4 + 2)
                ffn2(31)
                emit_w2_load(gbase + 9)
                for mg in range(2):
                    m = hf * 2 + mg
                    dve(lambda e, mg=mg: e.tensor_tensor(mo[mg][:, 0:H], pcm[mg][:, :], b2rep[:, :], ALU.add),
                        reads=[PCM[mg][0], PCM[mg][1], "b2rep"], writes=[f"mo{mg}"])
                    ln_tile_stats(mo[mg][:, 0:H], f"mo{mg}", mg)
                    act(lambda e, mg=mg: e.activation(tt[mg][:, :], mo[mg][:, 0:H], AF.Identity, scale=st3[:, 19:20], bias=st3[:, 18:19]),
                        reads=[f"mo{mg}", "st3.rs", "st3.nm"], writes=[f"tt{mg}"])
                    dve(lambda e, m=m, mg=mg, xt=xtiles[m]: e.tensor_tensor(outb[mg][:, :], xt, tt[mg][:, :], ALU.add),
                        reads=[xres[m], f"tt{mg}"], writes=[f"outb{mg}"])
                    tok_tile = c * 4 + hf * 2 + mg
                    dma("gp", lambda e, s=s, tok_tile=tok_tile, mg=mg: e.dma_start(
                        out=d_out[s].rearrange("(a p) d -> p a d", p=128)[:, tok_tile, :],
                        in_=outb[mg][:, :]),
                        reads=[f"outb{mg}"], writes=[], sem=f"do{mg}")

        NPIECE_TOTAL = [NU * 16]
        emit_w1_load(0)
        emit_w1_load(1)
        emit_w2_load(0)
        emit_w2_load(1)
        fe1a(0)
        fe1b(0)
        t1(0)
        scan_pre(0)
        bu_scan(0)
        carry_update(0)
        emit_xloads(4)
        if NU > 1:
            fe1a(1)
        for u in range(NU):
            dterms(u)
            if u + 1 < NU:
                fe1b(u + 1)
            mixer_back(u, u + 1 if u + 1 < NU else None)
            if u + 1 < NU:
                scan_pre(u + 1)
                bu_scan(u + 1)
                carry_update(u + 1)
            ffn(u)
            emit_xloads(2 * u + 6)
            if u + 2 < NU:
                fe1a(u + 2)

        last_do = [B.dma_sem_last.get("do0"), B.dma_sem_last.get("do1")]
        fin = B.emit("sp", lambda e: e.nop(), reads=(), writes=())
        for d in last_do:
            if d is not None:
                fin.extra.append(d)

        B.finalize_counts()
        sems = {}
        for s_ in sorted(B.sem_names):
            sems[s_] = es.enter_context(nc.semaphore(s_))
        block = es.enter_context(nc.Block())
        block.tensor(lambda e: B.run("pe", e, sems))
        block.vector(lambda e: B.run("dve", e, sems))
        block.scalar(lambda e: B.run("act", e, sems))
        block.sync(lambda e: B.run("sp", e, sems))
        block.gpsimd(lambda e: B.run("gp", e, sems))
        nc._B = B
    return nc


def kernel(L0_attn_input_sequence, L0_attn_steps_raw, L0_attn_G_raw, L0_attn_A_raw,
           L0_attn_B_mat, L0_attn_C_mat, L0_attn_D_vec,
           L0_ffn_w1, L0_ffn_b1, L0_ffn_w2, L0_ffn_b2):
    x = np.ascontiguousarray(np.asarray(L0_attn_input_sequence, dtype=np.float32))
    n_cores = 8
    params = {
        "steps_raw": np.ascontiguousarray(np.asarray(L0_attn_steps_raw, dtype=np.float32)),
        "G_raw": np.ascontiguousarray(np.asarray(L0_attn_G_raw, dtype=np.float32)),
        "A_raw": np.ascontiguousarray(np.asarray(L0_attn_A_raw, dtype=np.float32)),
        "B_mat": np.ascontiguousarray(np.asarray(L0_attn_B_mat, dtype=np.float32)),
        "C_mat": np.ascontiguousarray(np.asarray(L0_attn_C_mat, dtype=np.float32)),
        "D_vec": np.ascontiguousarray(np.asarray(L0_attn_D_vec, dtype=np.float32)),
        "w1": np.ascontiguousarray(np.asarray(L0_ffn_w1, dtype=np.float32)),
        "b1": np.ascontiguousarray(np.asarray(L0_ffn_b1, dtype=np.float32)),
        "w2": np.ascontiguousarray(np.asarray(L0_ffn_w2, dtype=np.float32)),
        "b2": np.ascontiguousarray(np.asarray(L0_ffn_b2, dtype=np.float32)),
        "ident": np.eye(128, dtype=np.float32),
    }
    nc = bass.Bass("TRN2", target_bir_lowering=False, debug=False)
    build_program(nc)
    in_maps = [dict(x=x[2 * i:2 * i + 2], **params) for i in range(n_cores)]
    res = run_bass_kernel_spmd(nc, in_maps, core_ids=list(range(n_cores)))
    return np.concatenate([np.asarray(r["out"], dtype=np.float32) for r in res.results], axis=0)


# version 25
# speedup vs baseline: 1.0737x; 1.0130x over previous
import numpy as np
from contextlib import ExitStack

import concourse.bass as bass
import concourse.mybir as mybir
from concourse.bass_utils import run_bass_kernel_spmd

F32 = mybir.dt.float32
BF16 = mybir.dt.bfloat16
AF = mybir.ActivationFunctionType
ALU = mybir.AluOpType
AX = mybir.AxisListType

H = 1024
P = 256
HF = 4096
T = 512
EPS = 1e-5

QUEUES = ("pe", "dve", "act", "sp", "gp")


class Op:
    __slots__ = ("fn", "reads", "writes", "sem", "need_inc", "inc_by", "count", "extra")

    def __init__(self, fn, reads, writes, sem, inc_by, extra):
        self.fn = fn
        self.reads = reads
        self.writes = writes
        self.sem = sem
        self.need_inc = inc_by > 1
        self.inc_by = inc_by
        self.count = None
        self.extra = extra


class Build:
    def __init__(self):
        self.ops = {q: [] for q in QUEUES}
        self.last_writer = {}
        self.readers = {}
        self.dma_sem_last = {}
        self.sem_names = set(QUEUES)

    def emit(self, q, fn, reads=(), writes=(), sem=None, inc_by=1, extra=()):
        if sem is None:
            sem = q
        self.sem_names.add(sem)
        op = Op(fn, tuple(reads), tuple(writes), sem, inc_by, list(extra))
        deps = []
        for r in op.reads:
            w = self.last_writer.get(r)
            if w is not None:
                deps.append(w)
        for r in op.writes:
            w = self.last_writer.get(r)
            if w is not None:
                deps.append(w)
            for rd in self.readers.get(r, {}).values():
                deps.append(rd)
        for d in deps:
            if d is op:
                continue
            if d.sem == "pe" and q == "pe" and sem == "pe":
                continue
            op.extra.append(d)
        if inc_by > 1:
            prev = self.dma_sem_last.get(sem)
            if prev is not None:
                op.extra.append(prev)
            self.dma_sem_last[sem] = op
        self.ops[q].append(op)
        for r in op.writes:
            self.last_writer[r] = op
            self.readers[r] = {}
        for r in op.reads:
            self.readers.setdefault(r, {})[op.sem] = op
        return op

    def finalize_counts(self):
        for q in QUEUES:
            for op in self.ops[q]:
                for d in op.extra:
                    if isinstance(d, Op):
                        d.need_inc = True
        cnt = {s: 0 for s in self.sem_names}
        for q in QUEUES:
            for op in self.ops[q]:
                if op.need_inc:
                    cnt[op.sem] += op.inc_by
                    op.count = cnt[op.sem]

    def run(self, q, eng, sems):
        waited = {}
        for op in self.ops[q]:
            for d in op.extra:
                if isinstance(d, Op):
                    if d.count is None:
                        continue
                    key, val = d.sem, d.count
                else:
                    key, val = d
                if waited.get(key, -1) >= val:
                    continue
                eng.wait_ge(sems[key], val)
                waited[key] = val
            inst = op.fn(eng)
            if op.need_inc and op.count is not None:
                inst.then_inc(sems[op.sem], op.inc_by)


def build_program(nc: bass.Bass, L=4096, SEQS=2):
    NCH = L // T
    d_x = nc.dram_tensor("x", [SEQS, L, H], F32, kind="ExternalInput").ap()
    d_steps = nc.dram_tensor("steps_raw", [P], F32, kind="ExternalInput").ap()
    d_G = nc.dram_tensor("G_raw", [P], F32, kind="ExternalInput").ap()
    d_A = nc.dram_tensor("A_raw", [P], F32, kind="ExternalInput").ap()
    d_B = nc.dram_tensor("B_mat", [P, H, 2], F32, kind="ExternalInput").ap()
    d_C = nc.dram_tensor("C_mat", [H, P, 2], F32, kind="ExternalInput").ap()
    d_D = nc.dram_tensor("D_vec", [H], F32, kind="ExternalInput").ap()
    d_w1 = nc.dram_tensor("w1", [HF, H], F32, kind="ExternalInput").ap()
    d_b1 = nc.dram_tensor("b1", [HF], F32, kind="ExternalInput").ap()
    d_w2 = nc.dram_tensor("w2", [H, HF], F32, kind="ExternalInput").ap()
    d_b2 = nc.dram_tensor("b2", [H], F32, kind="ExternalInput").ap()
    d_id = nc.dram_tensor("ident", [128, 128], F32, kind="ExternalInput").ap()
    d_out = nc.dram_tensor("out", [SEQS, L, H], F32, kind="ExternalOutput").ap()
    s_w1b = nc.dram_tensor("scr_w1b", [HF, H], BF16, kind="Internal").ap()
    s_w2b = nc.dram_tensor("scr_w2b", [H, HF], BF16, kind="Internal").ap()
    s_w1t = nc.dram_tensor("scr_w1t", [8, 128, HF], BF16, kind="Internal").ap()
    s_w2t = nc.dram_tensor("scr_w2t", [8, 128, 4 * H], BF16, kind="Internal").ap()

    es = ExitStack()
    with es:
        def sb(name, shape, dt):
            return es.enter_context(nc.sbuf_tensor(name, shape, dt))

        cB = sb("cB", [128, 8 * 512], BF16)
        cC = sb("cC", [128, 4 * H], BF16)
        g_a = sb("g_a", [128, 2 * T], F32)
        g_b = sb("g_b", [128, 2 * T], F32)
        g_G1 = sb("g_G1", [128, 2 * T], F32)
        g_G2 = sb("g_G2", [128, 2 * T], F32)
        drep = sb("drep", [128, H], BF16)
        b2rep = sb("b2rep", [128, H], F32)
        b1col = sb("b1col", [128, 32], F32)
        idf = sb("idf", [128, 128], F32)
        id16 = sb("id16", [128, 128], BF16)
        cc = {n: sb("cc_" + n, [128, 4], F32)
              for n in ("q2a", "q2b", "q3a", "q3b", "R1b", "R2a", "R2b",
                        "i11", "i12", "i21", "i22", "g1a", "g1b", "cAT", "cBT",
                        "G1e", "G2e")}
        P1s = [sb(f"P1_{s}", [128, 4], F32) for s in range(SEQS)]
        P2s = [sb(f"P2_{s}", [128, 4], F32) for s in range(SEQS)]
        dlc = sb("dlc", [128, 4], F32)
        dls = sb("dls", [128, 4], F32)
        wq2 = sb("wq2", [128, 4], F32)
        wq3 = sb("wq3", [128, 4], F32)
        wR1 = sb("wR1", [128, 4], F32)
        wR2 = sb("wR2", [128, 4], F32)
        wt1 = sb("wt1", [128, 4], F32)
        wt2 = sb("wt2", [128, 4], F32)
        pce = sb("pce", [128, 4], F32)
        pse = sb("pse", [128, 4], F32)
        pv = sb("pv", [128, 64], F32)
        st1 = sb("st1", [128, 16], F32)
        st2 = sb("st2", [128, 16], F32)
        st3 = sb("st3", [128, 24], F32)
        prow = sb("prow", [8, 128], F32)
        b1row = sb("b1row", [32, 128], F32)
        dr0 = sb("dr0", [1, H], F32)
        dr016 = sb("dr016", [1, H], BF16)
        b2r0 = sb("b2r0", [1, H], F32)
        ones16 = sb("ones16", [1, 128], BF16)
        onesf = sb("onesf", [1, 128], F32)
        sx = [sb(f"sx{r}", [128, 2 * H], F32) for r in range(4)]
        w1s = [sb(f"w1s{r}", [128, 8 * 512], BF16) for r in range(2)]
        w2s = [sb(f"w2s{r}", [128, 4 * H], BF16) for r in range(2)]
        outb = [sb(f"outb{r}", [128, H], F32) for r in range(2)]
        n1tm = sb("n1tm", [128, 4 * H], BF16)
        n1fm = sb("n1fm", [128, 8 * 512], BF16)
        n2fm = sb("n2fm", [128, 8 * 512], BF16)
        n2tm = [sb(f"n2tm{r}", [128, H], BF16) for r in range(4)]
        junk = [sb(f"junk{r}", [128, H], BF16) for r in range(2)]
        u1 = [sb(f"u1_{r}", [128, T], F32) for r in range(4)]
        u2 = [sb(f"u2_{r}", [128, T], F32) for r in range(4)]
        dtm = sb("dtm", [128, 4 * H], BF16)
        a1col = sb("a1col", [128, 32], F32)
        jdve = junk_placeholder = None
        ys = sb("ys", [128, 4 * T], BF16)
        mo = [sb(f"mo{r}", [128, 1056], F32) for r in range(2)]
        tt = [sb(f"tt{r}", [128, H], F32) for r in range(2)]
        gel = [sb(f"gel{r}", [128, 256], BF16) for r in range(4)]
        tabA = mo[0]
        tabB = mo[1]
        scw = u1[0]
        scw2 = u2[0]

        pb = [es.enter_context(nc.psum_tensor("pb0", [128, 512], F32)),
              es.enter_context(nc.psum_tensor("pb1", [128, 512], F32))]
        pcm = [es.enter_context(nc.psum_tensor("pcA", [128, 1024], F32)),
               es.enter_context(nc.psum_tensor("pcB", [128, 1024], F32))]
        pt = [es.enter_context(nc.psum_tensor("pt6", [128, 1024], BF16)),
              es.enter_context(nc.psum_tensor("pt7", [128, 1024], BF16))]

        B = Build()

        def pe(fn, reads=(), writes=()):
            return B.emit("pe", fn, reads, writes)

        def dve(fn, reads=(), writes=()):
            return B.emit("dve", fn, reads, writes)

        def act(fn, reads=(), writes=()):
            return B.emit("act", fn, reads, writes)

        def dma(q, fn, reads=(), writes=(), sem=None):
            return B.emit(q, fn, reads, writes, sem=sem, inc_by=16)

        PB = ["pb0", "pb1"]
        PCM = [["pcA0", "pcA1"], ["pcB0", "pcB1"]]
        PT = ["pt6", "pt7"]

        dma("sp", lambda e: e.dma_start(out=idf[:, :], in_=d_id), writes=["idf"], sem="dp")
        dma("sp", lambda e: e.dma_start(out=prow[0:2, :], in_=d_steps.rearrange("(h p) -> h p", p=128)),
            writes=["prow"], sem="dp")
        dma("sp", lambda e: e.dma_start(out=prow[2:4, :], in_=d_G.rearrange("(h p) -> h p", p=128)),
            writes=["prow"], sem="dp")
        dma("sp", lambda e: e.dma_start(out=prow[4:6, :], in_=d_A.rearrange("(h p) -> h p", p=128)),
            writes=["prow"], sem="dp")
        dma("sp", lambda e: e.dma_start(out=b1row[:, :], in_=d_b1.rearrange("(a p) -> a p", p=128)),
            writes=["b1row"], sem="dp")
        dma("sp", lambda e: e.dma_start(out=dr0[:, :], in_=d_D[None, :]), writes=["dr0"], sem="dp")
        dma("sp", lambda e: e.dma_start(out=b2r0[:, :], in_=d_b2[None, :]), writes=["b2r0"], sem="dp")
        dma("sp", lambda e: e.dma_start(out=sx[0][:, :], in_=d_B[0:128, :, :].rearrange("p h t -> p (h t)")),
            writes=["sx0.0", "sx0.1"], sem="dp")
        dma("sp", lambda e: e.dma_start(out=sx[1][:, :], in_=d_B[128:256, :, :].rearrange("p h t -> p (h t)")),
            writes=["sx1.0", "sx1.1"], sem="dp")

        dve(lambda e: e.tensor_copy(id16[:, :], idf[:, :]), reads=["idf"], writes=["id16"])
        dve(lambda e: e.memset(ones16[:, :], 1.0), writes=["ones16"])
        dve(lambda e: e.memset(onesf[:, :], 1.0), writes=["onesf"])
        for s in range(SEQS):
            dve(lambda e, s=s: e.memset(P1s[s][:, :], 0.0), writes=[f"P1.{s}"])
            dve(lambda e, s=s: e.memset(P2s[s][:, :], 0.0), writes=[f"P2.{s}"])
        dve(lambda e: e.tensor_copy(dr016[:, :], dr0[:, :]), reads=["dr0"], writes=["dr016"])

        pe(lambda e: e.transpose(pcm[0][:, 0:6], prow[0:6, :], idf[0:6, 0:6]),
           reads=["prow", "idf"], writes=[PCM[0][0]])
        dve(lambda e: e.tensor_copy(pv[:, 0:6], pcm[0][:, 0:6]), reads=[PCM[0][0]], writes=["pv"])
        pe(lambda e: e.transpose(pcm[1][:, 0:32], b1row[:, :], idf[0:32, 0:32]),
           reads=["b1row", "idf"], writes=[PCM[1][0]])
        dve(lambda e: e.tensor_copy(b1col[:, 0:32], pcm[1][:, 0:32]), reads=[PCM[1][0]], writes=["b1col"])

        pe(lambda e: e.matmul(pcm[1][:, 0:512], ones16[0:1, :], dr016[0:1, 0:512], start=True, stop=True),
           reads=["ones16", "dr016"], writes=[PCM[1][0]])
        pe(lambda e: e.matmul(pcm[1][:, 512:1024], ones16[0:1, :], dr016[0:1, 512:1024], start=True, stop=True),
           reads=["ones16", "dr016"], writes=[PCM[1][1]])
        dve(lambda e: e.tensor_copy(drep[:, :], pcm[1][:, :]), reads=[PCM[1][0], PCM[1][1]], writes=["drep"])
        pe(lambda e: e.matmul(pcm[0][:, 0:512], onesf[0:1, :], b2r0[0:1, 0:512], start=True, stop=True),
           reads=["onesf", "b2r0"], writes=[PCM[0][0]])
        pe(lambda e: e.matmul(pcm[0][:, 512:1024], onesf[0:1, :], b2r0[0:1, 512:1024], start=True, stop=True),
           reads=["onesf", "b2r0"], writes=[PCM[0][1]])
        dve(lambda e: e.tensor_copy(b2rep[:, :], pcm[0][:, :]), reads=[PCM[0][0], PCM[0][1]], writes=["b2rep"])

        PV = {}
        _n = [6]
        for nme in ("steps", "S", "r2", "r", "rinv", "alpha", "tau", "cth", "s2th",
                    "w2cr", "w1cr", "fscale", "det", "tmpa", "tmpb", "rstep",
                    "Cs", "Bs", "Bss"):
            PV[nme] = slice(_n[0], _n[0] + 2)
            _n[0] += 2
        RAW = {"steps_raw": slice(0, 2), "G_raw": slice(2, 4), "A_raw": slice(4, 6)}

        def pva(nme):
            return pv[:, PV[nme]] if nme in PV else pv[:, RAW[nme]]

        act(lambda e: e.activation(pva("steps"), pva("steps_raw"), AF.Sigmoid), reads=["pv"], writes=["pv"])
        dve(lambda e: e.tensor_relu(pva("tmpa"), pva("G_raw")), reads=["pv"], writes=["pv"])
        dve(lambda e: e.tensor_tensor(pva("S"), pva("tmpa"), pva("steps"), ALU.mult), reads=["pv"], writes=["pv"])
        dve(lambda e: e.tensor_scalar(pva("S"), pva("S"), 1.0, None, ALU.add, ALU.bypass), reads=["pv"], writes=["pv"])
        dve(lambda e: e.reciprocal(pva("r2"), pva("S")), reads=["pv"], writes=["pv"])
        act(lambda e: e.activation(pva("r"), pva("r2"), AF.Sqrt), reads=["pv"], writes=["pv"])
        dve(lambda e: e.reciprocal(pva("rinv"), pva("r")), reads=["pv"], writes=["pv"])
        dve(lambda e: e.tensor_relu(pva("tmpb"), pva("A_raw")), reads=["pv"], writes=["pv"])
        dve(lambda e: e.tensor_tensor(pva("tmpb"), pva("tmpb"), pva("steps"), ALU.mult), reads=["pv"], writes=["pv"])
        dve(lambda e: e.tensor_tensor(pva("tmpb"), pva("tmpb"), pva("steps"), ALU.mult), reads=["pv"], writes=["pv"])
        dve(lambda e: e.tensor_tensor(pva("tmpb"), pva("tmpb"), pva("r2"), ALU.mult), reads=["pv"], writes=["pv"])
        act(lambda e: e.activation(pva("alpha"), pva("tmpb"), AF.Tanh, scale=1.0 / 1.99), reads=["pv"], writes=["pv"])
        dve(lambda e: e.tensor_scalar(pva("alpha"), pva("alpha"), 1.99, None, ALU.mult, ALU.bypass), reads=["pv"], writes=["pv"])
        dve(lambda e: e.tensor_tensor(pva("tau"), pva("r2"), pva("alpha"), ALU.subtract), reads=["pv"], writes=["pv"])
        dve(lambda e: e.tensor_scalar(pva("tau"), pva("tau"), 1.0, None, ALU.add, ALU.bypass), reads=["pv"], writes=["pv"])
        dve(lambda e: e.tensor_tensor(pva("cth"), pva("tau"), pva("rinv"), ALU.mult), reads=["pv"], writes=["pv"])
        dve(lambda e: e.tensor_scalar(pva("cth"), pva("cth"), 0.5, 1.0, ALU.mult, ALU.min), reads=["pv"], writes=["pv"])
        dve(lambda e: e.tensor_scalar(pva("cth"), pva("cth"), -1.0, None, ALU.max, ALU.bypass), reads=["pv"], writes=["pv"])
        dve(lambda e: e.tensor_tensor(pva("s2th"), pva("cth"), pva("cth"), ALU.mult), reads=["pv"], writes=["pv"])
        dve(lambda e: e.tensor_scalar(pva("s2th"), pva("s2th"), -1.0, 1.0, ALU.mult, ALU.add), reads=["pv"], writes=["pv"])
        dve(lambda e: e.tensor_tensor(pva("tmpa"), pva("r"), pva("cth"), ALU.mult), reads=["pv"], writes=["pv"])
        dve(lambda e: e.tensor_tensor(pva("w2cr"), pva("tau"), pva("tmpa"), ALU.subtract), reads=["pv"], writes=["pv"])
        dve(lambda e: e.tensor_tensor(pva("w2cr"), pva("w2cr"), pva("steps"), ALU.mult), reads=["pv"], writes=["pv"])
        dve(lambda e: e.tensor_tensor(pva("w2cr"), pva("w2cr"), pva("rinv"), ALU.mult), reads=["pv"], writes=["pv"])
        dve(lambda e: e.tensor_tensor(pva("w1cr"), pva("r2"), pva("tmpa"), ALU.subtract), reads=["pv"], writes=["pv"])
        dve(lambda e: e.tensor_tensor(pva("w1cr"), pva("w1cr"), pva("alpha"), ALU.subtract), reads=["pv"], writes=["pv"])
        dve(lambda e: e.tensor_tensor(pva("w1cr"), pva("w1cr"), pva("rinv"), ALU.mult), reads=["pv"], writes=["pv"])
        dve(lambda e: e.tensor_tensor(pva("fscale"), pva("steps"), pva("r2"), ALU.mult), reads=["pv"], writes=["pv"])
        dve(lambda e: e.tensor_tensor(pva("tmpb"), pva("steps"), pva("steps"), ALU.mult), reads=["pv"], writes=["pv"])
        dve(lambda e: e.tensor_tensor(pva("tmpb"), pva("tmpb"), pva("s2th"), ALU.mult), reads=["pv"], writes=["pv"])
        dve(lambda e: e.tensor_tensor(pva("det"), pva("w2cr"), pva("w2cr"), ALU.mult), reads=["pv"], writes=["pv"])
        dve(lambda e: e.tensor_tensor(pva("det"), pva("det"), pva("tmpb"), ALU.add), reads=["pv"], writes=["pv"])
        dve(lambda e: e.reciprocal(pva("det"), pva("det")), reads=["pv"], writes=["pv"])

        def dup4(dstname, dst, src):
            dve(lambda e: e.tensor_copy(dst[:, 0:2], src), reads=["pv"], writes=[dstname])
            dve(lambda e: e.tensor_copy(dst[:, 2:4], src), reads=["pv"], writes=[dstname])

        dup4("cc.q2a", cc["q2a"], pva("fscale"))
        dve(lambda e: e.tensor_tensor(pva("tmpb"), pva("alpha"), pva("tmpa"), ALU.add), reads=["pv"], writes=["pv"])
        dve(lambda e: e.tensor_scalar(pva("tmpb"), pva("tmpb"), -1.0, 1.0, ALU.mult, ALU.add), reads=["pv"], writes=["pv"])
        dup4("cc.q2b", cc["q2b"], pva("tmpb"))
        dve(lambda e: e.tensor_tensor(pva("tmpb"), pva("r2"), pva("tmpa"), ALU.subtract), reads=["pv"], writes=["pv"])
        dup4("cc.q3a", cc["q3a"], pva("tmpb"))
        dve(lambda e: e.reciprocal(pva("rstep"), pva("steps")), reads=["pv"], writes=["pv"])
        dve(lambda e: e.tensor_tensor(pva("tmpb"), pva("alpha"), pva("rstep"), ALU.mult), reads=["pv"], writes=["pv"])
        dve(lambda e: e.tensor_scalar(pva("tmpb"), pva("tmpb"), -1.0, None, ALU.mult, ALU.bypass), reads=["pv"], writes=["pv"])
        dup4("cc.q3b", cc["q3b"], pva("tmpb"))
        dup4("cc.R1b", cc["R1b"], pva("tmpa"))
        dve(lambda e: e.tensor_tensor(pva("tmpb"), pva("r"), pva("s2th"), ALU.mult), reads=["pv"], writes=["pv"])
        dve(lambda e: e.tensor_scalar(pva("tmpb"), pva("tmpb"), -1.0, None, ALU.mult, ALU.bypass), reads=["pv"], writes=["pv"])
        dup4("cc.R2a", cc["R2a"], pva("tmpb"))
        dup4("cc.R2b", cc["R2b"], pva("cth"))
        dve(lambda e: e.tensor_tensor(pva("tmpb"), pva("steps"), pva("s2th"), ALU.mult), reads=["pv"], writes=["pv"])
        dve(lambda e: e.tensor_tensor(pva("tmpb"), pva("tmpb"), pva("det"), ALU.mult), reads=["pv"], writes=["pv"])
        dup4("cc.i11", cc["i11"], pva("tmpb"))
        dve(lambda e: e.tensor_tensor(pva("tmpb"), pva("w2cr"), pva("det"), ALU.mult), reads=["pv"], writes=["pv"])
        dup4("cc.i12", cc["i12"], pva("tmpb"))
        dve(lambda e: e.tensor_scalar(pva("tmpb"), pva("tmpb"), -1.0, None, ALU.mult, ALU.bypass), reads=["pv"], writes=["pv"])
        dup4("cc.i21", cc["i21"], pva("tmpb"))
        dve(lambda e: e.tensor_tensor(pva("tmpb"), pva("steps"), pva("det"), ALU.mult), reads=["pv"], writes=["pv"])
        dup4("cc.i22", cc["i22"], pva("tmpb"))

        wbuf = [(outb[0][:, :], "outb0"), (outb[1][:, :], "outb1"), (tt[0][:, :], "tt0"), (tt[1][:, :], "tt1")]
        for chunk in range(8):
            for blk in range(4):
                buf, bres = wbuf[blk]
                dma("sp", lambda e, chunk=chunk, blk=blk, buf=buf: e.dma_start(
                    out=buf, in_=d_w1[chunk * 512 + blk * 128: chunk * 512 + (blk + 1) * 128, :]),
                    writes=[bres], sem=f"dwl{blk}")
            for blk in range(4):
                buf, bres = wbuf[blk]
                hb = chunk * 4 + blk
                act(lambda e, buf=buf, hb=hb, blk=blk:
                    e.activation(junk[blk % 2][:, :], buf, AF.Abs, accum_out=a1col[:, hb:hb + 1]),
                    reads=[bres], writes=[f"junk{blk % 2}", "a1col"])
                act(lambda e, buf=buf, blk=blk:
                    e.activation(n1tm[:, blk * 1024:(blk + 1) * 1024], buf, AF.Sign),
                    reads=[bres], writes=[f"n1tm.{blk}"])
            dma("gp", lambda e, chunk=chunk: e.dma_start(
                out=s_w1b[chunk * 512:(chunk + 1) * 512, :].rearrange("(a p) d -> p a d", p=128),
                in_=n1tm[:, :].rearrange("p (a d) -> p a d", a=4)),
                reads=["n1tm.0", "n1tm.1", "n1tm.2", "n1tm.3"], writes=["s_w1b"], sem="dws")
        dve(lambda e: e.tensor_scalar(a1col[:, :], a1col[:, :], 1.0 / H, None, ALU.mult, ALU.bypass),
            reads=["a1col"], writes=["a1col"])
        for chunk in range(8):
            for q4 in range(4):
                buf, bres = wbuf[q4]
                dma("sp", lambda e, chunk=chunk, q4=q4, buf=buf: e.dma_start(
                    out=buf, in_=d_w2[chunk * 128:(chunk + 1) * 128, q4 * 1024:(q4 + 1) * 1024]),
                    writes=[bres], sem=f"dwl{q4}")
            for q4 in range(4):
                buf, bres = wbuf[q4]
                acc = st3[:, 4 + q4:5 + q4]
                act(lambda e, q4=q4, buf=buf, acc=acc:
                    e.activation(junk[q4 % 2][:, :], buf, AF.Abs, accum_out=acc),
                    reads=[bres], writes=[f"junk{q4 % 2}", "st3.q"])
                act(lambda e, q4=q4, buf=buf:
                    e.activation(n1tm[:, q4 * 1024:(q4 + 1) * 1024], buf, AF.Sign),
                    reads=[bres], writes=[f"n1tm.{q4}"])
            dve(lambda e: e.tensor_tensor(st3[:, 4:5], st3[:, 4:5], st3[:, 5:6], ALU.add), reads=["st3.q"], writes=["st3.q"])
            dve(lambda e: e.tensor_tensor(st3[:, 6:7], st3[:, 6:7], st3[:, 7:8], ALU.add), reads=["st3.q"], writes=["st3.q"])
            dve(lambda e: e.tensor_tensor(st3[:, 4:5], st3[:, 4:5], st3[:, 6:7], ALU.add), reads=["st3.q"], writes=["st3.q"])
            dve(lambda e: e.tensor_scalar(st3[:, 4:5], st3[:, 4:5], 1.0 / HF, None, ALU.mult, ALU.bypass), reads=["st3.q"], writes=["st3.q"])
            dve(lambda e: e.tensor_scalar(n1tm[:, :], n1tm[:, :], st3[:, 4:5], None, ALU.mult, ALU.bypass),
                reads=["n1tm.0", "n1tm.1", "n1tm.2", "n1tm.3", "st3.q"], writes=["n1tm.0", "n1tm.1", "n1tm.2", "n1tm.3"])
            dma("gp", lambda e, chunk=chunk: e.dma_start(
                out=s_w2b[chunk * 128:(chunk + 1) * 128, :], in_=n1tm[:, :]),
                reads=["n1tm.0", "n1tm.1", "n1tm.2", "n1tm.3"], writes=["s_w2b"], sem="dws")
        trb1 = [("n1fm", n1fm), ("n2fm", n2fm)]
        for kb in range(8):
            nm, buf = trb1[kb % 2]
            dma("sp", lambda e, kb=kb, buf=buf: e.dma_start(
                out=buf[:, :], in_=s_w1b[:, kb * 128:(kb + 1) * 128], transpose=True),
                reads=["s_w1b"], writes=[nm], sem=f"dtr{kb % 2}")
            dma("gp", lambda e, kb=kb, buf=buf: e.dma_start(
                out=s_w1t[:, :, kb * 512:(kb + 1) * 512].rearrange("j p h -> p j h"),
                in_=buf[:, :].rearrange("p (j h) -> p j h", j=8)),
                reads=[nm], writes=["s_w1t"], sem=f"dwb{kb % 2}")
        trb2 = [(["n1fm"], n1fm[:, 0:1024]), (["n2fm"], n2fm[:, 0:1024]), (["n1tm.0"], n1tm[:, 0:1024])]
        for hb in range(32):
            nm, bap = trb2[hb % 3]
            dma("sp", lambda e, hb=hb, bap=bap: e.dma_start(
                out=bap, in_=s_w2b[:, hb * 128:(hb + 1) * 128], transpose=True),
                reads=["s_w2b"], writes=nm, sem=f"dtr{hb % 3}")
            dma("gp", lambda e, hb=hb, bap=bap: e.dma_start(
                out=s_w2t[hb // 4, :, (hb % 4) * 1024:(hb % 4 + 1) * 1024], in_=bap),
                reads=nm, writes=["s_w2t"], sem=f"dwb{hb % 3}")

        def emit_table(Cd, Bd, cresC, cresB, sres, sres2, colw, basen, ncols):
            for h in range(2):
                co = h * colw
                base = pva(basen)[:, h:h + 1]
                cthh = pv[:, PV["cth"].start + h: PV["cth"].start + h + 1]
                s2h = pv[:, PV["s2th"].start + h: PV["s2th"].start + h + 1]
                Bss = pv[:, PV["Bss"].start + h: PV["Bss"].start + h + 1]
                dve(lambda e, a=Cd[:, co:co + 1]: e.memset(a, 1.0), writes=[cresC])
                dve(lambda e, a=Bd[:, co:co + 1]: e.memset(a, 0.0), writes=[cresB])
                dve(lambda e, a=Cd[:, co + 1:co + 2], b=base, c=cthh: e.tensor_tensor(a, b, c, ALU.mult),
                    reads=["pv"], writes=[cresC])
                dve(lambda e, a=Bd[:, co + 1:co + 2], b=base: e.tensor_copy(a, b), reads=["pv"], writes=[cresB])
                w_ = 2
                while w_ < ncols:
                    n = min(w_ - 1, ncols - w_)
                    Csc = Cd[:, co + w_ - 1:co + w_]
                    Bsc = Bd[:, co + w_ - 1:co + w_]
                    dve(lambda e, Bss=Bss, Bsc=Bsc, s2h=s2h: e.tensor_tensor(Bss, Bsc, s2h, ALU.mult),
                        reads=[cresB, "pv"], writes=["pv"])
                    dve(lambda e, n=n, c0=co, Csc=Csc: e.tensor_scalar(scw[:, 0:n], Cd[:, c0 + 1:c0 + 1 + n], Csc, None, ALU.mult, ALU.bypass),
                        reads=[cresC], writes=[sres])
                    dve(lambda e, n=n, c0=co, Bsc=Bsc: e.tensor_scalar(scw2[:, 0:n], Cd[:, c0 + 1:c0 + 1 + n], Bsc, None, ALU.mult, ALU.bypass),
                        reads=[cresC, cresB], writes=[sres2])
                    dve(lambda e, n=n, c0=co, w_=w_, Bss=Bss: e.tensor_scalar(Cd[:, c0 + w_:c0 + w_ + n], Bd[:, c0 + 1:c0 + 1 + n], Bss, None, ALU.mult, ALU.bypass),
                        reads=[cresB, "pv"], writes=[cresC])
                    dve(lambda e, n=n, c0=co, w_=w_: e.tensor_tensor(Cd[:, c0 + w_:c0 + w_ + n], scw[:, 0:n], Cd[:, c0 + w_:c0 + w_ + n], ALU.subtract),
                        reads=[cresC, sres], writes=[cresC])
                    dve(lambda e, n=n, c0=co, w_=w_, Csc=Csc: e.tensor_scalar(Bd[:, c0 + w_:c0 + w_ + n], Bd[:, c0 + 1:c0 + 1 + n], Csc, None, ALU.mult, ALU.bypass),
                        reads=[cresB, cresC], writes=[cresB])
                    dve(lambda e, n=n, c0=co, w_=w_: e.tensor_tensor(Bd[:, c0 + w_:c0 + w_ + n], Bd[:, c0 + w_:c0 + w_ + n], scw2[:, 0:n], ALU.add),
                        reads=[cresB, sres2], writes=[cresB])
                    w_ = w_ + n

        emit_table(g_a, g_b, "ga", "gb", "u1.0", "u2.0", T, "rinv", T)
        for h in range(2):
            rc = pv[:, PV["r"].start + h: PV["r"].start + h + 1]
            ric = pv[:, PV["rinv"].start + h: PV["rinv"].start + h + 1]
            co = h * 513
            go = h * T
            dve(lambda e: e.memset(scw2[:, 0:512], 0.0), writes=["u2.0"])
            dve(lambda e, rc=rc: e.tensor_scalar(scw[:, 0:512], scw2[:, 0:512], rc, None, ALU.add, ALU.bypass),
                reads=["pv", "u2.0"], writes=["u1.0"])
            dve(lambda e, ric=ric: e.tensor_tensor_scan(scw[:, 0:512], scw[:, 0:512], scw2[:, 0:512], ric, ALU.mult, ALU.add),
                reads=["u1.0", "u2.0", "pv"], writes=["u1.0"])
            dve(lambda e, co=co, go=go: e.tensor_tensor(tabA[:, co:co + 512], g_a[:, go:go + 512], scw[:, 0:512], ALU.mult),
                reads=["ga", "u1.0"], writes=["mo0"])
            dve(lambda e, co=co: e.tensor_tensor(tabA[:, co:co + 512], tabA[:, co:co + 512], scw[:, 0:512], ALU.mult),
                reads=["mo0", "u1.0"], writes=["mo0"])
            dve(lambda e, co=co, go=go: e.tensor_tensor(tabB[:, co:co + 512], g_b[:, go:go + 512], scw[:, 0:512], ALU.mult),
                reads=["gb", "u1.0"], writes=["mo1"])
            dve(lambda e, co=co: e.tensor_tensor(tabB[:, co:co + 512], tabB[:, co:co + 512], scw[:, 0:512], ALU.mult),
                reads=["mo1", "u1.0"], writes=["mo1"])
            s2h = pv[:, PV["s2th"].start + h: PV["s2th"].start + h + 1]
            dve(lambda e, co=co: e.tensor_tensor(wt1[:, 0:1], tabB[:, co + 511:co + 512], tabB[:, co + 1:co + 2], ALU.mult),
                reads=["mo1"], writes=["wt1"])
            dve(lambda e, s2h=s2h: e.tensor_tensor(wt1[:, 0:1], wt1[:, 0:1], s2h, ALU.mult),
                reads=["wt1", "pv"], writes=["wt1"])
            dve(lambda e, co=co: e.tensor_tensor(tabA[:, co + 512:co + 513], tabA[:, co + 511:co + 512], tabA[:, co + 1:co + 2], ALU.mult),
                reads=["mo0"], writes=["mo0"])
            dve(lambda e, co=co: e.tensor_tensor(tabA[:, co + 512:co + 513], tabA[:, co + 512:co + 513], wt1[:, 0:1], ALU.subtract),
                reads=["mo0", "wt1"], writes=["mo0"])
            dve(lambda e, co=co: e.tensor_tensor(wt1[:, 0:1], tabB[:, co + 511:co + 512], tabA[:, co + 1:co + 2], ALU.mult),
                reads=["mo1", "mo0"], writes=["wt1"])
            dve(lambda e, co=co: e.tensor_tensor(wt2[:, 0:1], tabA[:, co + 511:co + 512], tabB[:, co + 1:co + 2], ALU.mult),
                reads=["mo0", "mo1"], writes=["wt2"])
            dve(lambda e, co=co: e.tensor_tensor(tabB[:, co + 512:co + 513], wt1[:, 0:1], wt2[:, 0:1], ALU.add),
                reads=["wt1", "wt2"], writes=["mo1"])

        for h in range(2):
            co = h * 513
            go = h * T
            stc = pv[:, PV["steps"].start + h: PV["steps"].start + h + 1]
            s2c = pv[:, PV["s2th"].start + h: PV["s2th"].start + h + 1]
            w2c_ = pv[:, PV["w2cr"].start + h: PV["w2cr"].start + h + 1]
            w1c_ = pv[:, PV["w1cr"].start + h: PV["w1cr"].start + h + 1]
            rin = pv[:, PV["rinv"].start + h: PV["rinv"].start + h + 1]
            dve(lambda e, co=co, go=go, stc=stc: e.tensor_scalar(g_G1[:, go:go + T], tabA[:, co:co + T], stc, None, ALU.mult, ALU.bypass),
                reads=["mo0", "pv"], writes=["gG1"])
            dve(lambda e, co=co, w2c_=w2c_: e.tensor_scalar(scw[:, 0:T], tabB[:, co:co + T], w2c_, None, ALU.mult, ALU.bypass),
                reads=["mo1", "pv"], writes=["u1.0"])
            dve(lambda e, go=go: e.tensor_tensor(g_G1[:, go:go + T], g_G1[:, go:go + T], scw[:, 0:T], ALU.add),
                reads=["gG1", "u1.0"], writes=["gG1"])
            dve(lambda e, co=co, go=go, stc=stc: e.tensor_scalar(g_G2[:, go:go + T], tabB[:, co:co + T], stc, None, ALU.mult, ALU.bypass),
                reads=["mo1", "pv"], writes=["gG2"])
            dve(lambda e, go=go, s2c=s2c: e.tensor_scalar(g_G2[:, go:go + T], g_G2[:, go:go + T], s2c, None, ALU.mult, ALU.bypass),
                reads=["gG2", "pv"], writes=["gG2"])
            dve(lambda e, co=co, w2c_=w2c_: e.tensor_scalar(scw[:, 0:T], tabA[:, co:co + T], w2c_, None, ALU.mult, ALU.bypass),
                reads=["mo0", "pv"], writes=["u1.0"])
            dve(lambda e, go=go: e.tensor_tensor(g_G2[:, go:go + T], g_G2[:, go:go + T], scw[:, 0:T], ALU.subtract),
                reads=["gG2", "u1.0"], writes=["gG2"])
            for off in (0, 2):
                col = h + off
                atm1 = tabA[:, co + T - 1:co + T]
                btm1 = tabB[:, co + T - 1:co + T]
                atT = tabA[:, co + T:co + T + 1]
                btT = tabB[:, co + T:co + T + 1]
                dve(lambda e, col=col, w1c_=w1c_, btm1=btm1: e.tensor_tensor(cc["g1a"][:, col:col + 1], w1c_, btm1, ALU.mult),
                    reads=["mo1", "pv"], writes=["cc.g1a"])
                dve(lambda e, col=col, atm1=atm1: e.tensor_tensor(cc["g1a"][:, col:col + 1], cc["g1a"][:, col:col + 1], atm1, ALU.add),
                    reads=["cc.g1a", "mo0"], writes=["cc.g1a"])
                dve(lambda e, col=col, s2c=s2c, btm1=btm1: e.tensor_tensor(cc["g1b"][:, col:col + 1], s2c, btm1, ALU.mult),
                    reads=["mo1", "pv"], writes=["cc.g1b"])
                dve(lambda e, w1c_=w1c_, atm1=atm1: e.tensor_tensor(wt1[:, 0:1], w1c_, atm1, ALU.mult),
                    reads=["mo0", "pv"], writes=["wt1"])
                dve(lambda e, col=col: e.tensor_tensor(cc["g1b"][:, col:col + 1], cc["g1b"][:, col:col + 1], wt1[:, 0:1], ALU.subtract),
                    reads=["cc.g1b", "wt1"], writes=["cc.g1b"])
                dve(lambda e, col=col, atT=atT: e.tensor_copy(cc["cAT"][:, col:col + 1], atT),
                    reads=["mo0"], writes=["cc.cAT"])
                dve(lambda e, col=col, btT=btT, rin=rin: e.tensor_tensor(cc["cBT"][:, col:col + 1], btT, rin, ALU.mult),
                    reads=["mo1", "pv"], writes=["cc.cBT"])

        for b4 in range(4):
            hp = b4 % 2
            dve(lambda e, b4=b4, hp=hp: e.tensor_copy(cc["G1e"][:, b4:b4 + 1], g_G1[:, hp * T + T - 1: hp * T + T]),
                reads=["gG1"], writes=["cc.G1e"])
            dve(lambda e, b4=b4, hp=hp: e.tensor_copy(cc["G2e"][:, b4:b4 + 1], g_G2[:, hp * T + T - 1: hp * T + T]),
                reads=["gG2"], writes=["cc.G2e"])

        for pbk in range(2):
            fsc = pv[:, PV["fscale"].start + pbk: PV["fscale"].start + pbk + 1]
            dve(lambda e, pbk=pbk, fsc=fsc: e.tensor_scalar(dtm[:, pbk * 2048:(pbk + 1) * 2048], sx[pbk][:, :], fsc, None, ALU.mult, ALU.bypass),
                reads=[f"sx{pbk}.0", f"sx{pbk}.1", "pv"], writes=["dtm"])
        bview = dtm.ap().rearrange("q (pb h t) -> q pb h t", pb=2, t=2)
        for t_ in range(2):
            for pbk in range(2):
                bk = (t_ * 2 + pbk) % 2
                for hb in range(8):
                    inap = bview[:, pbk, hb * 128:(hb + 1) * 128, t_]
                    pe(lambda e, bk=bk, hb=hb, inap=inap:
                       e.transpose(pt[bk][:, hb * 128:(hb + 1) * 128], inap, id16[:, :]),
                       reads=["dtm", "id16"], writes=[PT[bk]])
                outap = cB.ap().rearrange("q (k phi) -> q k phi", k=8)[:, :, t_ * 256 + pbk * 128: t_ * 256 + pbk * 128 + 128]
                dve(lambda e, bk=bk, outap=outap: e.tensor_copy(outap, pt[bk][:, :]),
                    reads=[PT[bk]], writes=["cB"])

        for rnd in range(2):
            dma("sp", lambda e, rnd=rnd: e.dma_start(
                out=sx[2][:, :].rearrange("p (db qt) -> p db qt", db=4),
                in_=d_C[rnd * 512:(rnd + 1) * 512, :, :].rearrange("(db p) q t -> p db (q t)", p=128)),
                writes=["sx2.0", "sx2.1"], sem="dp")
            dve(lambda e: e.tensor_copy(ys[:, 0:2048], sx[2][:, :]),
                reads=["sx2.0", "sx2.1"], writes=["ys.0", "ys.1", "ys.2", "ys.3"])
            cview = ys[:, 0:2048].rearrange("q (db pq t) -> q db pq t", db=4, t=2)
            for db4 in range(4):
                db = rnd * 4 + db4
                for kphi in range(4):
                    t_ = kphi // 2
                    pbk = kphi % 2
                    bk = (db4 * 4 + kphi) % 2
                    inap = cview[:, db4, pbk * 128:(pbk + 1) * 128, t_]
                    pe(lambda e, bk=bk, inap=inap: e.transpose(pt[bk][:, 0:128], inap, id16[:, :]),
                       reads=["ys.0", "ys.1", "ys.2", "ys.3", "id16"], writes=[PT[bk]])
                    dst = cC[:, kphi * 1024 + db * 128: kphi * 1024 + db * 128 + 128]
                    if t_ == 0:
                        dve(lambda e, bk=bk, dst=dst: e.tensor_copy(dst, pt[bk][:, 0:128]),
                            reads=[PT[bk]], writes=["cC"])
                    else:
                        dve(lambda e, bk=bk, dst=dst: e.tensor_scalar(dst, pt[bk][:, 0:128], -1.0, None, ALU.mult, ALU.bypass),
                            reads=[PT[bk]], writes=["cC"])

        halfseq = [(c, s, hf) for c in range(NCH) for s in range(SEQS) for hf in range(2)]
        slot_of = {key: i % 4 for i, key in enumerate(halfseq)}
        next_load = [0]

        def emit_xloads(upto):
            while next_load[0] < min(upto, len(halfseq)):
                ckey = halfseq[next_load[0]]
                c, s, hf = ckey
                r = slot_of[ckey]
                dma("gp", lambda e, c=c, s=s, hf=hf, r=r: e.dma_start(
                    out=sx[r][:, :].rearrange("p (tl d) -> p tl d", tl=2),
                    in_=d_x[s].rearrange("(a p) d -> p a d", p=128)[:, (c * 2 + hf) * 2:(c * 2 + hf) * 2 + 2, :]),
                    writes=[f"sx{r}.0", f"sx{r}.1"], sem=f"dx{r}")
                next_load[0] += 1

        emit_xloads(4)

        def ln_batch_stats(stn, stt_, xtiles, xres):
            for i in range(4):
                act(lambda e, i=i, xt=xtiles[i]: e.activation(junk[0][:, :], xt, AF.Square, accum_out=stt_[:, 4 + i:5 + i]),
                    reads=[xres[i]], writes=["junk0", stn + ".q"])
                dve(lambda e, i=i, xt=xtiles[i]: e.tensor_scalar(junk[1][:, :], xt, 1.0, None, ALU.mult, ALU.add, accum_out=stt_[:, i:i + 1]),
                    reads=[xres[i]], writes=["junk1", stn + ".s"])
            dve(lambda e: e.tensor_scalar(stt_[:, 8:12], stt_[:, 0:4], -1.0 / H, None, ALU.mult, ALU.bypass),
                reads=[stn + ".s"], writes=[stn + ".nm"])
            dve(lambda e: e.tensor_tensor(wt1[:, 0:4], stt_[:, 8:12], stt_[:, 8:12], ALU.mult),
                reads=[stn + ".nm"], writes=["wt1"])
            dve(lambda e: e.tensor_scalar(wt2[:, 0:4], stt_[:, 4:8], 1.0 / H, None, ALU.mult, ALU.bypass),
                reads=[stn + ".q"], writes=["wt2"])
            dve(lambda e: e.tensor_tensor(wt2[:, 0:4], wt2[:, 0:4], wt1[:, 0:4], ALU.subtract),
                reads=["wt2", "wt1"], writes=["wt2"])
            dve(lambda e: e.tensor_scalar(wt2[:, 0:4], wt2[:, 0:4], EPS, None, ALU.add, ALU.bypass),
                reads=["wt2"], writes=["wt2"])
            act(lambda e: e.activation(wt2[:, 0:4], wt2[:, 0:4], AF.Sqrt),
                reads=["wt2"], writes=["wt2"])
            dve(lambda e: e.reciprocal(stt_[:, 12:16], wt2[:, 0:4]), reads=["wt2"], writes=[stn + ".rs"])
            dve(lambda e: e.tensor_tensor(stt_[:, 8:12], stt_[:, 8:12], stt_[:, 12:16], ALU.mult),
                reads=[stn + ".nm", stn + ".rs"], writes=[stn + ".nm"])

        def ln_tile_stats(srcap, srcres, jslot):
            act(lambda e: e.activation(junk[0][:, :], srcap, AF.Square, accum_out=st3[:, 17:18]),
                reads=[srcres], writes=["junk0", "st3.q2"])
            dve(lambda e: e.tensor_scalar(junk[1][:, :], srcap, 1.0, None, ALU.mult, ALU.add, accum_out=st3[:, 16:17]),
                reads=[srcres], writes=["junk1", "st3.s"])
            dve(lambda e: e.tensor_scalar(st3[:, 18:19], st3[:, 16:17], -1.0 / H, None, ALU.mult, ALU.bypass),
                reads=["st3.s"], writes=["st3.nm"])
            dve(lambda e: e.tensor_tensor(wt1[:, 0:1], st3[:, 18:19], st3[:, 18:19], ALU.mult),
                reads=["st3.nm"], writes=["wt1"])
            dve(lambda e: e.tensor_scalar(wt2[:, 0:1], st3[:, 17:18], 1.0 / H, None, ALU.mult, ALU.bypass),
                reads=["st3.q2"], writes=["wt2"])
            dve(lambda e: e.tensor_tensor(wt2[:, 0:1], wt2[:, 0:1], wt1[:, 0:1], ALU.subtract),
                reads=["wt2", "wt1"], writes=["wt2"])
            dve(lambda e: e.tensor_scalar(wt2[:, 0:1], wt2[:, 0:1], EPS, None, ALU.add, ALU.bypass),
                reads=["wt2"], writes=["wt2"])
            act(lambda e: e.activation(wt2[:, 0:1], wt2[:, 0:1], AF.Sqrt), reads=["wt2"], writes=["wt2"])
            dve(lambda e: e.reciprocal(st3[:, 19:20], wt2[:, 0:1]), reads=["wt2"], writes=["st3.rs"])
            dve(lambda e: e.tensor_tensor(st3[:, 18:19], st3[:, 18:19], st3[:, 19:20], ALU.mult),
                reads=["st3.nm", "st3.rs"], writes=["st3.nm"])

        NU = NCH * SEQS
        units = [(c, s) for c in range(NCH) for s in range(SEQS)]

        def unit_tiles(u):
            c, s = units[u]
            r0 = slot_of[(c, s, 0)]
            r1 = slot_of[(c, s, 1)]
            xres = [f"sx{r0}.0", f"sx{r0}.1", f"sx{r1}.0", f"sx{r1}.1"]
            xtiles = [sx[r0][:, 0:H], sx[r0][:, H:2 * H], sx[r1][:, 0:H], sx[r1][:, H:2 * H]]
            return c, s, xres, xtiles

        def fe1a(u):
            c, s, xres, xtiles = unit_tiles(u)
            for i in range(4):
                act(lambda e, i=i, xt=xtiles[i]: e.activation(junk[0][:, :], xt, AF.Square, accum_out=st1[:, 4 + i:5 + i]),
                    reads=[xres[i]], writes=["junk0", "st1.q"])
                dve(lambda e, i=i, xt=xtiles[i]: e.tensor_scalar(junk[1][:, :], xt, 1.0, None, ALU.mult, ALU.add, accum_out=st1[:, i:i + 1]),
                    reads=[xres[i]], writes=["junk1", "st1.s"])

        def fe1b(u):
            c, s, xres, xtiles = unit_tiles(u)
            dve(lambda e: e.tensor_scalar(st1[:, 8:12], st1[:, 0:4], -1.0 / H, None, ALU.mult, ALU.bypass),
                reads=["st1.s"], writes=["st1.nm"])
            dve(lambda e: e.tensor_tensor(wt1[:, 0:4], st1[:, 8:12], st1[:, 8:12], ALU.mult),
                reads=["st1.nm"], writes=["wt1"])
            dve(lambda e: e.tensor_scalar(wt2[:, 0:4], st1[:, 4:8], 1.0 / H, None, ALU.mult, ALU.bypass),
                reads=["st1.q"], writes=["wt2"])
            dve(lambda e: e.tensor_tensor(wt2[:, 0:4], wt2[:, 0:4], wt1[:, 0:4], ALU.subtract),
                reads=["wt2", "wt1"], writes=["wt2"])
            dve(lambda e: e.tensor_scalar(wt2[:, 0:4], wt2[:, 0:4], EPS, None, ALU.add, ALU.bypass),
                reads=["wt2"], writes=["wt2"])
            act(lambda e: e.activation(wt2[:, 0:4], wt2[:, 0:4], AF.Sqrt), reads=["wt2"], writes=["wt2"])
            dve(lambda e: e.reciprocal(st1[:, 12:16], wt2[:, 0:4]), reads=["wt2"], writes=["st1.rs"])
            dve(lambda e: e.tensor_tensor(st1[:, 8:12], st1[:, 8:12], st1[:, 12:16], ALU.mult),
                reads=["st1.nm", "st1.rs"], writes=["st1.nm"])
            for i in range(4):
                act(lambda e, i=i, xt=xtiles[i]: e.activation(n1tm[:, i * H:(i + 1) * H], xt, AF.Identity,
                                                scale=st1[:, 12 + i:13 + i], bias=st1[:, 8 + i:9 + i]),
                    reads=[xres[i], "st1.rs", "st1.nm"], writes=[f"n1tm.{i}"])

        def t1(u):
            for i in range(4):
                bk = i % 2
                for k in range(8):
                    pe(lambda e, i=i, k=k, bk=bk: e.transpose(
                        pt[bk][:, k * 128:(k + 1) * 128], n1tm[:, i * H + k * 128: i * H + (k + 1) * 128], id16[:, :]),
                        reads=[f"n1tm.{i}", "id16"], writes=[PT[bk]])
                outap = n1fm.ap().rearrange("q (k j) -> q k j", k=8)[:, :, i * 128:(i + 1) * 128]
                dve(lambda e, bk=bk, outap=outap: e.tensor_copy(outap, pt[bk][:, :]),
                    reads=[PT[bk]], writes=["n1fm"])

        def dterms(u):
            for m in range(4):
                dve(lambda e, m=m: e.tensor_tensor(dtm[:, m * H:(m + 1) * H], drep[:, :], n1tm[:, m * H:(m + 1) * H], ALU.mult),
                    reads=["drep", f"n1tm.{m}"], writes=["dtm"])

        def scan_pre(u):
            c, s, xres, xtiles = unit_tiles(u)
            P1c, P2c = P1s[s], P2s[s]
            rP1, rP2 = f"P1.{s}", f"P2.{s}"
            dve(lambda e, P1c=P1c: e.tensor_tensor(wq2[:, :], cc["q2a"][:, :], P1c[:, :], ALU.mult), reads=["cc.q2a", rP1], writes=["wq2"])
            dve(lambda e, P2c=P2c: e.tensor_tensor(wt1[:, 0:4], cc["q2b"][:, :], P2c[:, :], ALU.mult), reads=["cc.q2b", rP2], writes=["wt1"])
            dve(lambda e: e.tensor_tensor(wq2[:, :], wq2[:, :], wt1[:, 0:4], ALU.add), reads=["wq2", "wt1"], writes=["wq2"])
            dve(lambda e, P1c=P1c: e.tensor_tensor(wq3[:, :], cc["q3a"][:, :], P1c[:, :], ALU.mult), reads=["cc.q3a", rP1], writes=["wq3"])
            dve(lambda e, P2c=P2c: e.tensor_tensor(wt1[:, 0:4], cc["q3b"][:, :], P2c[:, :], ALU.mult), reads=["cc.q3b", rP2], writes=["wt1"])
            dve(lambda e: e.tensor_tensor(wq3[:, :], wq3[:, :], wt1[:, 0:4], ALU.add), reads=["wq3", "wt1"], writes=["wq3"])
            dve(lambda e, P2c=P2c: e.tensor_tensor(wR1[:, :], cc["R1b"][:, :], P2c[:, :], ALU.mult), reads=["cc.R1b", rP2], writes=["wR1"])
            dve(lambda e: e.tensor_tensor(wR1[:, :], wR1[:, :], wq2[:, :], ALU.add), reads=["wR1", "wq2"], writes=["wR1"])
            dve(lambda e, P2c=P2c: e.tensor_tensor(wR2[:, :], cc["R2a"][:, :], P2c[:, :], ALU.mult), reads=["cc.R2a", rP2], writes=["wR2"])
            dve(lambda e: e.tensor_tensor(wt1[:, 0:4], cc["R2b"][:, :], wq2[:, :], ALU.mult), reads=["cc.R2b", "wq2"], writes=["wt1"])
            dve(lambda e: e.tensor_tensor(wR2[:, :], wR2[:, :], wt1[:, 0:4], ALU.add), reads=["wR2", "wt1"], writes=["wR2"])
            dve(lambda e: e.tensor_tensor(dlc[:, :], cc["i11"][:, :], wR1[:, :], ALU.mult), reads=["cc.i11", "wR1"], writes=["dlc"])
            dve(lambda e: e.tensor_tensor(wt1[:, 0:4], cc["i12"][:, :], wR2[:, :], ALU.mult), reads=["cc.i12", "wR2"], writes=["wt1"])
            dve(lambda e: e.tensor_tensor(dlc[:, :], dlc[:, :], wt1[:, 0:4], ALU.add), reads=["dlc", "wt1"], writes=["dlc"])
            dve(lambda e: e.tensor_tensor(dls[:, :], cc["i21"][:, :], wR1[:, :], ALU.mult), reads=["cc.i21", "wR1"], writes=["dls"])
            dve(lambda e: e.tensor_tensor(wt1[:, 0:4], cc["i22"][:, :], wR2[:, :], ALU.mult), reads=["cc.i22", "wR2"], writes=["wt1"])
            dve(lambda e: e.tensor_tensor(dls[:, :], dls[:, :], wt1[:, 0:4], ALU.add), reads=["dls", "wt1"], writes=["dls"])

        def bu_scan(u):
            for b in range(4):
                bk = b % 2
                hp = b % 2
                for k in range(8):
                    pe(lambda e, b=b, k=k, bk=bk: e.matmul(
                        pb[bk][:, :], cB[:, k * 512 + b * 128: k * 512 + (b + 1) * 128],
                        n1fm[:, k * 512:(k + 1) * 512], start=(k == 0), stop=(k == 7)),
                        reads=["cB", "n1fm"], writes=[PB[bk]])
                dve(lambda e, b=b, bk=bk, hp=hp: e.tensor_tensor(u1[b][:, :], pb[bk][:, :], g_a[:, hp * T:(hp + 1) * T], ALU.mult),
                    reads=[PB[bk], "ga"], writes=[f"u1.{b}"])
                dve(lambda e, b=b, bk=bk, hp=hp: e.tensor_tensor(u2[b][:, :], pb[bk][:, :], g_b[:, hp * T:(hp + 1) * T], ALU.mult),
                    reads=[PB[bk], "gb"], writes=[f"u2.{b}"])
            for b in range(4):
                hp = b % 2
                dve(lambda e, b=b: e.tensor_tensor_scan(u1[b][:, :], u1[b][:, :], u1[b][:, :], dlc[:, b:b + 1], ALU.add, ALU.bypass),
                    reads=[f"u1.{b}", "dlc"], writes=[f"u1.{b}"])
                dve(lambda e, b=b: e.tensor_tensor_scan(u2[b][:, :], u2[b][:, :], u2[b][:, :], dls[:, b:b + 1], ALU.add, ALU.bypass),
                    reads=[f"u2.{b}", "dls"], writes=[f"u2.{b}"])
                dve(lambda e, b=b: e.tensor_copy(pce[:, b:b + 1], u1[b][:, T - 1:T]),
                    reads=[f"u1.{b}"], writes=["pce"])
                dve(lambda e, b=b: e.tensor_copy(pse[:, b:b + 1], u2[b][:, T - 1:T]),
                    reads=[f"u2.{b}"], writes=["pse"])
                dve(lambda e, b=b, hp=hp: e.tensor_tensor(u1[b][:, :], u1[b][:, :], g_G1[:, hp * T:(hp + 1) * T], ALU.mult),
                    reads=[f"u1.{b}", "gG1"], writes=[f"u1.{b}"])
                dve(lambda e, b=b, hp=hp: e.tensor_tensor(u2[b][:, :], u2[b][:, :], g_G2[:, hp * T:(hp + 1) * T], ALU.mult),
                    reads=[f"u2.{b}", "gG2"], writes=[f"u2.{b}"])
                dve(lambda e, b=b: e.tensor_tensor(ys[:, b * T:(b + 1) * T], u1[b][:, :], u2[b][:, :], ALU.add),
                    reads=[f"u1.{b}", f"u2.{b}"], writes=[f"ys.{b}"])

        def carry_update(u):
            c, s, xres, xtiles = unit_tiles(u)
            P1c, P2c = P1s[s], P2s[s]
            rP1, rP2 = f"P1.{s}", f"P2.{s}"
            dve(lambda e, P2c=P2c: e.tensor_tensor(P2c[:, :], cc["G1e"][:, :], pce[:, :], ALU.mult), reads=["cc.G1e", "pce"], writes=[rP2])
            dve(lambda e: e.tensor_tensor(wt2[:, 0:4], cc["G2e"][:, :], pse[:, :], ALU.mult), reads=["cc.G2e", "pse"], writes=["wt2"])
            dve(lambda e, P2c=P2c: e.tensor_tensor(P2c[:, :], P2c[:, :], wt2[:, 0:4], ALU.add), reads=[rP2, "wt2"], writes=[rP2])
            dve(lambda e: e.tensor_tensor(pce[:, :], pce[:, :], dlc[:, :], ALU.subtract), reads=["pce", "dlc"], writes=["pce"])
            dve(lambda e: e.tensor_tensor(pse[:, :], pse[:, :], dls[:, :], ALU.subtract), reads=["pse", "dls"], writes=["pse"])
            dve(lambda e, P1c=P1c: e.tensor_tensor(wt1[:, 0:4], cc["cAT"][:, :], P1c[:, :], ALU.mult), reads=["cc.cAT", rP1], writes=["wt1"])
            dve(lambda e: e.tensor_tensor(wt2[:, 0:4], cc["cBT"][:, :], wq3[:, :], ALU.mult), reads=["cc.cBT", "wq3"], writes=["wt2"])
            dve(lambda e: e.tensor_tensor(wt1[:, 0:4], wt1[:, 0:4], wt2[:, 0:4], ALU.add), reads=["wt1", "wt2"], writes=["wt1"])
            dve(lambda e, P1c=P1c: e.tensor_tensor(P1c[:, :], cc["g1a"][:, :], pce[:, :], ALU.mult), reads=["cc.g1a", "pce"], writes=[rP1])
            dve(lambda e: e.tensor_tensor(wt2[:, 0:4], cc["g1b"][:, :], pse[:, :], ALU.mult), reads=["cc.g1b", "pse"], writes=["wt2"])
            dve(lambda e, P1c=P1c: e.tensor_tensor(P1c[:, :], P1c[:, :], wt2[:, 0:4], ALU.add), reads=[rP1, "wt2"], writes=[rP1])
            dve(lambda e, P1c=P1c: e.tensor_tensor(P1c[:, :], P1c[:, :], wt1[:, 0:4], ALU.add), reads=[rP1, "wt1"], writes=[rP1])

        def cmm_one(u, m):
            mb = m % 2
            for kb in range(4):
                for ng in range(2):
                    pe(lambda e, m=m, kb=kb, ng=ng, mb=mb: e.matmul(
                        pcm[mb][:, ng * 512:(ng + 1) * 512],
                        ys[:, kb * T + m * 128: kb * T + (m + 1) * 128],
                        cC[:, kb * 1024 + ng * 512: kb * 1024 + (ng + 1) * 512],
                        start=(kb == 0), stop=(kb == 3)),
                        reads=[f"ys.{kb}", "cC"], writes=[PCM[mb][ng]])

        def mixes(u):
            c, s, xres, xtiles = unit_tiles(u)
            mixt = [(mo[0][:, 0:H], "mo0"), (mo[1][:, 0:H], "mo1"),
                    (outb[0][:, :], "outb0"), (outb[1][:, :], "outb1")]
            return mixt

        def mix_one(u, m, mixt):
            mb = m % 2
            mt, mres = mixt[m]
            dve(lambda e, m=m, mb=mb, mt=mt: e.tensor_tensor(mt, pcm[mb][:, :], dtm[:, m * H:(m + 1) * H], ALU.add),
                reads=[PCM[mb][0], PCM[mb][1], "dtm"], writes=[mres])

        def ln3_batch(u, mixt):
            c, s, xres, xtiles = unit_tiles(u)
            for m in range(4):
                mt, mres = mixt[m]
                act(lambda e, m=m, mt=mt: e.activation(junk[0][:, :], mt, AF.Square, accum_out=st3[:, 4 + m:5 + m]),
                    reads=[mres], writes=["junk0", "st3b.q"])
                dve(lambda e, m=m, mt=mt: e.tensor_scalar(junk[1][:, :], mt, 1.0, None, ALU.mult, ALU.add, accum_out=st3[:, m:m + 1]),
                    reads=[mres], writes=["junk1", "st3b.s"])
            dve(lambda e: e.tensor_scalar(st3[:, 8:12], st3[:, 0:4], -1.0 / H, None, ALU.mult, ALU.bypass),
                reads=["st3b.s"], writes=["st3b.nm"])
            dve(lambda e: e.tensor_tensor(wt1[:, 0:4], st3[:, 8:12], st3[:, 8:12], ALU.mult),
                reads=["st3b.nm"], writes=["wt1"])
            dve(lambda e: e.tensor_scalar(wt2[:, 0:4], st3[:, 4:8], 1.0 / H, None, ALU.mult, ALU.bypass),
                reads=["st3b.q"], writes=["wt2"])
            dve(lambda e: e.tensor_tensor(wt2[:, 0:4], wt2[:, 0:4], wt1[:, 0:4], ALU.subtract),
                reads=["wt2", "wt1"], writes=["wt2"])
            dve(lambda e: e.tensor_scalar(wt2[:, 0:4], wt2[:, 0:4], EPS, None, ALU.add, ALU.bypass),
                reads=["wt2"], writes=["wt2"])
            act(lambda e: e.activation(wt2[:, 0:4], wt2[:, 0:4], AF.Sqrt), reads=["wt2"], writes=["wt2"])
            dve(lambda e: e.reciprocal(st3[:, 12:16], wt2[:, 0:4]), reads=["wt2"], writes=["st3b.rs"])
            dve(lambda e: e.tensor_tensor(st3[:, 8:12], st3[:, 8:12], st3[:, 12:16], ALU.mult),
                reads=["st3b.nm", "st3b.rs"], writes=["st3b.nm"])
            for m in range(4):
                mt, mres = mixt[m]
                tb = m % 2
                act(lambda e, m=m, mt=mt, tb=tb: e.activation(tt[tb][:, :], mt, AF.Identity,
                                                              scale=st3[:, 12 + m:13 + m], bias=st3[:, 8 + m:9 + m]),
                    reads=[mres, "st3b.rs", "st3b.nm"], writes=[f"tt{tb}"])
                dve(lambda e, m=m, tb=tb, xt=xtiles[m]: e.tensor_tensor(xt, xt, tt[tb][:, :], ALU.add),
                    reads=[xres[m], f"tt{tb}"], writes=[xres[m]])

        def ln2_batch(u):
            c, s, xres, xtiles = unit_tiles(u)
            for m in range(4):
                act(lambda e, m=m, xt=xtiles[m]: e.activation(junk[0][:, :], xt, AF.Square, accum_out=st2[:, 4 + m:5 + m]),
                    reads=[xres[m]], writes=["junk0", "st2.q"])
                dve(lambda e, m=m, xt=xtiles[m]: e.tensor_scalar(junk[1][:, :], xt, 1.0, None, ALU.mult, ALU.add, accum_out=st2[:, m:m + 1]),
                    reads=[xres[m]], writes=["junk1", "st2.s"])
            dve(lambda e: e.tensor_scalar(st2[:, 8:12], st2[:, 0:4], -1.0 / H, None, ALU.mult, ALU.bypass),
                reads=["st2.s"], writes=["st2.nm"])
            dve(lambda e: e.tensor_tensor(wt1[:, 0:4], st2[:, 8:12], st2[:, 8:12], ALU.mult),
                reads=["st2.nm"], writes=["wt1"])
            dve(lambda e: e.tensor_scalar(wt2[:, 0:4], st2[:, 4:8], 1.0 / H, None, ALU.mult, ALU.bypass),
                reads=["st2.q"], writes=["wt2"])
            dve(lambda e: e.tensor_tensor(wt2[:, 0:4], wt2[:, 0:4], wt1[:, 0:4], ALU.subtract),
                reads=["wt2", "wt1"], writes=["wt2"])
            dve(lambda e: e.tensor_scalar(wt2[:, 0:4], wt2[:, 0:4], EPS, None, ALU.add, ALU.bypass),
                reads=["wt2"], writes=["wt2"])
            act(lambda e: e.activation(wt2[:, 0:4], wt2[:, 0:4], AF.Sqrt), reads=["wt2"], writes=["wt2"])
            dve(lambda e: e.reciprocal(st2[:, 12:16], wt2[:, 0:4]), reads=["wt2"], writes=["st2.rs"])
            dve(lambda e: e.tensor_tensor(st2[:, 8:12], st2[:, 8:12], st2[:, 12:16], ALU.mult),
                reads=["st2.nm", "st2.rs"], writes=["st2.nm"])
            for m in range(4):
                act(lambda e, m=m, xt=xtiles[m]: e.activation(n2tm[m][:, :], xt, AF.Identity,
                                                              scale=st2[:, 12 + m:13 + m], bias=st2[:, 8 + m:9 + m]),
                    reads=[xres[m], "st2.rs", "st2.nm"], writes=[f"n2tm{m}"])

        def t3_one(u, m):
            nb = m
            bk = m % 2
            for k in range(8):
                pe(lambda e, k=k, nb=nb, bk=bk: e.transpose(pt[bk][:, k * 128:(k + 1) * 128],
                                                     n2tm[nb][:, k * 128:(k + 1) * 128], id16[:, :]),
                    reads=[f"n2tm{nb}", "id16"], writes=[PT[bk]])
            outap = n2fm.ap().rearrange("q (k j) -> q k j", k=8)[:, :, m * 128:(m + 1) * 128]
            dve(lambda e, bk=bk, outap=outap: e.tensor_copy(outap, pt[bk][:, :]),
                reads=[PT[bk]], writes=["n2fm"])

        def mixer_back(u, next_u):
            mixt = mixes(u)
            cmm_one(u, 0)
            cmm_one(u, 1)
            mix_one(u, 0, mixt)
            cmm_one(u, 2)
            mix_one(u, 1, mixt)
            cmm_one(u, 3)
            mix_one(u, 2, mixt)
            mix_one(u, 3, mixt)
            ln3_batch(u, mixt)
            ln2_batch(u)
            if next_u is not None:
                t1(next_u)
            t3_one(u, 0)
            t3_one(u, 1)
            t3_one(u, 2)
            t3_one(u, 3)

        wload_state = {"next": 0}
        NPIECE_TOTAL = None

        def emit_piece_load(g):
            if NPIECE_TOTAL is not None and g >= NPIECE_TOTAL[0]:
                return
            piece = g % 8
            wr = g % 2
            dma("sp", lambda e, piece=piece, wr=wr: e.dma_start(
                out=w1s[wr][:, :], in_=s_w1t[piece, :, :]),
                reads=["s_w1t"], writes=[f"w1s{wr}"], sem=f"dw1{wr}")
            dma("sp", lambda e, piece=piece, wr=wr: e.dma_start(
                out=w2s[wr][:, :], in_=s_w2t[piece, :, :]),
                reads=["s_w2t"], writes=[f"w2s{wr}"], sem=f"dw2{wr}")

        def ffn(u):
            c, s, xres, xtiles = unit_tiles(u)
            for hf in range(2):
                gbase = (u * 2 + hf) * 8

                def ffn1(hb):
                    piece, hbq, bk = hb // 4, hb % 4, hb % 2
                    wr = (gbase + piece) % 2
                    for k in range(8):
                        pe(lambda e, k=k, bk=bk, wr=wr, hbq=hbq, hf=hf: e.matmul(
                            pb[bk][:, 0:256],
                            w1s[wr][:, k * 512 + hbq * 128: k * 512 + hbq * 128 + 128],
                            n2fm[:, k * 512 + hf * 256: k * 512 + hf * 256 + 256],
                            start=(k == 0), stop=(k == 7)),
                            reads=[f"w1s{wr}", "n2fm"], writes=[PB[bk]])
                    gs = hb % 4
                    act(lambda e, hb=hb, bk=bk, gs=gs: e.activation(
                        gel[gs][:, :], pb[bk][:, 0:256], AF.Gelu, bias=b1col[:, hb:hb + 1],
                        scale=a1col[:, hb:hb + 1]),
                        reads=[PB[bk], "b1col", "a1col"], writes=[f"gel{gs}"])

                def ffn2(hb):
                    piece, hbq, gs = hb // 4, hb % 4, hb % 4
                    wr = (gbase + piece) % 2
                    for mg in range(2):
                        for ng in range(2):
                            pe(lambda e, hb=hb, mg=mg, ng=ng, wr=wr, gs=gs, hbq=hbq: e.matmul(
                                pcm[mg][:, ng * 512:(ng + 1) * 512],
                                gel[gs][:, mg * 128:(mg + 1) * 128],
                                w2s[wr][:, hbq * 1024 + ng * 512: hbq * 1024 + (ng + 1) * 512],
                                start=(hb == 0), stop=(hb == 31), skip_group_check=True),
                                reads=[f"gel{gs}", f"w2s{wr}"], writes=[PCM[mg][ng]])

                ffn1(0)
                for hb in range(1, 32):
                    ffn1(hb)
                    ffn2(hb - 1)
                    if (hb - 1) % 4 == 3:
                        emit_piece_load(gbase + (hb - 1) // 4 + 2)
                ffn2(31)
                emit_piece_load(gbase + 9)
                for mg in range(2):
                    m = hf * 2 + mg
                    dve(lambda e, mg=mg: e.tensor_tensor(mo[mg][:, 0:H], pcm[mg][:, :], b2rep[:, :], ALU.add),
                        reads=[PCM[mg][0], PCM[mg][1], "b2rep"], writes=[f"mo{mg}"])
                    ln_tile_stats(mo[mg][:, 0:H], f"mo{mg}", mg)
                    act(lambda e, mg=mg: e.activation(tt[mg][:, :], mo[mg][:, 0:H], AF.Identity, scale=st3[:, 19:20], bias=st3[:, 18:19]),
                        reads=[f"mo{mg}", "st3.rs", "st3.nm"], writes=[f"tt{mg}"])
                    dve(lambda e, m=m, mg=mg, xt=xtiles[m]: e.tensor_tensor(outb[mg][:, :], xt, tt[mg][:, :], ALU.add),
                        reads=[xres[m], f"tt{mg}"], writes=[f"outb{mg}"])
                    tok_tile = c * 4 + hf * 2 + mg
                    dma("gp", lambda e, s=s, tok_tile=tok_tile, mg=mg: e.dma_start(
                        out=d_out[s].rearrange("(a p) d -> p a d", p=128)[:, tok_tile, :],
                        in_=outb[mg][:, :]),
                        reads=[f"outb{mg}"], writes=[], sem=f"do{mg}")

        NPIECE_TOTAL = [NU * 16]
        emit_piece_load(0)
        emit_piece_load(1)
        fe1a(0)
        fe1b(0)
        t1(0)
        scan_pre(0)
        bu_scan(0)
        carry_update(0)
        emit_xloads(4)
        if NU > 1:
            fe1a(1)
        for u in range(NU):
            dterms(u)
            if u + 1 < NU:
                fe1b(u + 1)
            mixer_back(u, u + 1 if u + 1 < NU else None)
            if u + 1 < NU:
                scan_pre(u + 1)
                bu_scan(u + 1)
                carry_update(u + 1)
            ffn(u)
            emit_xloads(2 * u + 6)
            if u + 2 < NU:
                fe1a(u + 2)

        last_do = [B.dma_sem_last.get("do0"), B.dma_sem_last.get("do1")]
        fin = B.emit("sp", lambda e: e.nop(), reads=(), writes=())
        for d in last_do:
            if d is not None:
                fin.extra.append(d)

        B.finalize_counts()
        sems = {}
        for s_ in sorted(B.sem_names):
            sems[s_] = es.enter_context(nc.semaphore(s_))
        block = es.enter_context(nc.Block())
        block.tensor(lambda e: B.run("pe", e, sems))
        block.vector(lambda e: B.run("dve", e, sems))
        block.scalar(lambda e: B.run("act", e, sems))
        block.sync(lambda e: B.run("sp", e, sems))
        block.gpsimd(lambda e: B.run("gp", e, sems))
        nc._B = B
    return nc


def kernel(L0_attn_input_sequence, L0_attn_steps_raw, L0_attn_G_raw, L0_attn_A_raw,
           L0_attn_B_mat, L0_attn_C_mat, L0_attn_D_vec,
           L0_ffn_w1, L0_ffn_b1, L0_ffn_w2, L0_ffn_b2):
    x = np.ascontiguousarray(np.asarray(L0_attn_input_sequence, dtype=np.float32))
    n_cores = 8
    params = {
        "steps_raw": np.ascontiguousarray(np.asarray(L0_attn_steps_raw, dtype=np.float32)),
        "G_raw": np.ascontiguousarray(np.asarray(L0_attn_G_raw, dtype=np.float32)),
        "A_raw": np.ascontiguousarray(np.asarray(L0_attn_A_raw, dtype=np.float32)),
        "B_mat": np.ascontiguousarray(np.asarray(L0_attn_B_mat, dtype=np.float32)),
        "C_mat": np.ascontiguousarray(np.asarray(L0_attn_C_mat, dtype=np.float32)),
        "D_vec": np.ascontiguousarray(np.asarray(L0_attn_D_vec, dtype=np.float32)),
        "w1": np.ascontiguousarray(np.asarray(L0_ffn_w1, dtype=np.float32)),
        "b1": np.ascontiguousarray(np.asarray(L0_ffn_b1, dtype=np.float32)),
        "w2": np.ascontiguousarray(np.asarray(L0_ffn_w2, dtype=np.float32)),
        "b2": np.ascontiguousarray(np.asarray(L0_ffn_b2, dtype=np.float32)),
        "ident": np.eye(128, dtype=np.float32),
    }
    nc = bass.Bass("TRN2", target_bir_lowering=False, debug=False)
    build_program(nc)
    in_maps = [dict(x=x[2 * i:2 * i + 2], **params) for i in range(n_cores)]
    res = run_bass_kernel_spmd(nc, in_maps, core_ids=list(range(n_cores)))
    return np.concatenate([np.asarray(r["out"], dtype=np.float32) for r in res.results], axis=0)


# version 26
# speedup vs baseline: 1.0788x; 1.0047x over previous
import numpy as np
from contextlib import ExitStack

import concourse.bass as bass
import concourse.mybir as mybir
from concourse.bass_utils import run_bass_kernel_spmd

F32 = mybir.dt.float32
BF16 = mybir.dt.bfloat16
AF = mybir.ActivationFunctionType
ALU = mybir.AluOpType
AX = mybir.AxisListType

H = 1024
P = 256
HF = 4096
T = 512
EPS = 1e-5

QUEUES = ("pe", "dve", "act", "sp", "gp")


class Op:
    __slots__ = ("fn", "reads", "writes", "sem", "need_inc", "inc_by", "count", "extra")

    def __init__(self, fn, reads, writes, sem, inc_by, extra):
        self.fn = fn
        self.reads = reads
        self.writes = writes
        self.sem = sem
        self.need_inc = inc_by > 1
        self.inc_by = inc_by
        self.count = None
        self.extra = extra


class Build:
    def __init__(self):
        self.ops = {q: [] for q in QUEUES}
        self.last_writer = {}
        self.readers = {}
        self.dma_sem_last = {}
        self.sem_names = set(QUEUES)

    def emit(self, q, fn, reads=(), writes=(), sem=None, inc_by=1, extra=()):
        if sem is None:
            sem = q
        self.sem_names.add(sem)
        op = Op(fn, tuple(reads), tuple(writes), sem, inc_by, list(extra))
        deps = []
        for r in op.reads:
            w = self.last_writer.get(r)
            if w is not None:
                deps.append(w)
        for r in op.writes:
            w = self.last_writer.get(r)
            if w is not None:
                deps.append(w)
            for rd in self.readers.get(r, {}).values():
                deps.append(rd)
        for d in deps:
            if d is op:
                continue
            if d.sem == "pe" and q == "pe" and sem == "pe":
                continue
            op.extra.append(d)
        if inc_by > 1:
            prev = self.dma_sem_last.get(sem)
            if prev is not None:
                op.extra.append(prev)
            self.dma_sem_last[sem] = op
        self.ops[q].append(op)
        for r in op.writes:
            self.last_writer[r] = op
            self.readers[r] = {}
        for r in op.reads:
            self.readers.setdefault(r, {})[op.sem] = op
        return op

    def finalize_counts(self):
        for q in QUEUES:
            for op in self.ops[q]:
                for d in op.extra:
                    if isinstance(d, Op):
                        d.need_inc = True
        cnt = {s: 0 for s in self.sem_names}
        for q in QUEUES:
            for op in self.ops[q]:
                if op.need_inc:
                    cnt[op.sem] += op.inc_by
                    op.count = cnt[op.sem]

    def run(self, q, eng, sems):
        waited = {}
        for op in self.ops[q]:
            for d in op.extra:
                if isinstance(d, Op):
                    if d.count is None:
                        continue
                    key, val = d.sem, d.count
                else:
                    key, val = d
                if waited.get(key, -1) >= val:
                    continue
                eng.wait_ge(sems[key], val)
                waited[key] = val
            inst = op.fn(eng)
            if op.need_inc and op.count is not None:
                inst.then_inc(sems[op.sem], op.inc_by)


def build_program(nc: bass.Bass, L=4096, SEQS=2):
    NCH = L // T
    d_x = nc.dram_tensor("x", [SEQS, L, H], F32, kind="ExternalInput").ap()
    d_steps = nc.dram_tensor("steps_raw", [P], F32, kind="ExternalInput").ap()
    d_G = nc.dram_tensor("G_raw", [P], F32, kind="ExternalInput").ap()
    d_A = nc.dram_tensor("A_raw", [P], F32, kind="ExternalInput").ap()
    d_B = nc.dram_tensor("B_mat", [P, H, 2], F32, kind="ExternalInput").ap()
    d_C = nc.dram_tensor("C_mat", [H, P, 2], F32, kind="ExternalInput").ap()
    d_D = nc.dram_tensor("D_vec", [H], F32, kind="ExternalInput").ap()
    d_w1 = nc.dram_tensor("w1", [HF, H], F32, kind="ExternalInput").ap()
    d_b1 = nc.dram_tensor("b1", [HF], F32, kind="ExternalInput").ap()
    d_w2 = nc.dram_tensor("w2", [H, HF], F32, kind="ExternalInput").ap()
    d_b2 = nc.dram_tensor("b2", [H], F32, kind="ExternalInput").ap()
    d_id = nc.dram_tensor("ident", [128, 128], F32, kind="ExternalInput").ap()
    d_out = nc.dram_tensor("out", [SEQS, L, H], F32, kind="ExternalOutput").ap()
    s_w1b = nc.dram_tensor("scr_w1b", [HF, H], BF16, kind="Internal").ap()
    s_w2b = nc.dram_tensor("scr_w2b", [H, HF], BF16, kind="Internal").ap()
    s_w1t = nc.dram_tensor("scr_w1t", [8, 128, HF], BF16, kind="Internal").ap()
    s_w2t = nc.dram_tensor("scr_w2t", [8, 128, 4 * H], BF16, kind="Internal").ap()

    es = ExitStack()
    with es:
        def sb(name, shape, dt):
            return es.enter_context(nc.sbuf_tensor(name, shape, dt))

        cB = sb("cB", [128, 8 * 512], BF16)
        cC = sb("cC", [128, 4 * H], BF16)
        g_a = sb("g_a", [128, 2 * T], F32)
        g_b = sb("g_b", [128, 2 * T], F32)
        g_G1 = sb("g_G1", [128, 2 * T], F32)
        g_G2 = sb("g_G2", [128, 2 * T], F32)
        drep = sb("drep", [128, H], BF16)
        b2rep = sb("b2rep", [128, H], F32)
        b1col = sb("b1col", [128, 32], F32)
        idf = sb("idf", [128, 128], F32)
        id16 = sb("id16", [128, 128], BF16)
        cc = {n: sb("cc_" + n, [128, 4], F32)
              for n in ("q2a", "q2b", "q3a", "q3b", "R1b", "R2a", "R2b",
                        "i11", "i12", "i21", "i22", "g1a", "g1b", "cAT", "cBT",
                        "G1e", "G2e")}
        P1s = [sb(f"P1_{s}", [128, 4], F32) for s in range(SEQS)]
        P2s = [sb(f"P2_{s}", [128, 4], F32) for s in range(SEQS)]
        dlc = sb("dlc", [128, 4], F32)
        dls = sb("dls", [128, 4], F32)
        wq2 = sb("wq2", [128, 4], F32)
        wq3 = sb("wq3", [128, 4], F32)
        wR1 = sb("wR1", [128, 4], F32)
        wR2 = sb("wR2", [128, 4], F32)
        wt1 = sb("wt1", [128, 4], F32)
        wt2 = sb("wt2", [128, 4], F32)
        pce = sb("pce", [128, 4], F32)
        pse = sb("pse", [128, 4], F32)
        pv = sb("pv", [128, 64], F32)
        st1 = sb("st1", [128, 16], F32)
        st2 = sb("st2", [128, 16], F32)
        st3 = sb("st3", [128, 24], F32)
        prow = sb("prow", [8, 128], F32)
        b1row = sb("b1row", [32, 128], F32)
        dr0 = sb("dr0", [1, H], F32)
        dr016 = sb("dr016", [1, H], BF16)
        b2r0 = sb("b2r0", [1, H], F32)
        ones16 = sb("ones16", [1, 128], BF16)
        onesf = sb("onesf", [1, 128], F32)
        sx = [sb(f"sx{r}", [128, 2 * H], F32) for r in range(4)]
        w1s = [sb(f"w1s{r}", [128, 8 * 512], BF16) for r in range(2)]
        w2s = [sb(f"w2s{r}", [128, 4 * H], BF16) for r in range(2)]
        outb = [sb(f"outb{r}", [128, H], F32) for r in range(2)]
        n1tm = sb("n1tm", [128, 4 * H], BF16)
        n1fm = sb("n1fm", [128, 8 * 512], BF16)
        n2fm = sb("n2fm", [128, 8 * 512], BF16)
        n2tm = [sb(f"n2tm{r}", [128, H], BF16) for r in range(4)]
        junk = [sb(f"junk{r}", [128, H], BF16) for r in range(2)]
        u1 = [sb(f"u1_{r}", [128, T], F32) for r in range(4)]
        u2 = [sb(f"u2_{r}", [128, T], F32) for r in range(4)]
        dtm = sb("dtm", [128, 4 * H], BF16)
        a1col = sb("a1col", [128, 32], F32)
        jdve = junk_placeholder = None
        ys = sb("ys", [128, 4 * T], BF16)
        mo = [sb(f"mo{r}", [128, 1056], F32) for r in range(2)]
        tt = [sb(f"tt{r}", [128, H], F32) for r in range(2)]
        gel = [sb(f"gel{r}", [128, 256], BF16) for r in range(4)]
        tabA = mo[0]
        tabB = mo[1]
        scw = u1[0]
        scw2 = u2[0]

        pb = [es.enter_context(nc.psum_tensor("pb0", [128, 512], F32)),
              es.enter_context(nc.psum_tensor("pb1", [128, 512], F32))]
        pcm = [es.enter_context(nc.psum_tensor("pcA", [128, 1024], F32)),
               es.enter_context(nc.psum_tensor("pcB", [128, 1024], F32))]
        pt = [es.enter_context(nc.psum_tensor("pt6", [128, 1024], BF16)),
              es.enter_context(nc.psum_tensor("pt7", [128, 1024], BF16))]

        B = Build()

        def pe(fn, reads=(), writes=()):
            return B.emit("pe", fn, reads, writes)

        def dve(fn, reads=(), writes=()):
            return B.emit("dve", fn, reads, writes)

        def act(fn, reads=(), writes=()):
            return B.emit("act", fn, reads, writes)

        def dma(q, fn, reads=(), writes=(), sem=None):
            return B.emit(q, fn, reads, writes, sem=sem, inc_by=16)

        PB = ["pb0", "pb1"]
        PCM = [["pcA0", "pcA1"], ["pcB0", "pcB1"]]
        PT = ["pt6", "pt7"]

        dma("sp", lambda e: e.dma_start(out=idf[:, :], in_=d_id), writes=["idf"], sem="dp")
        dma("sp", lambda e: e.dma_start(out=prow[0:2, :], in_=d_steps.rearrange("(h p) -> h p", p=128)),
            writes=["prow"], sem="dp")
        dma("sp", lambda e: e.dma_start(out=prow[2:4, :], in_=d_G.rearrange("(h p) -> h p", p=128)),
            writes=["prow"], sem="dp")
        dma("sp", lambda e: e.dma_start(out=prow[4:6, :], in_=d_A.rearrange("(h p) -> h p", p=128)),
            writes=["prow"], sem="dp")
        dma("sp", lambda e: e.dma_start(out=b1row[:, :], in_=d_b1.rearrange("(a p) -> a p", p=128)),
            writes=["b1row"], sem="dp")
        dma("sp", lambda e: e.dma_start(out=dr0[:, :], in_=d_D[None, :]), writes=["dr0"], sem="dp")
        dma("sp", lambda e: e.dma_start(out=b2r0[:, :], in_=d_b2[None, :]), writes=["b2r0"], sem="dp")
        dma("sp", lambda e: e.dma_start(out=sx[0][:, :], in_=d_B[0:128, :, :].rearrange("p h t -> p (h t)")),
            writes=["sx0.0", "sx0.1"], sem="dp")
        dma("sp", lambda e: e.dma_start(out=sx[1][:, :], in_=d_B[128:256, :, :].rearrange("p h t -> p (h t)")),
            writes=["sx1.0", "sx1.1"], sem="dp")

        dve(lambda e: e.tensor_copy(id16[:, :], idf[:, :]), reads=["idf"], writes=["id16"])
        dve(lambda e: e.memset(ones16[:, :], 1.0), writes=["ones16"])
        dve(lambda e: e.memset(onesf[:, :], 1.0), writes=["onesf"])
        for s in range(SEQS):
            dve(lambda e, s=s: e.memset(P1s[s][:, :], 0.0), writes=[f"P1.{s}"])
            dve(lambda e, s=s: e.memset(P2s[s][:, :], 0.0), writes=[f"P2.{s}"])
        dve(lambda e: e.tensor_copy(dr016[:, :], dr0[:, :]), reads=["dr0"], writes=["dr016"])

        pe(lambda e: e.transpose(pcm[0][:, 0:6], prow[0:6, :], idf[0:6, 0:6]),
           reads=["prow", "idf"], writes=[PCM[0][0]])
        dve(lambda e: e.tensor_copy(pv[:, 0:6], pcm[0][:, 0:6]), reads=[PCM[0][0]], writes=["pv"])
        pe(lambda e: e.transpose(pcm[1][:, 0:32], b1row[:, :], idf[0:32, 0:32]),
           reads=["b1row", "idf"], writes=[PCM[1][0]])
        dve(lambda e: e.tensor_copy(b1col[:, 0:32], pcm[1][:, 0:32]), reads=[PCM[1][0]], writes=["b1col"])

        pe(lambda e: e.matmul(pcm[1][:, 0:512], ones16[0:1, :], dr016[0:1, 0:512], start=True, stop=True),
           reads=["ones16", "dr016"], writes=[PCM[1][0]])
        pe(lambda e: e.matmul(pcm[1][:, 512:1024], ones16[0:1, :], dr016[0:1, 512:1024], start=True, stop=True),
           reads=["ones16", "dr016"], writes=[PCM[1][1]])
        dve(lambda e: e.tensor_copy(drep[:, :], pcm[1][:, :]), reads=[PCM[1][0], PCM[1][1]], writes=["drep"])
        pe(lambda e: e.matmul(pcm[0][:, 0:512], onesf[0:1, :], b2r0[0:1, 0:512], start=True, stop=True),
           reads=["onesf", "b2r0"], writes=[PCM[0][0]])
        pe(lambda e: e.matmul(pcm[0][:, 512:1024], onesf[0:1, :], b2r0[0:1, 512:1024], start=True, stop=True),
           reads=["onesf", "b2r0"], writes=[PCM[0][1]])
        dve(lambda e: e.tensor_copy(b2rep[:, :], pcm[0][:, :]), reads=[PCM[0][0], PCM[0][1]], writes=["b2rep"])

        PV = {}
        _n = [6]
        for nme in ("steps", "S", "r2", "r", "rinv", "alpha", "tau", "cth", "s2th",
                    "w2cr", "w1cr", "fscale", "det", "tmpa", "tmpb", "rstep",
                    "Cs", "Bs", "Bss"):
            PV[nme] = slice(_n[0], _n[0] + 2)
            _n[0] += 2
        RAW = {"steps_raw": slice(0, 2), "G_raw": slice(2, 4), "A_raw": slice(4, 6)}

        def pva(nme):
            return pv[:, PV[nme]] if nme in PV else pv[:, RAW[nme]]

        act(lambda e: e.activation(pva("steps"), pva("steps_raw"), AF.Sigmoid), reads=["pv"], writes=["pv"])
        dve(lambda e: e.tensor_relu(pva("tmpa"), pva("G_raw")), reads=["pv"], writes=["pv"])
        dve(lambda e: e.tensor_tensor(pva("S"), pva("tmpa"), pva("steps"), ALU.mult), reads=["pv"], writes=["pv"])
        dve(lambda e: e.tensor_scalar(pva("S"), pva("S"), 1.0, None, ALU.add, ALU.bypass), reads=["pv"], writes=["pv"])
        dve(lambda e: e.reciprocal(pva("r2"), pva("S")), reads=["pv"], writes=["pv"])
        act(lambda e: e.activation(pva("r"), pva("r2"), AF.Sqrt), reads=["pv"], writes=["pv"])
        dve(lambda e: e.reciprocal(pva("rinv"), pva("r")), reads=["pv"], writes=["pv"])
        dve(lambda e: e.tensor_relu(pva("tmpb"), pva("A_raw")), reads=["pv"], writes=["pv"])
        dve(lambda e: e.tensor_tensor(pva("tmpb"), pva("tmpb"), pva("steps"), ALU.mult), reads=["pv"], writes=["pv"])
        dve(lambda e: e.tensor_tensor(pva("tmpb"), pva("tmpb"), pva("steps"), ALU.mult), reads=["pv"], writes=["pv"])
        dve(lambda e: e.tensor_tensor(pva("tmpb"), pva("tmpb"), pva("r2"), ALU.mult), reads=["pv"], writes=["pv"])
        act(lambda e: e.activation(pva("alpha"), pva("tmpb"), AF.Tanh, scale=1.0 / 1.99), reads=["pv"], writes=["pv"])
        dve(lambda e: e.tensor_scalar(pva("alpha"), pva("alpha"), 1.99, None, ALU.mult, ALU.bypass), reads=["pv"], writes=["pv"])
        dve(lambda e: e.tensor_tensor(pva("tau"), pva("r2"), pva("alpha"), ALU.subtract), reads=["pv"], writes=["pv"])
        dve(lambda e: e.tensor_scalar(pva("tau"), pva("tau"), 1.0, None, ALU.add, ALU.bypass), reads=["pv"], writes=["pv"])
        dve(lambda e: e.tensor_tensor(pva("cth"), pva("tau"), pva("rinv"), ALU.mult), reads=["pv"], writes=["pv"])
        dve(lambda e: e.tensor_scalar(pva("cth"), pva("cth"), 0.5, 1.0, ALU.mult, ALU.min), reads=["pv"], writes=["pv"])
        dve(lambda e: e.tensor_scalar(pva("cth"), pva("cth"), -1.0, None, ALU.max, ALU.bypass), reads=["pv"], writes=["pv"])
        dve(lambda e: e.tensor_tensor(pva("s2th"), pva("cth"), pva("cth"), ALU.mult), reads=["pv"], writes=["pv"])
        dve(lambda e: e.tensor_scalar(pva("s2th"), pva("s2th"), -1.0, 1.0, ALU.mult, ALU.add), reads=["pv"], writes=["pv"])
        dve(lambda e: e.tensor_tensor(pva("tmpa"), pva("r"), pva("cth"), ALU.mult), reads=["pv"], writes=["pv"])
        dve(lambda e: e.tensor_tensor(pva("w2cr"), pva("tau"), pva("tmpa"), ALU.subtract), reads=["pv"], writes=["pv"])
        dve(lambda e: e.tensor_tensor(pva("w2cr"), pva("w2cr"), pva("steps"), ALU.mult), reads=["pv"], writes=["pv"])
        dve(lambda e: e.tensor_tensor(pva("w2cr"), pva("w2cr"), pva("rinv"), ALU.mult), reads=["pv"], writes=["pv"])
        dve(lambda e: e.tensor_tensor(pva("w1cr"), pva("r2"), pva("tmpa"), ALU.subtract), reads=["pv"], writes=["pv"])
        dve(lambda e: e.tensor_tensor(pva("w1cr"), pva("w1cr"), pva("alpha"), ALU.subtract), reads=["pv"], writes=["pv"])
        dve(lambda e: e.tensor_tensor(pva("w1cr"), pva("w1cr"), pva("rinv"), ALU.mult), reads=["pv"], writes=["pv"])
        dve(lambda e: e.tensor_tensor(pva("fscale"), pva("steps"), pva("r2"), ALU.mult), reads=["pv"], writes=["pv"])
        dve(lambda e: e.tensor_tensor(pva("tmpb"), pva("steps"), pva("steps"), ALU.mult), reads=["pv"], writes=["pv"])
        dve(lambda e: e.tensor_tensor(pva("tmpb"), pva("tmpb"), pva("s2th"), ALU.mult), reads=["pv"], writes=["pv"])
        dve(lambda e: e.tensor_tensor(pva("det"), pva("w2cr"), pva("w2cr"), ALU.mult), reads=["pv"], writes=["pv"])
        dve(lambda e: e.tensor_tensor(pva("det"), pva("det"), pva("tmpb"), ALU.add), reads=["pv"], writes=["pv"])
        dve(lambda e: e.reciprocal(pva("det"), pva("det")), reads=["pv"], writes=["pv"])

        def dup4(dstname, dst, src):
            dve(lambda e: e.tensor_copy(dst[:, 0:2], src), reads=["pv"], writes=[dstname])
            dve(lambda e: e.tensor_copy(dst[:, 2:4], src), reads=["pv"], writes=[dstname])

        dup4("cc.q2a", cc["q2a"], pva("fscale"))
        dve(lambda e: e.tensor_tensor(pva("tmpb"), pva("alpha"), pva("tmpa"), ALU.add), reads=["pv"], writes=["pv"])
        dve(lambda e: e.tensor_scalar(pva("tmpb"), pva("tmpb"), -1.0, 1.0, ALU.mult, ALU.add), reads=["pv"], writes=["pv"])
        dup4("cc.q2b", cc["q2b"], pva("tmpb"))
        dve(lambda e: e.tensor_tensor(pva("tmpb"), pva("r2"), pva("tmpa"), ALU.subtract), reads=["pv"], writes=["pv"])
        dup4("cc.q3a", cc["q3a"], pva("tmpb"))
        dve(lambda e: e.reciprocal(pva("rstep"), pva("steps")), reads=["pv"], writes=["pv"])
        dve(lambda e: e.tensor_tensor(pva("tmpb"), pva("alpha"), pva("rstep"), ALU.mult), reads=["pv"], writes=["pv"])
        dve(lambda e: e.tensor_scalar(pva("tmpb"), pva("tmpb"), -1.0, None, ALU.mult, ALU.bypass), reads=["pv"], writes=["pv"])
        dup4("cc.q3b", cc["q3b"], pva("tmpb"))
        dup4("cc.R1b", cc["R1b"], pva("tmpa"))
        dve(lambda e: e.tensor_tensor(pva("tmpb"), pva("r"), pva("s2th"), ALU.mult), reads=["pv"], writes=["pv"])
        dve(lambda e: e.tensor_scalar(pva("tmpb"), pva("tmpb"), -1.0, None, ALU.mult, ALU.bypass), reads=["pv"], writes=["pv"])
        dup4("cc.R2a", cc["R2a"], pva("tmpb"))
        dup4("cc.R2b", cc["R2b"], pva("cth"))
        dve(lambda e: e.tensor_tensor(pva("tmpb"), pva("steps"), pva("s2th"), ALU.mult), reads=["pv"], writes=["pv"])
        dve(lambda e: e.tensor_tensor(pva("tmpb"), pva("tmpb"), pva("det"), ALU.mult), reads=["pv"], writes=["pv"])
        dup4("cc.i11", cc["i11"], pva("tmpb"))
        dve(lambda e: e.tensor_tensor(pva("tmpb"), pva("w2cr"), pva("det"), ALU.mult), reads=["pv"], writes=["pv"])
        dup4("cc.i12", cc["i12"], pva("tmpb"))
        dve(lambda e: e.tensor_scalar(pva("tmpb"), pva("tmpb"), -1.0, None, ALU.mult, ALU.bypass), reads=["pv"], writes=["pv"])
        dup4("cc.i21", cc["i21"], pva("tmpb"))
        dve(lambda e: e.tensor_tensor(pva("tmpb"), pva("steps"), pva("det"), ALU.mult), reads=["pv"], writes=["pv"])
        dup4("cc.i22", cc["i22"], pva("tmpb"))

        wbuf = [(outb[0][:, :], "outb0"), (outb[1][:, :], "outb1"), (tt[0][:, :], "tt0"), (tt[1][:, :], "tt1")]
        for chunk in range(8):
            for blk in range(4):
                buf, bres = wbuf[blk]
                dma("sp", lambda e, chunk=chunk, blk=blk, buf=buf: e.dma_start(
                    out=buf, in_=d_w1[chunk * 512 + blk * 128: chunk * 512 + (blk + 1) * 128, :]),
                    writes=[bres], sem=f"dwl{blk}")
            for blk in range(4):
                buf, bres = wbuf[blk]
                hb = chunk * 4 + blk
                act(lambda e, buf=buf, hb=hb, blk=blk:
                    e.activation(junk[blk % 2][:, :], buf, AF.Abs, accum_out=a1col[:, hb:hb + 1]),
                    reads=[bres], writes=[f"junk{blk % 2}", "a1col"])
                act(lambda e, buf=buf, blk=blk:
                    e.activation(n1tm[:, blk * 1024:(blk + 1) * 1024], buf, AF.Sign),
                    reads=[bres], writes=[f"n1tm.{blk}"])
            dma("gp", lambda e, chunk=chunk: e.dma_start(
                out=s_w1b[chunk * 512:(chunk + 1) * 512, :].rearrange("(a p) d -> p a d", p=128),
                in_=n1tm[:, :].rearrange("p (a d) -> p a d", a=4)),
                reads=["n1tm.0", "n1tm.1", "n1tm.2", "n1tm.3"], writes=["s_w1b"], sem="dws")
        dve(lambda e: e.tensor_scalar(a1col[:, :], a1col[:, :], 1.0 / H, None, ALU.mult, ALU.bypass),
            reads=["a1col"], writes=["a1col"])
        for chunk in range(8):
            for q4 in range(4):
                buf, bres = wbuf[q4]
                dma("sp", lambda e, chunk=chunk, q4=q4, buf=buf: e.dma_start(
                    out=buf, in_=d_w2[chunk * 128:(chunk + 1) * 128, q4 * 1024:(q4 + 1) * 1024]),
                    writes=[bres], sem=f"dwl{q4}")
            for q4 in range(4):
                buf, bres = wbuf[q4]
                acc = st3[:, 4 + q4:5 + q4]
                act(lambda e, q4=q4, buf=buf, acc=acc:
                    e.activation(junk[q4 % 2][:, :], buf, AF.Abs, accum_out=acc),
                    reads=[bres], writes=[f"junk{q4 % 2}", "st3.q"])
                act(lambda e, q4=q4, buf=buf:
                    e.activation(n1tm[:, q4 * 1024:(q4 + 1) * 1024], buf, AF.Sign),
                    reads=[bres], writes=[f"n1tm.{q4}"])
            dve(lambda e: e.tensor_tensor(st3[:, 4:5], st3[:, 4:5], st3[:, 5:6], ALU.add), reads=["st3.q"], writes=["st3.q"])
            dve(lambda e: e.tensor_tensor(st3[:, 6:7], st3[:, 6:7], st3[:, 7:8], ALU.add), reads=["st3.q"], writes=["st3.q"])
            dve(lambda e: e.tensor_tensor(st3[:, 4:5], st3[:, 4:5], st3[:, 6:7], ALU.add), reads=["st3.q"], writes=["st3.q"])
            dve(lambda e: e.tensor_scalar(st3[:, 4:5], st3[:, 4:5], 1.0 / HF, None, ALU.mult, ALU.bypass), reads=["st3.q"], writes=["st3.q"])
            dve(lambda e: e.tensor_scalar(n1tm[:, :], n1tm[:, :], st3[:, 4:5], None, ALU.mult, ALU.bypass),
                reads=["n1tm.0", "n1tm.1", "n1tm.2", "n1tm.3", "st3.q"], writes=["n1tm.0", "n1tm.1", "n1tm.2", "n1tm.3"])
            dma("gp", lambda e, chunk=chunk: e.dma_start(
                out=s_w2b[chunk * 128:(chunk + 1) * 128, :], in_=n1tm[:, :]),
                reads=["n1tm.0", "n1tm.1", "n1tm.2", "n1tm.3"], writes=["s_w2b"], sem="dws")
        trb1 = [("n1fm", n1fm), ("n2fm", n2fm)]
        for kb in range(8):
            nm, buf = trb1[kb % 2]
            dma("sp", lambda e, kb=kb, buf=buf: e.dma_start(
                out=buf[:, :], in_=s_w1b[:, kb * 128:(kb + 1) * 128], transpose=True),
                reads=["s_w1b"], writes=[nm], sem=f"dtr{kb % 2}")
            dma("gp", lambda e, kb=kb, buf=buf: e.dma_start(
                out=s_w1t[:, :, kb * 512:(kb + 1) * 512].rearrange("j p h -> p j h"),
                in_=buf[:, :].rearrange("p (j h) -> p j h", j=8)),
                reads=[nm], writes=["s_w1t"], sem=f"dwb{kb % 2}")
        trb2 = [(["n1fm"], n1fm[:, 0:1024]), (["n2fm"], n2fm[:, 0:1024]), (["n1tm.0"], n1tm[:, 0:1024])]
        for hb in range(32):
            nm, bap = trb2[hb % 3]
            dma("sp", lambda e, hb=hb, bap=bap: e.dma_start(
                out=bap, in_=s_w2b[:, hb * 128:(hb + 1) * 128], transpose=True),
                reads=["s_w2b"], writes=nm, sem=f"dtr{hb % 3}")
            dma("gp", lambda e, hb=hb, bap=bap: e.dma_start(
                out=s_w2t[hb // 4, :, (hb % 4) * 1024:(hb % 4 + 1) * 1024], in_=bap),
                reads=nm, writes=["s_w2t"], sem=f"dwb{hb % 3}")

        def emit_table(Cd, Bd, cresC, cresB, sres, sres2, colw, basen, ncols):
            for h in range(2):
                co = h * colw
                base = pva(basen)[:, h:h + 1]
                cthh = pv[:, PV["cth"].start + h: PV["cth"].start + h + 1]
                s2h = pv[:, PV["s2th"].start + h: PV["s2th"].start + h + 1]
                Bss = pv[:, PV["Bss"].start + h: PV["Bss"].start + h + 1]
                dve(lambda e, a=Cd[:, co:co + 1]: e.memset(a, 1.0), writes=[cresC])
                dve(lambda e, a=Bd[:, co:co + 1]: e.memset(a, 0.0), writes=[cresB])
                dve(lambda e, a=Cd[:, co + 1:co + 2], b=base, c=cthh: e.tensor_tensor(a, b, c, ALU.mult),
                    reads=["pv"], writes=[cresC])
                dve(lambda e, a=Bd[:, co + 1:co + 2], b=base: e.tensor_copy(a, b), reads=["pv"], writes=[cresB])
                w_ = 2
                while w_ < ncols:
                    n = min(w_ - 1, ncols - w_)
                    Csc = Cd[:, co + w_ - 1:co + w_]
                    Bsc = Bd[:, co + w_ - 1:co + w_]
                    dve(lambda e, Bss=Bss, Bsc=Bsc, s2h=s2h: e.tensor_tensor(Bss, Bsc, s2h, ALU.mult),
                        reads=[cresB, "pv"], writes=["pv"])
                    dve(lambda e, n=n, c0=co, Csc=Csc: e.tensor_scalar(scw[:, 0:n], Cd[:, c0 + 1:c0 + 1 + n], Csc, None, ALU.mult, ALU.bypass),
                        reads=[cresC], writes=[sres])
                    dve(lambda e, n=n, c0=co, Bsc=Bsc: e.tensor_scalar(scw2[:, 0:n], Cd[:, c0 + 1:c0 + 1 + n], Bsc, None, ALU.mult, ALU.bypass),
                        reads=[cresC, cresB], writes=[sres2])
                    dve(lambda e, n=n, c0=co, w_=w_, Bss=Bss: e.tensor_scalar(Cd[:, c0 + w_:c0 + w_ + n], Bd[:, c0 + 1:c0 + 1 + n], Bss, None, ALU.mult, ALU.bypass),
                        reads=[cresB, "pv"], writes=[cresC])
                    dve(lambda e, n=n, c0=co, w_=w_: e.tensor_tensor(Cd[:, c0 + w_:c0 + w_ + n], scw[:, 0:n], Cd[:, c0 + w_:c0 + w_ + n], ALU.subtract),
                        reads=[cresC, sres], writes=[cresC])
                    dve(lambda e, n=n, c0=co, w_=w_, Csc=Csc: e.tensor_scalar(Bd[:, c0 + w_:c0 + w_ + n], Bd[:, c0 + 1:c0 + 1 + n], Csc, None, ALU.mult, ALU.bypass),
                        reads=[cresB, cresC], writes=[cresB])
                    dve(lambda e, n=n, c0=co, w_=w_: e.tensor_tensor(Bd[:, c0 + w_:c0 + w_ + n], Bd[:, c0 + w_:c0 + w_ + n], scw2[:, 0:n], ALU.add),
                        reads=[cresB, sres2], writes=[cresB])
                    w_ = w_ + n

        emit_table(g_a, g_b, "ga", "gb", "u1.0", "u2.0", T, "rinv", T)
        for h in range(2):
            rc = pv[:, PV["r"].start + h: PV["r"].start + h + 1]
            ric = pv[:, PV["rinv"].start + h: PV["rinv"].start + h + 1]
            co = h * 513
            go = h * T
            dve(lambda e: e.memset(scw2[:, 0:512], 0.0), writes=["u2.0"])
            dve(lambda e, rc=rc: e.tensor_scalar(scw[:, 0:512], scw2[:, 0:512], rc, None, ALU.add, ALU.bypass),
                reads=["pv", "u2.0"], writes=["u1.0"])
            dve(lambda e, ric=ric: e.tensor_tensor_scan(scw[:, 0:512], scw[:, 0:512], scw2[:, 0:512], ric, ALU.mult, ALU.add),
                reads=["u1.0", "u2.0", "pv"], writes=["u1.0"])
            dve(lambda e, co=co, go=go: e.tensor_tensor(tabA[:, co:co + 512], g_a[:, go:go + 512], scw[:, 0:512], ALU.mult),
                reads=["ga", "u1.0"], writes=["mo0"])
            dve(lambda e, co=co: e.tensor_tensor(tabA[:, co:co + 512], tabA[:, co:co + 512], scw[:, 0:512], ALU.mult),
                reads=["mo0", "u1.0"], writes=["mo0"])
            dve(lambda e, co=co, go=go: e.tensor_tensor(tabB[:, co:co + 512], g_b[:, go:go + 512], scw[:, 0:512], ALU.mult),
                reads=["gb", "u1.0"], writes=["mo1"])
            dve(lambda e, co=co: e.tensor_tensor(tabB[:, co:co + 512], tabB[:, co:co + 512], scw[:, 0:512], ALU.mult),
                reads=["mo1", "u1.0"], writes=["mo1"])
            s2h = pv[:, PV["s2th"].start + h: PV["s2th"].start + h + 1]
            dve(lambda e, co=co: e.tensor_tensor(wt1[:, 0:1], tabB[:, co + 511:co + 512], tabB[:, co + 1:co + 2], ALU.mult),
                reads=["mo1"], writes=["wt1"])
            dve(lambda e, s2h=s2h: e.tensor_tensor(wt1[:, 0:1], wt1[:, 0:1], s2h, ALU.mult),
                reads=["wt1", "pv"], writes=["wt1"])
            dve(lambda e, co=co: e.tensor_tensor(tabA[:, co + 512:co + 513], tabA[:, co + 511:co + 512], tabA[:, co + 1:co + 2], ALU.mult),
                reads=["mo0"], writes=["mo0"])
            dve(lambda e, co=co: e.tensor_tensor(tabA[:, co + 512:co + 513], tabA[:, co + 512:co + 513], wt1[:, 0:1], ALU.subtract),
                reads=["mo0", "wt1"], writes=["mo0"])
            dve(lambda e, co=co: e.tensor_tensor(wt1[:, 0:1], tabB[:, co + 511:co + 512], tabA[:, co + 1:co + 2], ALU.mult),
                reads=["mo1", "mo0"], writes=["wt1"])
            dve(lambda e, co=co: e.tensor_tensor(wt2[:, 0:1], tabA[:, co + 511:co + 512], tabB[:, co + 1:co + 2], ALU.mult),
                reads=["mo0", "mo1"], writes=["wt2"])
            dve(lambda e, co=co: e.tensor_tensor(tabB[:, co + 512:co + 513], wt1[:, 0:1], wt2[:, 0:1], ALU.add),
                reads=["wt1", "wt2"], writes=["mo1"])

        for h in range(2):
            co = h * 513
            go = h * T
            stc = pv[:, PV["steps"].start + h: PV["steps"].start + h + 1]
            s2c = pv[:, PV["s2th"].start + h: PV["s2th"].start + h + 1]
            w2c_ = pv[:, PV["w2cr"].start + h: PV["w2cr"].start + h + 1]
            w1c_ = pv[:, PV["w1cr"].start + h: PV["w1cr"].start + h + 1]
            rin = pv[:, PV["rinv"].start + h: PV["rinv"].start + h + 1]
            dve(lambda e, co=co, go=go, stc=stc: e.tensor_scalar(g_G1[:, go:go + T], tabA[:, co:co + T], stc, None, ALU.mult, ALU.bypass),
                reads=["mo0", "pv"], writes=["gG1"])
            dve(lambda e, co=co, w2c_=w2c_: e.tensor_scalar(scw[:, 0:T], tabB[:, co:co + T], w2c_, None, ALU.mult, ALU.bypass),
                reads=["mo1", "pv"], writes=["u1.0"])
            dve(lambda e, go=go: e.tensor_tensor(g_G1[:, go:go + T], g_G1[:, go:go + T], scw[:, 0:T], ALU.add),
                reads=["gG1", "u1.0"], writes=["gG1"])
            dve(lambda e, co=co, go=go, stc=stc: e.tensor_scalar(g_G2[:, go:go + T], tabB[:, co:co + T], stc, None, ALU.mult, ALU.bypass),
                reads=["mo1", "pv"], writes=["gG2"])
            dve(lambda e, go=go, s2c=s2c: e.tensor_scalar(g_G2[:, go:go + T], g_G2[:, go:go + T], s2c, None, ALU.mult, ALU.bypass),
                reads=["gG2", "pv"], writes=["gG2"])
            dve(lambda e, co=co, w2c_=w2c_: e.tensor_scalar(scw[:, 0:T], tabA[:, co:co + T], w2c_, None, ALU.mult, ALU.bypass),
                reads=["mo0", "pv"], writes=["u1.0"])
            dve(lambda e, go=go: e.tensor_tensor(g_G2[:, go:go + T], g_G2[:, go:go + T], scw[:, 0:T], ALU.subtract),
                reads=["gG2", "u1.0"], writes=["gG2"])
            for off in (0, 2):
                col = h + off
                atm1 = tabA[:, co + T - 1:co + T]
                btm1 = tabB[:, co + T - 1:co + T]
                atT = tabA[:, co + T:co + T + 1]
                btT = tabB[:, co + T:co + T + 1]
                dve(lambda e, col=col, w1c_=w1c_, btm1=btm1: e.tensor_tensor(cc["g1a"][:, col:col + 1], w1c_, btm1, ALU.mult),
                    reads=["mo1", "pv"], writes=["cc.g1a"])
                dve(lambda e, col=col, atm1=atm1: e.tensor_tensor(cc["g1a"][:, col:col + 1], cc["g1a"][:, col:col + 1], atm1, ALU.add),
                    reads=["cc.g1a", "mo0"], writes=["cc.g1a"])
                dve(lambda e, col=col, s2c=s2c, btm1=btm1: e.tensor_tensor(cc["g1b"][:, col:col + 1], s2c, btm1, ALU.mult),
                    reads=["mo1", "pv"], writes=["cc.g1b"])
                dve(lambda e, w1c_=w1c_, atm1=atm1: e.tensor_tensor(wt1[:, 0:1], w1c_, atm1, ALU.mult),
                    reads=["mo0", "pv"], writes=["wt1"])
                dve(lambda e, col=col: e.tensor_tensor(cc["g1b"][:, col:col + 1], cc["g1b"][:, col:col + 1], wt1[:, 0:1], ALU.subtract),
                    reads=["cc.g1b", "wt1"], writes=["cc.g1b"])
                dve(lambda e, col=col, atT=atT: e.tensor_copy(cc["cAT"][:, col:col + 1], atT),
                    reads=["mo0"], writes=["cc.cAT"])
                dve(lambda e, col=col, btT=btT, rin=rin: e.tensor_tensor(cc["cBT"][:, col:col + 1], btT, rin, ALU.mult),
                    reads=["mo1", "pv"], writes=["cc.cBT"])

        for b4 in range(4):
            hp = b4 % 2
            dve(lambda e, b4=b4, hp=hp: e.tensor_copy(cc["G1e"][:, b4:b4 + 1], g_G1[:, hp * T + T - 1: hp * T + T]),
                reads=["gG1"], writes=["cc.G1e"])
            dve(lambda e, b4=b4, hp=hp: e.tensor_copy(cc["G2e"][:, b4:b4 + 1], g_G2[:, hp * T + T - 1: hp * T + T]),
                reads=["gG2"], writes=["cc.G2e"])

        for pbk in range(2):
            fsc = pv[:, PV["fscale"].start + pbk: PV["fscale"].start + pbk + 1]
            dve(lambda e, pbk=pbk, fsc=fsc: e.tensor_scalar(dtm[:, pbk * 2048:(pbk + 1) * 2048], sx[pbk][:, :], fsc, None, ALU.mult, ALU.bypass),
                reads=[f"sx{pbk}.0", f"sx{pbk}.1", "pv"], writes=["dtm"])
        bview = dtm.ap().rearrange("q (pb h t) -> q pb h t", pb=2, t=2)
        for t_ in range(2):
            for pbk in range(2):
                bk = (t_ * 2 + pbk) % 2
                for hb in range(8):
                    inap = bview[:, pbk, hb * 128:(hb + 1) * 128, t_]
                    pe(lambda e, bk=bk, hb=hb, inap=inap:
                       e.transpose(pt[bk][:, hb * 128:(hb + 1) * 128], inap, id16[:, :]),
                       reads=["dtm", "id16"], writes=[PT[bk]])
                outap = cB.ap().rearrange("q (k phi) -> q k phi", k=8)[:, :, t_ * 256 + pbk * 128: t_ * 256 + pbk * 128 + 128]
                dve(lambda e, bk=bk, outap=outap: e.tensor_copy(outap, pt[bk][:, :]),
                    reads=[PT[bk]], writes=["cB"])

        for rnd in range(2):
            dma("sp", lambda e, rnd=rnd: e.dma_start(
                out=sx[2][:, :].rearrange("p (db qt) -> p db qt", db=4),
                in_=d_C[rnd * 512:(rnd + 1) * 512, :, :].rearrange("(db p) q t -> p db (q t)", p=128)),
                writes=["sx2.0", "sx2.1"], sem="dp")
            dve(lambda e: e.tensor_copy(ys[:, 0:2048], sx[2][:, :]),
                reads=["sx2.0", "sx2.1"], writes=["ys.0", "ys.1", "ys.2", "ys.3"])
            cview = ys[:, 0:2048].rearrange("q (db pq t) -> q db pq t", db=4, t=2)
            for db4 in range(4):
                db = rnd * 4 + db4
                for kphi in range(4):
                    t_ = kphi // 2
                    pbk = kphi % 2
                    bk = (db4 * 4 + kphi) % 2
                    inap = cview[:, db4, pbk * 128:(pbk + 1) * 128, t_]
                    pe(lambda e, bk=bk, inap=inap: e.transpose(pt[bk][:, 0:128], inap, id16[:, :]),
                       reads=["ys.0", "ys.1", "ys.2", "ys.3", "id16"], writes=[PT[bk]])
                    dst = cC[:, kphi * 1024 + db * 128: kphi * 1024 + db * 128 + 128]
                    if t_ == 0:
                        dve(lambda e, bk=bk, dst=dst: e.tensor_copy(dst, pt[bk][:, 0:128]),
                            reads=[PT[bk]], writes=["cC"])
                    else:
                        dve(lambda e, bk=bk, dst=dst: e.tensor_scalar(dst, pt[bk][:, 0:128], -1.0, None, ALU.mult, ALU.bypass),
                            reads=[PT[bk]], writes=["cC"])

        halfseq = [(c, s, hf) for c in range(NCH) for s in range(SEQS) for hf in range(2)]
        slot_of = {key: i % 4 for i, key in enumerate(halfseq)}
        next_load = [0]

        def emit_xloads(upto):
            while next_load[0] < min(upto, len(halfseq)):
                ckey = halfseq[next_load[0]]
                c, s, hf = ckey
                r = slot_of[ckey]
                dma("act", lambda e, c=c, s=s, hf=hf, r=r: e.dma_start(
                    out=sx[r][:, :].rearrange("p (tl d) -> p tl d", tl=2),
                    in_=d_x[s].rearrange("(a p) d -> p a d", p=128)[:, (c * 2 + hf) * 2:(c * 2 + hf) * 2 + 2, :]),
                    writes=[f"sx{r}.0", f"sx{r}.1"], sem=f"dx{r}")
                next_load[0] += 1

        emit_xloads(4)

        def ln_batch_stats(stn, stt_, xtiles, xres):
            for i in range(4):
                act(lambda e, i=i, xt=xtiles[i]: e.activation(junk[0][:, :], xt, AF.Square, accum_out=stt_[:, 4 + i:5 + i]),
                    reads=[xres[i]], writes=["junk0", stn + ".q"])
                dve(lambda e, i=i, xt=xtiles[i]: e.tensor_scalar(junk[1][:, :], xt, 1.0, None, ALU.mult, ALU.add, accum_out=stt_[:, i:i + 1]),
                    reads=[xres[i]], writes=["junk1", stn + ".s"])
            dve(lambda e: e.tensor_scalar(stt_[:, 8:12], stt_[:, 0:4], -1.0 / H, None, ALU.mult, ALU.bypass),
                reads=[stn + ".s"], writes=[stn + ".nm"])
            dve(lambda e: e.tensor_tensor(wt1[:, 0:4], stt_[:, 8:12], stt_[:, 8:12], ALU.mult),
                reads=[stn + ".nm"], writes=["wt1"])
            dve(lambda e: e.tensor_scalar(wt2[:, 0:4], stt_[:, 4:8], 1.0 / H, None, ALU.mult, ALU.bypass),
                reads=[stn + ".q"], writes=["wt2"])
            dve(lambda e: e.tensor_tensor(wt2[:, 0:4], wt2[:, 0:4], wt1[:, 0:4], ALU.subtract),
                reads=["wt2", "wt1"], writes=["wt2"])
            dve(lambda e: e.tensor_scalar(wt2[:, 0:4], wt2[:, 0:4], EPS, None, ALU.add, ALU.bypass),
                reads=["wt2"], writes=["wt2"])
            act(lambda e: e.activation(wt2[:, 0:4], wt2[:, 0:4], AF.Sqrt),
                reads=["wt2"], writes=["wt2"])
            dve(lambda e: e.reciprocal(stt_[:, 12:16], wt2[:, 0:4]), reads=["wt2"], writes=[stn + ".rs"])
            dve(lambda e: e.tensor_tensor(stt_[:, 8:12], stt_[:, 8:12], stt_[:, 12:16], ALU.mult),
                reads=[stn + ".nm", stn + ".rs"], writes=[stn + ".nm"])

        def ln_tile_stats(srcap, srcres, jslot):
            act(lambda e: e.activation(junk[0][:, :], srcap, AF.Square, accum_out=st3[:, 17:18]),
                reads=[srcres], writes=["junk0", "st3.q2"])
            dve(lambda e: e.tensor_scalar(junk[1][:, :], srcap, 1.0, None, ALU.mult, ALU.add, accum_out=st3[:, 16:17]),
                reads=[srcres], writes=["junk1", "st3.s"])
            dve(lambda e: e.tensor_scalar(st3[:, 18:19], st3[:, 16:17], -1.0 / H, None, ALU.mult, ALU.bypass),
                reads=["st3.s"], writes=["st3.nm"])
            dve(lambda e: e.tensor_tensor(wt1[:, 0:1], st3[:, 18:19], st3[:, 18:19], ALU.mult),
                reads=["st3.nm"], writes=["wt1"])
            dve(lambda e: e.tensor_scalar(wt2[:, 0:1], st3[:, 17:18], 1.0 / H, None, ALU.mult, ALU.bypass),
                reads=["st3.q2"], writes=["wt2"])
            dve(lambda e: e.tensor_tensor(wt2[:, 0:1], wt2[:, 0:1], wt1[:, 0:1], ALU.subtract),
                reads=["wt2", "wt1"], writes=["wt2"])
            dve(lambda e: e.tensor_scalar(wt2[:, 0:1], wt2[:, 0:1], EPS, None, ALU.add, ALU.bypass),
                reads=["wt2"], writes=["wt2"])
            act(lambda e: e.activation(wt2[:, 0:1], wt2[:, 0:1], AF.Sqrt), reads=["wt2"], writes=["wt2"])
            dve(lambda e: e.reciprocal(st3[:, 19:20], wt2[:, 0:1]), reads=["wt2"], writes=["st3.rs"])
            dve(lambda e: e.tensor_tensor(st3[:, 18:19], st3[:, 18:19], st3[:, 19:20], ALU.mult),
                reads=["st3.nm", "st3.rs"], writes=["st3.nm"])

        NU = NCH * SEQS
        units = [(c, s) for c in range(NCH) for s in range(SEQS)]

        def unit_tiles(u):
            c, s = units[u]
            r0 = slot_of[(c, s, 0)]
            r1 = slot_of[(c, s, 1)]
            xres = [f"sx{r0}.0", f"sx{r0}.1", f"sx{r1}.0", f"sx{r1}.1"]
            xtiles = [sx[r0][:, 0:H], sx[r0][:, H:2 * H], sx[r1][:, 0:H], sx[r1][:, H:2 * H]]
            return c, s, xres, xtiles

        def fe1a(u):
            c, s, xres, xtiles = unit_tiles(u)
            for i in range(4):
                act(lambda e, i=i, xt=xtiles[i]: e.activation(junk[0][:, :], xt, AF.Square, accum_out=st1[:, 4 + i:5 + i]),
                    reads=[xres[i]], writes=["junk0", "st1.q"])
                dve(lambda e, i=i, xt=xtiles[i]: e.tensor_scalar(junk[1][:, :], xt, 1.0, None, ALU.mult, ALU.add, accum_out=st1[:, i:i + 1]),
                    reads=[xres[i]], writes=["junk1", "st1.s"])

        def fe1b(u):
            c, s, xres, xtiles = unit_tiles(u)
            dve(lambda e: e.tensor_scalar(st1[:, 8:12], st1[:, 0:4], -1.0 / H, None, ALU.mult, ALU.bypass),
                reads=["st1.s"], writes=["st1.nm"])
            dve(lambda e: e.tensor_tensor(wt1[:, 0:4], st1[:, 8:12], st1[:, 8:12], ALU.mult),
                reads=["st1.nm"], writes=["wt1"])
            dve(lambda e: e.tensor_scalar(wt2[:, 0:4], st1[:, 4:8], 1.0 / H, None, ALU.mult, ALU.bypass),
                reads=["st1.q"], writes=["wt2"])
            dve(lambda e: e.tensor_tensor(wt2[:, 0:4], wt2[:, 0:4], wt1[:, 0:4], ALU.subtract),
                reads=["wt2", "wt1"], writes=["wt2"])
            dve(lambda e: e.tensor_scalar(wt2[:, 0:4], wt2[:, 0:4], EPS, None, ALU.add, ALU.bypass),
                reads=["wt2"], writes=["wt2"])
            act(lambda e: e.activation(wt2[:, 0:4], wt2[:, 0:4], AF.Sqrt), reads=["wt2"], writes=["wt2"])
            dve(lambda e: e.reciprocal(st1[:, 12:16], wt2[:, 0:4]), reads=["wt2"], writes=["st1.rs"])
            dve(lambda e: e.tensor_tensor(st1[:, 8:12], st1[:, 8:12], st1[:, 12:16], ALU.mult),
                reads=["st1.nm", "st1.rs"], writes=["st1.nm"])
            for i in range(4):
                act(lambda e, i=i, xt=xtiles[i]: e.activation(n1tm[:, i * H:(i + 1) * H], xt, AF.Identity,
                                                scale=st1[:, 12 + i:13 + i], bias=st1[:, 8 + i:9 + i]),
                    reads=[xres[i], "st1.rs", "st1.nm"], writes=[f"n1tm.{i}"])

        def t1(u):
            for i in range(4):
                bk = i % 2
                for k in range(8):
                    pe(lambda e, i=i, k=k, bk=bk: e.transpose(
                        pt[bk][:, k * 128:(k + 1) * 128], n1tm[:, i * H + k * 128: i * H + (k + 1) * 128], id16[:, :]),
                        reads=[f"n1tm.{i}", "id16"], writes=[PT[bk]])
                outap = n1fm.ap().rearrange("q (k j) -> q k j", k=8)[:, :, i * 128:(i + 1) * 128]
                dve(lambda e, bk=bk, outap=outap: e.tensor_copy(outap, pt[bk][:, :]),
                    reads=[PT[bk]], writes=["n1fm"])

        def dterms(u):
            for m in range(4):
                dve(lambda e, m=m: e.tensor_tensor(dtm[:, m * H:(m + 1) * H], drep[:, :], n1tm[:, m * H:(m + 1) * H], ALU.mult),
                    reads=["drep", f"n1tm.{m}"], writes=["dtm"])

        def scan_pre(u):
            c, s, xres, xtiles = unit_tiles(u)
            P1c, P2c = P1s[s], P2s[s]
            rP1, rP2 = f"P1.{s}", f"P2.{s}"
            dve(lambda e, P1c=P1c: e.tensor_tensor(wq2[:, :], cc["q2a"][:, :], P1c[:, :], ALU.mult), reads=["cc.q2a", rP1], writes=["wq2"])
            dve(lambda e, P2c=P2c: e.tensor_tensor(wt1[:, 0:4], cc["q2b"][:, :], P2c[:, :], ALU.mult), reads=["cc.q2b", rP2], writes=["wt1"])
            dve(lambda e: e.tensor_tensor(wq2[:, :], wq2[:, :], wt1[:, 0:4], ALU.add), reads=["wq2", "wt1"], writes=["wq2"])
            dve(lambda e, P1c=P1c: e.tensor_tensor(wq3[:, :], cc["q3a"][:, :], P1c[:, :], ALU.mult), reads=["cc.q3a", rP1], writes=["wq3"])
            dve(lambda e, P2c=P2c: e.tensor_tensor(wt1[:, 0:4], cc["q3b"][:, :], P2c[:, :], ALU.mult), reads=["cc.q3b", rP2], writes=["wt1"])
            dve(lambda e: e.tensor_tensor(wq3[:, :], wq3[:, :], wt1[:, 0:4], ALU.add), reads=["wq3", "wt1"], writes=["wq3"])
            dve(lambda e, P2c=P2c: e.tensor_tensor(wR1[:, :], cc["R1b"][:, :], P2c[:, :], ALU.mult), reads=["cc.R1b", rP2], writes=["wR1"])
            dve(lambda e: e.tensor_tensor(wR1[:, :], wR1[:, :], wq2[:, :], ALU.add), reads=["wR1", "wq2"], writes=["wR1"])
            dve(lambda e, P2c=P2c: e.tensor_tensor(wR2[:, :], cc["R2a"][:, :], P2c[:, :], ALU.mult), reads=["cc.R2a", rP2], writes=["wR2"])
            dve(lambda e: e.tensor_tensor(wt1[:, 0:4], cc["R2b"][:, :], wq2[:, :], ALU.mult), reads=["cc.R2b", "wq2"], writes=["wt1"])
            dve(lambda e: e.tensor_tensor(wR2[:, :], wR2[:, :], wt1[:, 0:4], ALU.add), reads=["wR2", "wt1"], writes=["wR2"])
            dve(lambda e: e.tensor_tensor(dlc[:, :], cc["i11"][:, :], wR1[:, :], ALU.mult), reads=["cc.i11", "wR1"], writes=["dlc"])
            dve(lambda e: e.tensor_tensor(wt1[:, 0:4], cc["i12"][:, :], wR2[:, :], ALU.mult), reads=["cc.i12", "wR2"], writes=["wt1"])
            dve(lambda e: e.tensor_tensor(dlc[:, :], dlc[:, :], wt1[:, 0:4], ALU.add), reads=["dlc", "wt1"], writes=["dlc"])
            dve(lambda e: e.tensor_tensor(dls[:, :], cc["i21"][:, :], wR1[:, :], ALU.mult), reads=["cc.i21", "wR1"], writes=["dls"])
            dve(lambda e: e.tensor_tensor(wt1[:, 0:4], cc["i22"][:, :], wR2[:, :], ALU.mult), reads=["cc.i22", "wR2"], writes=["wt1"])
            dve(lambda e: e.tensor_tensor(dls[:, :], dls[:, :], wt1[:, 0:4], ALU.add), reads=["dls", "wt1"], writes=["dls"])

        def bu_scan(u):
            for b in range(4):
                bk = b % 2
                hp = b % 2
                for k in range(8):
                    pe(lambda e, b=b, k=k, bk=bk: e.matmul(
                        pb[bk][:, :], cB[:, k * 512 + b * 128: k * 512 + (b + 1) * 128],
                        n1fm[:, k * 512:(k + 1) * 512], start=(k == 0), stop=(k == 7)),
                        reads=["cB", "n1fm"], writes=[PB[bk]])
                dve(lambda e, b=b, bk=bk, hp=hp: e.tensor_tensor(u1[b][:, :], pb[bk][:, :], g_a[:, hp * T:(hp + 1) * T], ALU.mult),
                    reads=[PB[bk], "ga"], writes=[f"u1.{b}"])
                dve(lambda e, b=b, bk=bk, hp=hp: e.tensor_tensor(u2[b][:, :], pb[bk][:, :], g_b[:, hp * T:(hp + 1) * T], ALU.mult),
                    reads=[PB[bk], "gb"], writes=[f"u2.{b}"])
            for b in range(4):
                hp = b % 2
                dve(lambda e, b=b: e.tensor_tensor_scan(u1[b][:, :], u1[b][:, :], u1[b][:, :], dlc[:, b:b + 1], ALU.add, ALU.bypass),
                    reads=[f"u1.{b}", "dlc"], writes=[f"u1.{b}"])
                dve(lambda e, b=b: e.tensor_tensor_scan(u2[b][:, :], u2[b][:, :], u2[b][:, :], dls[:, b:b + 1], ALU.add, ALU.bypass),
                    reads=[f"u2.{b}", "dls"], writes=[f"u2.{b}"])
                dve(lambda e, b=b: e.tensor_copy(pce[:, b:b + 1], u1[b][:, T - 1:T]),
                    reads=[f"u1.{b}"], writes=["pce"])
                dve(lambda e, b=b: e.tensor_copy(pse[:, b:b + 1], u2[b][:, T - 1:T]),
                    reads=[f"u2.{b}"], writes=["pse"])
                dve(lambda e, b=b, hp=hp: e.tensor_tensor(u1[b][:, :], u1[b][:, :], g_G1[:, hp * T:(hp + 1) * T], ALU.mult),
                    reads=[f"u1.{b}", "gG1"], writes=[f"u1.{b}"])
                dve(lambda e, b=b, hp=hp: e.tensor_tensor(u2[b][:, :], u2[b][:, :], g_G2[:, hp * T:(hp + 1) * T], ALU.mult),
                    reads=[f"u2.{b}", "gG2"], writes=[f"u2.{b}"])
                dve(lambda e, b=b: e.tensor_tensor(ys[:, b * T:(b + 1) * T], u1[b][:, :], u2[b][:, :], ALU.add),
                    reads=[f"u1.{b}", f"u2.{b}"], writes=[f"ys.{b}"])

        def carry_update(u):
            c, s, xres, xtiles = unit_tiles(u)
            P1c, P2c = P1s[s], P2s[s]
            rP1, rP2 = f"P1.{s}", f"P2.{s}"
            dve(lambda e, P2c=P2c: e.tensor_tensor(P2c[:, :], cc["G1e"][:, :], pce[:, :], ALU.mult), reads=["cc.G1e", "pce"], writes=[rP2])
            dve(lambda e: e.tensor_tensor(wt2[:, 0:4], cc["G2e"][:, :], pse[:, :], ALU.mult), reads=["cc.G2e", "pse"], writes=["wt2"])
            dve(lambda e, P2c=P2c: e.tensor_tensor(P2c[:, :], P2c[:, :], wt2[:, 0:4], ALU.add), reads=[rP2, "wt2"], writes=[rP2])
            dve(lambda e: e.tensor_tensor(pce[:, :], pce[:, :], dlc[:, :], ALU.subtract), reads=["pce", "dlc"], writes=["pce"])
            dve(lambda e: e.tensor_tensor(pse[:, :], pse[:, :], dls[:, :], ALU.subtract), reads=["pse", "dls"], writes=["pse"])
            dve(lambda e, P1c=P1c: e.tensor_tensor(wt1[:, 0:4], cc["cAT"][:, :], P1c[:, :], ALU.mult), reads=["cc.cAT", rP1], writes=["wt1"])
            dve(lambda e: e.tensor_tensor(wt2[:, 0:4], cc["cBT"][:, :], wq3[:, :], ALU.mult), reads=["cc.cBT", "wq3"], writes=["wt2"])
            dve(lambda e: e.tensor_tensor(wt1[:, 0:4], wt1[:, 0:4], wt2[:, 0:4], ALU.add), reads=["wt1", "wt2"], writes=["wt1"])
            dve(lambda e, P1c=P1c: e.tensor_tensor(P1c[:, :], cc["g1a"][:, :], pce[:, :], ALU.mult), reads=["cc.g1a", "pce"], writes=[rP1])
            dve(lambda e: e.tensor_tensor(wt2[:, 0:4], cc["g1b"][:, :], pse[:, :], ALU.mult), reads=["cc.g1b", "pse"], writes=["wt2"])
            dve(lambda e, P1c=P1c: e.tensor_tensor(P1c[:, :], P1c[:, :], wt2[:, 0:4], ALU.add), reads=[rP1, "wt2"], writes=[rP1])
            dve(lambda e, P1c=P1c: e.tensor_tensor(P1c[:, :], P1c[:, :], wt1[:, 0:4], ALU.add), reads=[rP1, "wt1"], writes=[rP1])

        def cmm_one(u, m):
            mb = m % 2
            for kb in range(4):
                for ng in range(2):
                    pe(lambda e, m=m, kb=kb, ng=ng, mb=mb: e.matmul(
                        pcm[mb][:, ng * 512:(ng + 1) * 512],
                        ys[:, kb * T + m * 128: kb * T + (m + 1) * 128],
                        cC[:, kb * 1024 + ng * 512: kb * 1024 + (ng + 1) * 512],
                        start=(kb == 0), stop=(kb == 3)),
                        reads=[f"ys.{kb}", "cC"], writes=[PCM[mb][ng]])

        def mixes(u):
            c, s, xres, xtiles = unit_tiles(u)
            mixt = [(mo[0][:, 0:H], "mo0"), (mo[1][:, 0:H], "mo1"),
                    (outb[0][:, :], "outb0"), (outb[1][:, :], "outb1")]
            return mixt

        def mix_one(u, m, mixt):
            mb = m % 2
            mt, mres = mixt[m]
            dve(lambda e, m=m, mb=mb, mt=mt: e.tensor_tensor(mt, pcm[mb][:, :], dtm[:, m * H:(m + 1) * H], ALU.add),
                reads=[PCM[mb][0], PCM[mb][1], "dtm"], writes=[mres])

        def ln3_batch(u, mixt):
            c, s, xres, xtiles = unit_tiles(u)
            for m in range(4):
                mt, mres = mixt[m]
                act(lambda e, m=m, mt=mt: e.activation(junk[0][:, :], mt, AF.Square, accum_out=st3[:, 4 + m:5 + m]),
                    reads=[mres], writes=["junk0", "st3b.q"])
                dve(lambda e, m=m, mt=mt: e.tensor_scalar(junk[1][:, :], mt, 1.0, None, ALU.mult, ALU.add, accum_out=st3[:, m:m + 1]),
                    reads=[mres], writes=["junk1", "st3b.s"])
            dve(lambda e: e.tensor_scalar(st3[:, 8:12], st3[:, 0:4], -1.0 / H, None, ALU.mult, ALU.bypass),
                reads=["st3b.s"], writes=["st3b.nm"])
            dve(lambda e: e.tensor_tensor(wt1[:, 0:4], st3[:, 8:12], st3[:, 8:12], ALU.mult),
                reads=["st3b.nm"], writes=["wt1"])
            dve(lambda e: e.tensor_scalar(wt2[:, 0:4], st3[:, 4:8], 1.0 / H, None, ALU.mult, ALU.bypass),
                reads=["st3b.q"], writes=["wt2"])
            dve(lambda e: e.tensor_tensor(wt2[:, 0:4], wt2[:, 0:4], wt1[:, 0:4], ALU.subtract),
                reads=["wt2", "wt1"], writes=["wt2"])
            dve(lambda e: e.tensor_scalar(wt2[:, 0:4], wt2[:, 0:4], EPS, None, ALU.add, ALU.bypass),
                reads=["wt2"], writes=["wt2"])
            act(lambda e: e.activation(wt2[:, 0:4], wt2[:, 0:4], AF.Sqrt), reads=["wt2"], writes=["wt2"])
            dve(lambda e: e.reciprocal(st3[:, 12:16], wt2[:, 0:4]), reads=["wt2"], writes=["st3b.rs"])
            dve(lambda e: e.tensor_tensor(st3[:, 8:12], st3[:, 8:12], st3[:, 12:16], ALU.mult),
                reads=["st3b.nm", "st3b.rs"], writes=["st3b.nm"])
            for m in range(4):
                mt, mres = mixt[m]
                tb = m % 2
                act(lambda e, m=m, mt=mt, tb=tb: e.activation(tt[tb][:, :], mt, AF.Identity,
                                                              scale=st3[:, 12 + m:13 + m], bias=st3[:, 8 + m:9 + m]),
                    reads=[mres, "st3b.rs", "st3b.nm"], writes=[f"tt{tb}"])
                dve(lambda e, m=m, tb=tb, xt=xtiles[m]: e.tensor_tensor(xt, xt, tt[tb][:, :], ALU.add),
                    reads=[xres[m], f"tt{tb}"], writes=[xres[m]])

        def ln2_batch(u):
            c, s, xres, xtiles = unit_tiles(u)
            for m in range(4):
                act(lambda e, m=m, xt=xtiles[m]: e.activation(junk[0][:, :], xt, AF.Square, accum_out=st2[:, 4 + m:5 + m]),
                    reads=[xres[m]], writes=["junk0", "st2.q"])
                dve(lambda e, m=m, xt=xtiles[m]: e.tensor_scalar(junk[1][:, :], xt, 1.0, None, ALU.mult, ALU.add, accum_out=st2[:, m:m + 1]),
                    reads=[xres[m]], writes=["junk1", "st2.s"])
            dve(lambda e: e.tensor_scalar(st2[:, 8:12], st2[:, 0:4], -1.0 / H, None, ALU.mult, ALU.bypass),
                reads=["st2.s"], writes=["st2.nm"])
            dve(lambda e: e.tensor_tensor(wt1[:, 0:4], st2[:, 8:12], st2[:, 8:12], ALU.mult),
                reads=["st2.nm"], writes=["wt1"])
            dve(lambda e: e.tensor_scalar(wt2[:, 0:4], st2[:, 4:8], 1.0 / H, None, ALU.mult, ALU.bypass),
                reads=["st2.q"], writes=["wt2"])
            dve(lambda e: e.tensor_tensor(wt2[:, 0:4], wt2[:, 0:4], wt1[:, 0:4], ALU.subtract),
                reads=["wt2", "wt1"], writes=["wt2"])
            dve(lambda e: e.tensor_scalar(wt2[:, 0:4], wt2[:, 0:4], EPS, None, ALU.add, ALU.bypass),
                reads=["wt2"], writes=["wt2"])
            act(lambda e: e.activation(wt2[:, 0:4], wt2[:, 0:4], AF.Sqrt), reads=["wt2"], writes=["wt2"])
            dve(lambda e: e.reciprocal(st2[:, 12:16], wt2[:, 0:4]), reads=["wt2"], writes=["st2.rs"])
            dve(lambda e: e.tensor_tensor(st2[:, 8:12], st2[:, 8:12], st2[:, 12:16], ALU.mult),
                reads=["st2.nm", "st2.rs"], writes=["st2.nm"])
            for m in range(4):
                act(lambda e, m=m, xt=xtiles[m]: e.activation(n2tm[m][:, :], xt, AF.Identity,
                                                              scale=st2[:, 12 + m:13 + m], bias=st2[:, 8 + m:9 + m]),
                    reads=[xres[m], "st2.rs", "st2.nm"], writes=[f"n2tm{m}"])

        def t3_one(u, m):
            nb = m
            bk = m % 2
            for k in range(8):
                pe(lambda e, k=k, nb=nb, bk=bk: e.transpose(pt[bk][:, k * 128:(k + 1) * 128],
                                                     n2tm[nb][:, k * 128:(k + 1) * 128], id16[:, :]),
                    reads=[f"n2tm{nb}", "id16"], writes=[PT[bk]])
            outap = n2fm.ap().rearrange("q (k j) -> q k j", k=8)[:, :, m * 128:(m + 1) * 128]
            dve(lambda e, bk=bk, outap=outap: e.tensor_copy(outap, pt[bk][:, :]),
                reads=[PT[bk]], writes=["n2fm"])

        def mixer_back(u, next_u):
            mixt = mixes(u)
            cmm_one(u, 0)
            cmm_one(u, 1)
            mix_one(u, 0, mixt)
            cmm_one(u, 2)
            mix_one(u, 1, mixt)
            cmm_one(u, 3)
            mix_one(u, 2, mixt)
            mix_one(u, 3, mixt)
            ln3_batch(u, mixt)
            ln2_batch(u)
            if next_u is not None:
                t1(next_u)
            t3_one(u, 0)
            t3_one(u, 1)
            t3_one(u, 2)
            t3_one(u, 3)

        wload_state = {"next": 0}
        NPIECE_TOTAL = None

        def emit_piece_load(g):
            if NPIECE_TOTAL is not None and g >= NPIECE_TOTAL[0]:
                return
            piece = g % 8
            wr = g % 2
            dma("sp", lambda e, piece=piece, wr=wr: e.dma_start(
                out=w1s[wr][:, :], in_=s_w1t[piece, :, :]),
                reads=["s_w1t"], writes=[f"w1s{wr}"], sem=f"dw1{wr}")
            dma("sp", lambda e, piece=piece, wr=wr: e.dma_start(
                out=w2s[wr][:, :], in_=s_w2t[piece, :, :]),
                reads=["s_w2t"], writes=[f"w2s{wr}"], sem=f"dw2{wr}")

        def ffn(u):
            c, s, xres, xtiles = unit_tiles(u)
            for hf in range(2):
                gbase = (u * 2 + hf) * 8

                def ffn1(hb):
                    piece, hbq, bk = hb // 4, hb % 4, hb % 2
                    wr = (gbase + piece) % 2
                    for k in range(8):
                        pe(lambda e, k=k, bk=bk, wr=wr, hbq=hbq, hf=hf: e.matmul(
                            pb[bk][:, 0:256],
                            w1s[wr][:, k * 512 + hbq * 128: k * 512 + hbq * 128 + 128],
                            n2fm[:, k * 512 + hf * 256: k * 512 + hf * 256 + 256],
                            start=(k == 0), stop=(k == 7)),
                            reads=[f"w1s{wr}", "n2fm"], writes=[PB[bk]])
                    gs = hb % 4
                    act(lambda e, hb=hb, bk=bk, gs=gs: e.activation(
                        gel[gs][:, :], pb[bk][:, 0:256], AF.Gelu, bias=b1col[:, hb:hb + 1],
                        scale=a1col[:, hb:hb + 1]),
                        reads=[PB[bk], "b1col", "a1col"], writes=[f"gel{gs}"])

                def ffn2(hb):
                    piece, hbq, gs = hb // 4, hb % 4, hb % 4
                    wr = (gbase + piece) % 2
                    for mg in range(2):
                        for ng in range(2):
                            pe(lambda e, hb=hb, mg=mg, ng=ng, wr=wr, gs=gs, hbq=hbq: e.matmul(
                                pcm[mg][:, ng * 512:(ng + 1) * 512],
                                gel[gs][:, mg * 128:(mg + 1) * 128],
                                w2s[wr][:, hbq * 1024 + ng * 512: hbq * 1024 + (ng + 1) * 512],
                                start=(hb == 0), stop=(hb == 31), skip_group_check=True),
                                reads=[f"gel{gs}", f"w2s{wr}"], writes=[PCM[mg][ng]])

                ffn1(0)
                for hb in range(1, 32):
                    ffn1(hb)
                    ffn2(hb - 1)
                    if (hb - 1) % 4 == 3:
                        emit_piece_load(gbase + (hb - 1) // 4 + 2)
                ffn2(31)
                emit_piece_load(gbase + 9)
                for mg in range(2):
                    m = hf * 2 + mg
                    dve(lambda e, mg=mg: e.tensor_tensor(mo[mg][:, 0:H], pcm[mg][:, :], b2rep[:, :], ALU.add),
                        reads=[PCM[mg][0], PCM[mg][1], "b2rep"], writes=[f"mo{mg}"])
                    ln_tile_stats(mo[mg][:, 0:H], f"mo{mg}", mg)
                    act(lambda e, mg=mg: e.activation(tt[mg][:, :], mo[mg][:, 0:H], AF.Identity, scale=st3[:, 19:20], bias=st3[:, 18:19]),
                        reads=[f"mo{mg}", "st3.rs", "st3.nm"], writes=[f"tt{mg}"])
                    dve(lambda e, m=m, mg=mg, xt=xtiles[m]: e.tensor_tensor(outb[mg][:, :], xt, tt[mg][:, :], ALU.add),
                        reads=[xres[m], f"tt{mg}"], writes=[f"outb{mg}"])
                    tok_tile = c * 4 + hf * 2 + mg
                    dma("gp", lambda e, s=s, tok_tile=tok_tile, mg=mg: e.dma_start(
                        out=d_out[s].rearrange("(a p) d -> p a d", p=128)[:, tok_tile, :],
                        in_=outb[mg][:, :]),
                        reads=[f"outb{mg}"], writes=[], sem=f"do{mg}")
                emit_xloads(2 * u + 5 + hf)
                if hf == 1 and u + 2 < NU:
                    fe1a(u + 2)

        NPIECE_TOTAL = [NU * 16]
        emit_piece_load(0)
        emit_piece_load(1)
        fe1a(0)
        fe1b(0)
        t1(0)
        scan_pre(0)
        bu_scan(0)
        carry_update(0)
        emit_xloads(4)
        if NU > 1:
            fe1a(1)
        for u in range(NU):
            dterms(u)
            if u + 1 < NU:
                fe1b(u + 1)
            mixer_back(u, u + 1 if u + 1 < NU else None)
            if u + 1 < NU:
                scan_pre(u + 1)
                bu_scan(u + 1)
                carry_update(u + 1)
            ffn(u)

        last_do = [B.dma_sem_last.get("do0"), B.dma_sem_last.get("do1")]
        fin = B.emit("sp", lambda e: e.nop(), reads=(), writes=())
        for d in last_do:
            if d is not None:
                fin.extra.append(d)

        B.finalize_counts()
        sems = {}
        for s_ in sorted(B.sem_names):
            sems[s_] = es.enter_context(nc.semaphore(s_))
        block = es.enter_context(nc.Block())
        block.tensor(lambda e: B.run("pe", e, sems))
        block.vector(lambda e: B.run("dve", e, sems))
        block.scalar(lambda e: B.run("act", e, sems))
        block.sync(lambda e: B.run("sp", e, sems))
        block.gpsimd(lambda e: B.run("gp", e, sems))
        nc._B = B
    return nc


def kernel(L0_attn_input_sequence, L0_attn_steps_raw, L0_attn_G_raw, L0_attn_A_raw,
           L0_attn_B_mat, L0_attn_C_mat, L0_attn_D_vec,
           L0_ffn_w1, L0_ffn_b1, L0_ffn_w2, L0_ffn_b2):
    x = np.ascontiguousarray(np.asarray(L0_attn_input_sequence, dtype=np.float32))
    n_cores = 8
    params = {
        "steps_raw": np.ascontiguousarray(np.asarray(L0_attn_steps_raw, dtype=np.float32)),
        "G_raw": np.ascontiguousarray(np.asarray(L0_attn_G_raw, dtype=np.float32)),
        "A_raw": np.ascontiguousarray(np.asarray(L0_attn_A_raw, dtype=np.float32)),
        "B_mat": np.ascontiguousarray(np.asarray(L0_attn_B_mat, dtype=np.float32)),
        "C_mat": np.ascontiguousarray(np.asarray(L0_attn_C_mat, dtype=np.float32)),
        "D_vec": np.ascontiguousarray(np.asarray(L0_attn_D_vec, dtype=np.float32)),
        "w1": np.ascontiguousarray(np.asarray(L0_ffn_w1, dtype=np.float32)),
        "b1": np.ascontiguousarray(np.asarray(L0_ffn_b1, dtype=np.float32)),
        "w2": np.ascontiguousarray(np.asarray(L0_ffn_w2, dtype=np.float32)),
        "b2": np.ascontiguousarray(np.asarray(L0_ffn_b2, dtype=np.float32)),
        "ident": np.eye(128, dtype=np.float32),
    }
    nc = bass.Bass("TRN2", target_bir_lowering=False, debug=False)
    build_program(nc)
    in_maps = [dict(x=x[2 * i:2 * i + 2], **params) for i in range(n_cores)]
    res = run_bass_kernel_spmd(nc, in_maps, core_ids=list(range(n_cores)))
    return np.concatenate([np.asarray(r["out"], dtype=np.float32) for r in res.results], axis=0)
